# version 01
# baseline (speedup 1.0000x reference)
import contextlib
import numpy as np
import concourse.bass as bass
import concourse.mybir as mybir
from concourse.bass_utils import run_bass_kernel_spmd

try:
    import ml_dtypes
    NP_BF16 = ml_dtypes.bfloat16
except Exception:
    NP_BF16 = None

F32 = mybir.dt.float32
BF16 = mybir.dt.bfloat16
AF = mybir.ActivationFunctionType
ALU = mybir.AluOpType
AX = mybir.AxisListType

D = 1024
T = 2048
NCH = 16
SEQ = 8192
GCH = 64
TC = 512
ND = 8

ENGINES = ("sync", "vector", "scalar", "tensor", "gpsimd")
RG = [[0, 1, 2, 3], [4, 5, 6, 7]]


class Sem:
    __slots__ = ("h", "n", "name")

    def __init__(self, h, name):
        self.h = h
        self.n = 0
        self.name = name

    def will(self, k=1):
        self.n += k
        return self.n

    @property
    def now(self):
        return (self, self.n)


class PsumTile:
    __slots__ = ("t", "gate")

    def __init__(self, t):
        self.t = t
        self.gate = None


class Prog:
    def __init__(self, nc, es):
        self.nc = nc
        self.es = es
        self.q = {e: [] for e in ENGINES}
        self._names = {}
        self.LD = self.sem("LD")
        self.GP = self.sem("GP")
        self.TE = self.sem("TE")
        self.VE = self.sem("VE")
        self.SE = self.sem("SE")
        self.GE = self.sem("GE")

    def sem_for(self, eng):
        return {"vector": self.VE, "scalar": self.SE, "tensor": self.TE, "gpsimd": self.GE}[eng]

    def sem(self, name):
        h = self.es.enter_context(self.nc.semaphore(self.uname(name)))
        return Sem(h, name)

    def uname(self, base):
        i = self._names.get(base, 0)
        self._names[base] = i + 1
        return f"{base}{i}" if i else base

    def sbuf(self, name, shape, dtype):
        return self.es.enter_context(self.nc.sbuf_tensor(self.uname(name), shape, dtype))

    def psum(self, name, shape, dtype=F32):
        return PsumTile(self.es.enter_context(self.nc.psum_tensor(self.uname(name), shape, dtype)))

    def dram(self, name, shape, dtype=F32):
        return self.nc.dram_tensor(self.uname(name), shape, dtype).ap()

    def on(self, eng, fn):
        self.q[eng].append(fn)

    def op(self, eng, spec, waits=(), inc=False):
        s = self.sem_for(eng) if inc else None
        tgt = s.will() if s else None
        waits = tuple(w for w in waits if w is not None)
        def fn(e, spec=spec, waits=waits, s=s):
            for (sem, v) in waits:
                if v > 0:
                    e.wait_ge(sem.h, v)
            ins = spec(e)
            if s is not None:
                ins.then_inc(s.h, 1)
        self.on(eng, fn)
        return (s, tgt) if s else None

    NDCH = 6

    def dma(self, queue, out, in_, waits=(), sem=None, **kw):
        if not hasattr(self, "dchan"):
            self.dchan = {}
            self.drot = {}
        if queue not in self.dchan:
            self.dchan[queue] = [self.sem(f"dc_{queue}{i}") for i in range(self.NDCH)]
            self.drot[queue] = 0
        ch = self.dchan[queue][self.drot[queue] % self.NDCH]
        self.drot[queue] += 1
        prev = ch.n
        tgt = ch.will(16)
        waits = tuple(w for w in waits if w is not None)
        def fn(e, out=out, in_=in_, waits=waits, kw=kw, prev=prev):
            if prev > 0:
                e.wait_ge(ch.h, prev)
            for (s, v) in waits:
                if v > 0:
                    e.wait_ge(s.h, v)
            e.dma_start(out=out, in_=in_, **kw).then_inc(ch.h, 16)
        self.on(queue, fn)
        return (ch, tgt)

    def dma_frontier(self, queue):
        if not hasattr(self, "dchan") or queue not in self.dchan:
            return []
        return [(ch, ch.n) for ch in self.dchan[queue]]

    def collective(self, kind, op, ins, outs, waits=()):
        cs = self.sem("cc")
        tgt = cs.will()
        waits = tuple(w for w in waits if w is not None)
        def fn(e, waits=waits):
            for (s, v) in waits:
                if v > 0:
                    e.wait_ge(s.h, v)
            e.collective_compute(kind, op, replica_groups=RG, ins=ins, outs=outs).then_inc(cs.h, 1)
        self.on("gpsimd", fn)
        return (cs, tgt)

    def finalize(self):
        with self.nc.Block() as block:
            for eng in ENGINES:
                def runner(e, _q=tuple(self.q[eng])):
                    for fn in _q:
                        fn(e)
                getattr(block, eng)(runner)


class WTile:
    def __init__(self, buf, nic, ncols, ready, ws, slot):
        self.buf, self.nic, self.ncols, self.ready, self.ws, self.slot = buf, nic, ncols, ready, ws, slot

    def lhsT(self, ic, o0, ow):
        base = ic * self.ncols + o0
        return self.buf[:, base:base + ow]

    def release(self, gate):
        self.ws.free_at[self.slot] = gate


class WStream:
    def __init__(self, P, max_cols=2048, nbuf=2, tag="w", bias_cols=512):
        self.P = P
        self.nbuf = nbuf
        self.max_cols = max_cols
        self.bias_cols = bias_cols
        tot = max_cols + bias_cols
        self.f32 = [P.sbuf(tag + "ld", [128, tot], F32) for _ in range(nbuf)]
        self.bf = [P.sbuf(tag + "bf", [128, tot], BF16) for _ in range(nbuf)]
        self.free_at = [None] * nbuf
        self.i = 0

    def load(self, w_dram, IN, ncols, col0=0, bias_dram=None, extra=()):
        P = self.P
        nic = IN // 128
        n = nic * ncols
        assert n <= self.max_cols and ncols <= self.bias_cols, (IN, ncols)
        bf32, bbf = self.f32[self.i], self.bf[self.i]
        gate = self.free_at[self.i]
        src = w_dram[:, col0:col0 + ncols].rearrange("(ic p) o -> p ic o", p=128)
        dst = bf32[:, 0:n].rearrange("p (ic o) -> p ic o", o=ncols)
        ldg = P.dma("sync", dst, src, waits=(gate,) + tuple(extra),
                    **({"allow_slow_non_contiguous": True} if ncols < 16 else {}))
        lds = [ldg]
        if bias_dram is not None:
            bsrc = bias_dram[col0:col0 + ncols].rearrange("(u n) -> u n", u=1)
            lds.append(P.dma("sync", bf32[0:1, n:n + ncols], bsrc, waits=(gate,) + tuple(extra)))
        cast = P.op("vector", lambda e, a=bbf, b=bf32, n=n: e.tensor_copy(a[:, 0:n], b[:, 0:n]),
                    waits=tuple(lds), inc=True)
        if bias_dram is not None:
            cast = P.op("vector", lambda e, a=bbf, b=bf32, n=n, m=ncols: e.tensor_copy(a[0:1, n:n + m], b[0:1, n:n + m]),
                        waits=(cast,), inc=True)
        wt = WTile(bbf, nic, ncols, cast, self, self.i)
        wt.bias_off = n
        self.i = (self.i + 1) % self.nbuf
        return wt

    def brow(self, wt, o0, ow):
        return wt.buf[0:1, wt.bias_off + o0: wt.bias_off + o0 + ow]

    def load_rows(self, w_dram, r0, rows, ncols, col0=0, bias_dram=None, base=0):
        P = self.P
        assert ncols <= self.max_cols and (bias_dram is None or ncols <= self.bias_cols)
        bf32, bbf = self.f32[self.i], self.bf[self.i]
        gate = self.free_at[self.i]
        lds = [P.dma("sync", bf32[base:base + rows, 0:ncols], w_dram[r0:r0 + rows, col0:col0 + ncols], waits=(gate,))]
        if bias_dram is not None:
            lds.append(P.dma("sync", bf32[0:1, self.max_cols:self.max_cols + ncols],
                             bias_dram[col0:col0 + ncols].rearrange("(u n) -> u n", u=1), waits=(gate,)))
            cast = P.op("vector", lambda e, a=bbf, b=bf32, rows=rows, n=ncols, base=base:
                        e.tensor_copy(a[base:base + rows, 0:n], b[base:base + rows, 0:n]), waits=tuple(lds), inc=True)
            cast = P.op("vector", lambda e, a=bbf, b=bf32, n=ncols:
                        e.tensor_copy(a[0:1, self.max_cols:self.max_cols + n],
                                      b[0:1, self.max_cols:self.max_cols + n]), waits=(cast,), inc=True)
        else:
            cast = P.op("vector", lambda e, a=bbf, b=bf32, rows=rows, n=ncols, base=base:
                        e.tensor_copy(a[base:base + rows, 0:n], b[base:base + rows, 0:n]), waits=tuple(lds), inc=True)
        wt = WTile(bbf, 1, ncols, cast, self, self.i)
        wt.bias_off = self.max_cols
        wt.row_count = rows
        self.i = (self.i + 1) % self.nbuf
        return wt


class Rows:

    def __init__(self, P, width=2048, reserve=64):
        self.t = P.sbuf("rows", [128, width], F32)
        self.width = width
        self.reserve = reserve
        self.cur = reserve
        self.rcur = 0

    def seg(self, n, base=0):
        if self.cur + n > self.width:
            self.cur = self.reserve
        ap = self.t[base:base + 1, self.cur:self.cur + n]
        self.cur += n
        return ap

    def perm(self, n):
        assert self.rcur + n <= self.reserve
        ap = self.t[0:1, self.rcur:self.rcur + n]
        self.rcur += n
        return ap

    def permf(self, n, rows=128):
        assert self.rcur + n <= self.reserve
        ap = self.t[0:rows, self.rcur:self.rcur + n]
        self.rcur += n
        return ap


def load_row(P, dram_vec, n, tag="row"):
    r = P.sbuf(tag, [1, n], F32)
    src = dram_vec if len(dram_vec.shape) == 2 else dram_vec.rearrange("(u n) -> u n", u=1)
    g = P.dma("sync", r[:, :], src)
    rb = P.sbuf(tag + "b", [1, n], BF16)
    g2 = P.op("vector", lambda e: e.tensor_copy(rb[:, :], r[:, :]), waits=(g,), inc=True)
    return r, rb, g2


def emit_linear(P, wt, xin, OUT, Ttok, augs, epilogue, psums, o_base=0, extra_waits=(), oc_list=None):
    k = 0
    first = True
    for oc in (oc_list if oc_list is not None else range(OUT // 128)):
        for t0 in range(0, Ttok, TC):
            tw = min(TC, Ttok - t0)
            ps = psums[k % len(psums)]
            k += 1
            waits = []
            if first:
                waits += list(extra_waits)
                if wt is not None:
                    waits.append(wt.ready)
            first = False
            if ps.gate is not None:
                waits.append(ps.gate)
            def spec(e, oc=oc, t0=t0, tw=tw, wt=wt, augs=augs, ps=ps):
                m = None
                nic = wt.nic if wt else 0
                for ic in range(nic):
                    m = e.matmul(ps.t[:, 0:tw], wt.lhsT(ic, o_base + oc * 128, 128),
                                 xin(ic, t0, tw), start=(ic == 0),
                                 stop=(not augs and ic == nic - 1))
                last = len(augs) - 1
                for ai, (xrow, wrow) in enumerate(augs):
                    m = e.matmul(ps.t[:, 0:tw], wrow(o_base + oc * 128, 128),
                                 xrow(t0, tw), start=(wt is None and ai == 0), stop=(ai == last))
                return m
            mmg = P.op("tensor", spec, waits=waits, inc=True)
            epilogue(P, ps, oc, t0, tw, mmg)


def linear_chunked(P, ws, wdram, bias_dram, IN, OUT, xin, Ttok, onesrow, epilogue, psums,
                   extra_waits=(), load_waits=()):
    step = (ws.max_cols // (IN // 128)) // 128 * 128
    step = min(step, ws.bias_cols)
    first = True
    for col0 in range(0, OUT, step):
        ncols = min(step, OUT - col0)
        wt = ws.load(wdram, IN, ncols, col0=col0, bias_dram=bias_dram, extra=load_waits)
        augs = [(onesrow, lambda o0, ow, wt=wt: ws.brow(wt, o0, ow))] if bias_dram is not None else []
        def epi(P, ps, oc, t0, tw, mmg, col0=col0):
            epilogue(P, ps, col0 // 128 + oc, t0, tw, mmg)
        emit_linear(P, wt, xin, ncols, Ttok, augs, epi, psums,
                    extra_waits=(extra_waits if first else ()))
        first = False
        wt.release(P.TE.now)
    return P.TE.now


def epi_act(P_, dst, func=AF.Copy, scale=1.0, eng="scalar", gates=None):
    def epilogue(P, ps, oc, t0, tw, mmg):
        g = P.op(eng, lambda e, ps=ps, oc=oc, t0=t0, tw=tw:
                 e.activation(dst(oc, t0, tw), ps.t[:, 0:tw], func, scale=scale),
                 waits=(mmg,), inc=True)
        ps.gate = g
        if gates is not None:
            gates.append(g)
    return epilogue


def emit_bcast(P, ps, c0, n, row_ap, C, waits=()):
    w = list(waits)
    if ps.gate is not None:
        w.append(ps.gate)
        ps.gate = None
    b = row_ap.base_partition()
    return P.op("tensor", lambda e, b=b: e.matmul(ps.t[:, c0:c0 + n], C.onescol(b),
                                                  row_ap, start=True, stop=True), waits=w, inc=True)


class Consts:
    pass


FCONSTS = (1e-5, 1e-6, 6.4e-4, 1.0 / D, 1.0 / SEQ, -0.5, 1.4, -1.0, 2.0,
           1.0 / 256, 1.0 / 64, 8.0, 0.125, 64.0 ** -0.5, 96.0 ** -0.5)


def emit_consts(P):
    C = Consts()
    C.ones128_bf = P.sbuf("ones128bf", [128, 1], BF16)
    C.ones_col_f32 = P.sbuf("onescolf", [128, 128], F32)
    C.ones_row_bf = P.sbuf("onesrowbf", [1, T], BF16)
    C.fc = P.sbuf("fconsts", [128, len(FCONSTS)], F32)
    g = None
    for tgt, val in ((C.ones128_bf, 1.0), (C.ones_col_f32, 1.0), (C.ones_row_bf, 1.0)):
        g = P.op("vector", lambda e, t=tgt, v=val: e.memset(t[:, :], v), inc=True)
    C.onescol = lambda b: C.ones_col_f32[b:b + 1, 0:128]
    for k, v in enumerate(FCONSTS):
        g = P.op("vector", lambda e, k=k, v=v: e.memset(C.fc[:, k:k + 1], v), inc=True)
        P.nc.const_aps.register(F32, v, C.fc[:, k:k + 1])
    C.ready = g
    for eng in ("scalar", "tensor", "gpsimd"):
        P.on(eng, lambda e, g=g: e.wait_ge(g[0].h, g[1]))
    C.onesrow = lambda t0, tw: C.ones_row_bf[0:1, t0:t0 + tw]
    return C


def emit_ln(P, C, rows, x_tiles, xn_tiles, scr_bf, ps_misc, Ttok=T, eps=1e-5, x_ready=()):
    mu = rows.seg(TC)
    m2 = rows.seg(TC)
    rs = rows.seg(TC)
    out_gates = []
    for t0 in range(0, Ttok, TC):
        cast_g = sq_g = None
        for i in range(ND):
            cast_g = P.op("vector", lambda e, i=i, t0=t0: e.tensor_copy(scr_bf[i][:, 0:TC], x_tiles[i][:, t0:t0 + TC]),
                          waits=tuple(x_ready) + ((ps_misc.gate,) if i == 0 and ps_misc.gate else ()), inc=True)
            sq_g = P.op("scalar", lambda e, i=i: e.activation(scr_bf[ND + i][:, 0:TC], scr_bf[i][:, 0:TC], AF.Square),
                        waits=(cast_g,), inc=True)
        w = [cast_g, sq_g]
        if ps_misc.gate is not None:
            w.append(ps_misc.gate)
            ps_misc.gate = None
        def spec(e):
            for i in range(ND):
                e.matmul(ps_misc.t[0:1, 0:TC], C.ones128_bf[:, 0:1], scr_bf[i][:, 0:TC],
                         start=(i == 0), stop=(i == ND - 1))
            m = None
            for i in range(ND):
                m = e.matmul(ps_misc.t[0:1, TC:2 * TC], C.ones128_bf[:, 0:1], scr_bf[ND + i][:, 0:TC],
                             start=(i == 0), stop=(i == ND - 1))
            return m
        stat_g = P.op("tensor", spec, waits=w, inc=True)
        g1 = P.op("scalar", lambda e: e.activation(mu, ps_misc.t[0:1, 0:TC], AF.Copy, scale=1.0 / D),
                  waits=(stat_g,), inc=True)
        g2 = P.op("scalar", lambda e: e.activation(m2, mu, AF.Square), waits=(g1,), inc=True)
        g3 = P.op("scalar", lambda e: e.activation(rs, ps_misc.t[0:1, TC:2 * TC], AF.Copy, scale=1.0 / D),
                  waits=(g2,), inc=True)
        g4 = P.op("vector", lambda e: e.tensor_sub(rs, rs, m2), waits=(g3,), inc=True)
        g5 = P.op("scalar", lambda e, eps=eps: e.activation(m2, rs, AF.Ln, bias=eps), waits=(g4,), inc=True)
        g6 = P.op("scalar", lambda e: e.activation(rs, m2, AF.Exp, scale=-0.5), waits=(g5,), inc=True)
        g7 = P.op("vector", lambda e: e.tensor_mul(m2, mu, rs), waits=(g6,), inc=True)
        b1 = emit_bcast(P, ps_misc, 0, TC, rs, C, waits=(g6,))
        b2 = emit_bcast(P, ps_misc, TC, TC, m2, C, waits=(g7,))
        last = None
        for i in range(ND):
            a1 = P.op("vector", lambda e, i=i, t0=t0: e.tensor_mul(xn_tiles[i][:, t0:t0 + TC], x_tiles[i][:, t0:t0 + TC], ps_misc.t[:, 0:TC]),
                      waits=(b1,), inc=True)
            last = P.op("vector", lambda e, i=i, t0=t0: e.tensor_sub(xn_tiles[i][:, t0:t0 + TC], xn_tiles[i][:, t0:t0 + TC], ps_misc.t[:, TC:2 * TC]),
                        waits=(b2, a1), inc=True)
        ps_misc.gate = last
        out_gates.append(last)
    return out_gates


def emit_ln_inplace(P, C, rows, x_tiles, xn_tiles, scr_bf, ps_misc, Ttok=T, eps=1e-5, x_ready=()):
    mu = rows.seg(TC)
    m2 = rows.seg(TC)
    rs = rows.seg(TC)
    out_gates = []
    for t0 in range(0, Ttok, TC):
        cast_g = sq_g = None
        for i in range(ND):
            cast_g = P.op("vector", lambda e, i=i, t0=t0: e.tensor_copy(scr_bf[i][:, 0:TC], x_tiles[i][:, t0:t0 + TC]),
                          waits=tuple(x_ready) + ((ps_misc.gate,) if i == 0 and ps_misc.gate else ()), inc=True)
            sq_g = P.op("scalar", lambda e, i=i: e.activation(scr_bf[ND + i][:, 0:TC], scr_bf[i][:, 0:TC], AF.Square),
                        waits=(cast_g,), inc=True)
        w = [cast_g, sq_g]
        if ps_misc.gate is not None:
            w.append(ps_misc.gate)
            ps_misc.gate = None
        def spec(e):
            for i in range(ND):
                e.matmul(ps_misc.t[0:1, 0:TC], C.ones128_bf[:, 0:1], scr_bf[i][:, 0:TC],
                         start=(i == 0), stop=(i == ND - 1))
            m = None
            for i in range(ND):
                m = e.matmul(ps_misc.t[0:1, TC:2 * TC], C.ones128_bf[:, 0:1], scr_bf[ND + i][:, 0:TC],
                             start=(i == 0), stop=(i == ND - 1))
            return m
        stat_g = P.op("tensor", spec, waits=w, inc=True)
        g1 = P.op("scalar", lambda e: e.activation(mu, ps_misc.t[0:1, 0:TC], AF.Copy, scale=1.0 / D),
                  waits=(stat_g,), inc=True)
        g2 = P.op("scalar", lambda e: e.activation(m2, mu, AF.Square), waits=(g1,), inc=True)
        g3 = P.op("scalar", lambda e: e.activation(rs, ps_misc.t[0:1, TC:2 * TC], AF.Copy, scale=1.0 / D),
                  waits=(g2,), inc=True)
        g4 = P.op("vector", lambda e: e.tensor_sub(rs, rs, m2), waits=(g3,), inc=True)
        g5 = P.op("scalar", lambda e, eps=eps: e.activation(m2, rs, AF.Ln, bias=eps), waits=(g4,), inc=True)
        g6 = P.op("scalar", lambda e: e.activation(rs, m2, AF.Exp, scale=-0.5), waits=(g5,), inc=True)
        g7 = P.op("vector", lambda e: e.tensor_mul(m2, mu, rs), waits=(g6,), inc=True)
        b1 = emit_bcast(P, ps_misc, 0, TC, rs, C, waits=(g6,))
        b2 = emit_bcast(P, ps_misc, TC, TC, m2, C, waits=(g7, b1))
        last = None
        for i in range(ND):
            a1 = P.op("vector", lambda e, i=i, t0=t0: e.tensor_mul(x_tiles[i][:, t0:t0 + TC], x_tiles[i][:, t0:t0 + TC], ps_misc.t[:, 0:TC]),
                      waits=(b1,), inc=True)
            a2 = P.op("vector", lambda e, i=i, t0=t0: e.tensor_sub(x_tiles[i][:, t0:t0 + TC], x_tiles[i][:, t0:t0 + TC], ps_misc.t[:, TC:2 * TC]),
                      waits=(b2, a1), inc=True)
            last = P.op("vector", lambda e, i=i, t0=t0: e.tensor_copy(xn_tiles[i][:, t0:t0 + TC], x_tiles[i][:, t0:t0 + TC]),
                        waits=(a2,), inc=True)
        ps_misc.gate = last
        out_gates.append(last)
    return out_gates


def emit_F0(P, C, W, x_tiles, env, x_ready=()):
    xn = env["BA"]
    scr = env["SMALL"]
    ps_misc = env["ps_misc"]
    ps = env["ps_pair"]
    ps_small = env["ps_small"]
    ws = env["ws"]
    SF = env["SF"]
    BB = env["BB"]

    rows = env["rows"]
    ln_g = emit_ln(P, C, rows, x_tiles, xn, scr, ps_misc, x_ready=x_ready)

    part = P.sbuf("gfpart", [128, ND], F32)
    rg = None
    for i in range(ND):
        rg = P.op("vector", lambda e, i=i: e.reduce_sum(part[:, i:i + 1], xn[i][:, 0:T], axis=AX.X),
                  waits=(ln_g[-1],) if i == 0 else (), inc=True)
    cc_in = P.dram("gf_in", [128, ND])
    cc_out = P.dram("gf_out", [128, ND])
    g = P.dma("gpsimd", cc_in, part[:, :], waits=(rg,))
    ccg = P.collective("AllReduce", ALU.add, [cc_in], [cc_out], waits=(g,))
    gsum = P.sbuf("gsum", [128, ND], F32)
    g2 = P.dma("gpsimd", gsum[:, :], cc_out, waits=(ccg,))
    gctx = P.sbuf("gctx", [128, ND], BF16)
    gc_g = P.op("scalar", lambda e: e.activation(gctx[:, :], gsum[:, :], AF.Copy, scale=1.0 / SEQ),
                waits=(g2,), inc=True)

    def col_linear(w_key, b_key, IN, OUT, xcol, xcol_g, act, tag):
        ncol = (OUT + 127) // 128
        out_col = P.sbuf(tag, [128, ncol], BF16)
        gate = None
        for oc in range(ncol):
            ow = min(128, OUT - oc * 128)
            wt = ws.load(W[w_key], IN, ow, col0=oc * 128, bias_dram=W[b_key])
            w = [wt.ready, xcol_g]
            if ps_small.gate is not None:
                w.append(ps_small.gate)
            def spec(e, oc=oc, ow=ow, wt=wt):
                for ic in range(IN // 128):
                    e.matmul(ps_small.t[0:ow, 0:1], wt.lhsT(ic, 0, ow), xcol[:, ic:ic + 1],
                             start=(ic == 0), stop=False)
                return e.matmul(ps_small.t[0:ow, 0:1], ws.brow(wt, 0, ow),
                                C.ones_row_bf[0:1, 0:1], start=False, stop=True)
            mg = P.op("tensor", spec, waits=w, inc=True)
            gate = P.op("scalar", lambda e, oc=oc, ow=ow: e.activation(out_col[0:ow, oc:oc + 1], ps_small.t[0:ow, 0:1], act),
                        waits=(mg,), inc=True)
            ps_small.gate = gate
            wt.release(gate)
        return out_col, gate

    h1, h1g = col_linear("gc_w1", "gc_b1", D, 512, gctx, gc_g, AF.Gelu, "gfh1")
    h2, h2g = col_linear("gc_w2", "gc_b2", 512, 256, h1, h1g, AF.Gelu, "gfh2")
    gfc, gfg = col_linear("gc_w3", "gc_b3", 256, 1, h2, h2g, AF.Sigmoid, "gfv")

    ar_row = rows.perm(4)[0:1, 0:1]
    ar_g = P.dma("sync", ar_row, W["adaptive_range"].rearrange("(u n) -> u n", u=1))
    sc = rows.perm(2)
    s1 = P.op("vector", lambda e: e.tensor_scalar(sc[0:1, 0:1], ar_row, 0.6, None, ALU.mult),
              waits=(ar_g,), inc=True)
    s2 = P.op("scalar", lambda e: e.activation(sc[0:1, 1:2], gfc[0:1, 0:1], AF.Copy, scale=1.4, bias=-1.0),
              waits=(gfg,), inc=True)
    s3 = P.op("vector", lambda e: e.tensor_mul(sc[0:1, 1:2], sc[0:1, 1:2], ar_row), waits=(s2, s1), inc=True)
    s4 = P.op("vector", lambda e: e.tensor_scalar(sc[0:1, 1:2], sc[0:1, 1:2], 0.1, None, ALU.add), waits=(s3,), inc=True)
    cpl_ready = s4


    res_gates = []
    H = 1024
    for h0 in (0, H):
        out_f = [SF[i // 2][:, (i % 2) * H:(i % 2) * H + H] for i in range(ND)]
        ig = None
        for i in range(ND):
            ig = P.op("vector", lambda e, i=i, h0=h0: e.tensor_copy(out_f[i][:, 0:H], xn[i][:, h0:h0 + H]),
                      waits=(ln_g[-1],) if i == 0 else (), inc=True)
        prev_gate = ig
        for cyc in range(3):
            outb = [BB[i][:, 0:H] for i in range(ND)]
            t1 = [BB[i][:, H:2 * H] for i in range(ND)]
            t2 = [env["t2"][i] for i in range(4)]
            t3 = [env["t3"][i] for i in range(2)]
            cg = None
            for i in range(ND):
                cg = P.op("vector", lambda e, i=i: e.tensor_copy(outb[i][:, 0:H], out_f[i][:, 0:H]),
                          waits=(prev_gate,) if i == 0 else (), inc=True)
            eg1, eg2, eg3 = [], [], []
            linear_chunked(P, ws, W["cn_w1"], W["cn_b1"], D, D,
                           lambda ic, t0, tw: outb[ic][:, t0:t0 + tw], H, C.onesrow,
                           epi_act(P, lambda oc, t0, tw: t1[oc][:, t0:t0 + tw], AF.Gelu, gates=eg1),
                           ps, extra_waits=(cg,))
            linear_chunked(P, ws, W["cn_w2"], W["cn_b2"], D, 512,
                           lambda ic, t0, tw: t1[ic][:, t0:t0 + tw], H, C.onesrow,
                           epi_act(P, lambda oc, t0, tw: t2[oc][:, t0:t0 + tw], AF.Gelu, gates=eg2),
                           ps, extra_waits=(eg1[-1],))
            linear_chunked(P, ws, W["cn_w3"], W["cn_b3"], 512, 256,
                           lambda ic, t0, tw: t2[ic][:, t0:t0 + tw], H, C.onesrow,
                           epi_act(P, lambda oc, t0, tw: t3[oc][:, t0:t0 + tw], AF.Gelu, gates=eg3),
                           ps, extra_waits=(eg2[-1],))
            wt4 = ws.load(W["cn_w4"], 256, 1, bias_dram=W["cn_b4"])
            tf_row = env.setdefault("f0_tf", rows.seg(TC))
            cp_row = env.setdefault("f0_cp", rows.seg(TC))
            ncp_row = env.setdefault("f0_ncp", rows.seg(TC))
            f0r_prev = env.setdefault("f0r_prev", [None])
            cyc_gates = []
            tmp = env["tmp_tt"]
            for t0 in range(0, H, TC):
                w = [wt4.ready, eg3[-1]]
                if ps_small.gate is not None:
                    w.append(ps_small.gate)
                def spec(e, t0=t0, wt4=wt4):
                    for ic in range(2):
                        e.matmul(ps_small.t[0:1, 0:TC], wt4.lhsT(ic, 0, 1), t3[ic][:, t0:t0 + TC],
                                 start=(ic == 0), stop=False)
                    return e.matmul(ps_small.t[0:1, 0:TC], ws.brow(wt4, 0, 1),
                                    C.ones_row_bf[0:1, t0:t0 + TC], start=False, stop=True)
                mg = P.op("tensor", spec, waits=w, inc=True)
                sg = P.op("scalar", lambda e, t0=t0: e.activation(tf_row, ps_small.t[0:1, 0:TC], AF.Sigmoid),
                          waits=(mg, f0r_prev[0]), inc=True)
                ps_small.gate = sg
                tfg = P.op("vector", lambda e: e.tensor_scalar(cp_row, tf_row, sc[0:1, 0:1], sc[0:1, 1:2], ALU.mult, ALU.add),
                           waits=(sg, cpl_ready), inc=True)
                ng = P.op("vector", lambda e: e.tensor_scalar(ncp_row, cp_row, -1.0, None, ALU.mult),
                          waits=(tfg,), inc=True)
                bp = emit_bcast(P, ps_misc, 0, TC, cp_row, C, waits=(tfg,))
                bn = emit_bcast(P, ps_misc, TC, TC, ncp_row, C, waits=(ng,))
                f0r_prev[0] = bn
                last = None
                for (a, b, ca, cb) in ((0, 4, 0, TC), (1, 5, 0, TC), (2, 6, TC, 0), (3, 7, TC, 0)):
                    ga = P.op("vector", lambda e, b=b, t0=t0, ca=ca: e.tensor_mul(tmp[0][:, 0:TC], out_f[b][:, t0:t0 + TC], ps_misc.t[:, ca:ca + TC]),
                              waits=(bp, bn), inc=True)
                    gb = P.op("vector", lambda e, a=a, t0=t0, cb=cb: e.tensor_mul(tmp[1][:, 0:TC], out_f[a][:, t0:t0 + TC], ps_misc.t[:, cb:cb + TC]),
                              waits=(ga,), inc=True)
                    gc = P.op("vector", lambda e, a=a, t0=t0: e.tensor_add(out_f[a][:, t0:t0 + TC], out_f[a][:, t0:t0 + TC], tmp[0][:, 0:TC]),
                              waits=(gb,), inc=True)
                    last = P.op("vector", lambda e, b=b, t0=t0: e.tensor_add(out_f[b][:, t0:t0 + TC], out_f[b][:, t0:t0 + TC], tmp[1][:, 0:TC]),
                              waits=(gc,), inc=True)
                ps_misc.gate = last
                cyc_gates.append(last)
            wt4.release(cyc_gates[-1])
            prev_gate = cyc_gates[-1]
        for i in range(ND):
            g = P.op("vector", lambda e, i=i, h0=h0: e.scalar_tensor_tensor(x_tiles[i][:, h0:h0 + H], out_f[i][:, 0:H], 0.5,
                                                                            x_tiles[i][:, h0:h0 + H], ALU.mult, ALU.add),
                     waits=(prev_gate,) if i == 0 else (), inc=True)
            res_gates.append(g)
    return res_gates



def emit_A0(P, C, W, x_tiles, env, x_ready=(), upto=None):
    xn = env["BA"]
    scr = env["SMALL"]
    ps_misc = env["ps_misc"]
    ps = env["ps_pair"]
    ps_small = env["ps_small"]
    ps_acc = env["ps_acc"]
    ps_trv = env["ps_trv"]
    ws = env["ws"]
    rows = env["rows"]
    BA, BB, SF = env["BA"], env["BB"], env["SF"]
    SFB = [SF[i][:, 0:T].bitcast(BF16) for i in range(4)]
    ident = env["ident"]

    ln_g = emit_ln(P, C, rows, x_tiles, xn, scr, ps_misc, x_ready=x_ready)
    lng = ln_g[-1]

    tabs_g1 = P.dma("gpsimd", SFB[2][64:96, 0:2048], W["a0_tabs"][0:32, :])
    tabs_g2 = P.dma("gpsimd", SFB[2][64:96, 2048:4096], W["a0_tabs"][32:64, :])
    tabs_g = (tabs_g1, tabs_g2)
    perm = P.sbuf("permsb", [128, 32], BF16)
    perm_g = P.dma("gpsimd", perm[:, :], W["perm32"])

    gcol = P.sbuf("gkvcol", [128, 2], F32)
    gcol_gs = [P.dma("sync", gcol[:, c:c + 1], W["g_kv"][c * 128:(c + 1) * 128].rearrange("(p u) -> p u", u=1))
               for c in range(2)]

    kvc = [SF[0], SF[1]]
    kvg = []
    linear_chunked(P, ws, W["W_kv"], None, D, 256,
                   lambda ic, t0, tw: xn[ic][:, t0:t0 + tw], T, C.onesrow,
                   epi_act(P, lambda oc, t0, tw: kvc[oc][:, t0:t0 + tw], AF.Copy, gates=kvg),
                   ps, extra_waits=(lng,))
    sqg = []
    for i in range(2):
        sqg.append(P.op("scalar", lambda e, i=i: e.activation(BB[2 + i][:, 0:T], kvc[i][:, 0:T], AF.Square),
                        waits=(kvg[-1],), inc=True))
    kvcn = [BB[4], BB[5]]
    rms_gates = []
    srow = env["srow"][0:1, 0:TC]
    srow_prev = env["srow_prev"]
    for t0 in range(0, T, TC):
        w = [sqg[-1]]
        if ps_small.gate is not None:
            w.append(ps_small.gate)
        def spec(e, t0=t0):
            m = None
            for i in range(2):
                m = e.matmul(ps_small.t[0:1, 0:TC], C.ones128_bf[:, 0:1], BB[2 + i][:, t0:t0 + TC],
                             start=(i == 0), stop=(i == 1))
            return m
        sg = P.op("tensor", spec, waits=w, inc=True)
        rr = srow
        g1 = P.op("scalar", lambda e, rr=rr: e.activation(rr, ps_small.t[0:1, 0:TC], AF.Ln, bias=1e-6, scale=1.0 / 256),
                  waits=(sg, srow_prev[0]), inc=True)
        g2 = P.op("scalar", lambda e, rr=rr: e.activation(rr, rr, AF.Exp, scale=-0.5), waits=(g1,), inc=True)
        ps_small.gate = g2
        bg = emit_bcast(P, ps_misc, 0, TC, rr, C, waits=(g2,))
        srow_prev[0] = bg
        last = None
        for i in range(2):
            last = P.op("vector", lambda e, i=i, t0=t0: e.scalar_tensor_tensor(
                kvcn[i][:, t0:t0 + TC], kvc[i][:, t0:t0 + TC], gcol[:, i:i + 1], ps_misc.t[:, 0:TC],
                ALU.mult, ALU.mult), waits=(bg, gcol_gs[0], gcol_gs[1]), inc=True)
        ps_misc.gate = last
        rms_gates.append(last)
    kvcn_g = rms_gates[-1]

    kcon = P.dram("a0_kcon", [16 * 98, T], BF16)
    vcon = P.dram("a0_vcon", [16 * T, 65], BF16)
    kg_d = P.dram("a0_kg", [4 * 16 * 98, T], BF16)
    vg_d = P.dram("a0_vg", [4 * 16 * T, 65], BF16)
    kconv = kcon.rearrange("(h f) t -> h f t", h=16)
    vconv = vcon.rearrange("(h t) c -> h t c", h=16)

    kstr = BB[6]
    scr1 = BB[7]
    scr2 = SFB[3]
    vhead = SFB[3]
    vstr = env["small6"][0]
    kones_g = P.dma("gpsimd", kstr[96:97, 0:T], C.ones_row_bf[0:1, 0:T], waits=(lng,))
    vones_g = P.op("vector", lambda e: e.memset(vstr[:, 64:65], 1.0), inc=True)
    knmax = rows.perm(16)
    km_g = P.op("vector", lambda e: e.memset(knmax, -1e30), inc=True)
    rowbuf = P.sbuf("rowbuf", [1, TC], BF16)

    con_gates = []
    prev_kstr_dma = [None]
    prev_vstr_dma = [None]
    prev_rowbuf_dma = [None]
    for h in range(16):
        wt_kd = ws.load(W["W_kdec"], 256, 64, col0=h * 64)
        wt_kr = ws.load(W["W_krope"], D, 32, col0=h * 32)
        ev_c = []
        for t0 in range(0, T, TC):
            pp = ps[(t0 // TC) % 2]
            w = [wt_kd.ready, wt_kr.ready, kvcn_g]
            if pp.gate is not None:
                w.append(pp.gate)
            def spec(e, t0=t0, wt_kd=wt_kd, wt_kr=wt_kr, pp=pp):
                for ic in range(2):
                    e.matmul(pp.t[0:64, 0:TC], wt_kd.lhsT(ic, 0, 64), kvcn[ic][:, t0:t0 + TC],
                             start=(ic == 0), stop=(ic == 1))
                m = None
                for ic in range(8):
                    m = e.matmul(pp.t[64:96, 0:TC], wt_kr.lhsT(ic, 0, 32), xn[ic][:, t0:t0 + TC],
                                 start=(ic == 0), stop=(ic == 7))
                return m
            mg = P.op("tensor", spec, waits=w, inc=True)
            g1 = P.op("scalar", lambda e, t0=t0, pp=pp: e.activation(kstr[0:64, t0:t0 + TC], pp.t[0:64, 0:TC], AF.Copy),
                      waits=(mg, prev_kstr_dma[0]), inc=True)
            g2 = P.op("scalar", lambda e, t0=t0, pp=pp: e.activation(scr1[64:96, t0:t0 + TC], pp.t[64:96, 0:TC], AF.Copy),
                      waits=(g1,), inc=True)
            pp.gate = g2
            ev_c.append(g2)
        wt_kd.release(P.TE.now)
        wt_kr.release(P.TE.now)
        rope_gates = []
        for t0 in range(0, T, TC):
            pp = ps[(t0 // TC) % 2]
            w = [ev_c[-1], perm_g]
            if pp.gate is not None:
                w.append(pp.gate)
            mg = P.op("tensor", lambda e, t0=t0, pp=pp: e.matmul(pp.t[64:96, 0:TC], perm[64:96, 0:32],
                                                                scr1[64:96, t0:t0 + TC], start=True, stop=True),
                      waits=w, inc=True)
            v1 = P.op("vector", lambda e, t0=t0, pp=pp: e.tensor_mul(scr2[64:96, t0:t0 + TC], pp.t[64:96, 0:TC],
                                                                    SFB[2][64:96, 2048 + t0:2048 + t0 + TC]),
                      waits=(mg,) + tabs_g, inc=True)
            v2 = P.op("vector", lambda e, t0=t0: e.tensor_mul(kstr[64:96, t0:t0 + TC], scr1[64:96, t0:t0 + TC],
                                                              SFB[2][64:96, t0:t0 + TC]),
                      waits=(v1,), inc=True)
            v3 = P.op("vector", lambda e, t0=t0: e.tensor_add(kstr[64:96, t0:t0 + TC], kstr[64:96, t0:t0 + TC],
                                                              scr2[64:96, t0:t0 + TC]),
                      waits=(v2,), inc=True)
            pp.gate = v1
            rope_gates.append(v3)
        sq_g2 = P.op("scalar", lambda e: e.activation(BB[2][0:96, 0:T], kstr[0:96, 0:T], AF.Square),
                     waits=(rope_gates[-1], kones_g), inc=True)
        for t0 in range(0, T, TC):
            w = [sq_g2]
            if ps_small.gate is not None:
                w.append(ps_small.gate)
            def spec(e, t0=t0):
                return e.matmul(ps_small.t[0:1, 0:TC], C.ones128_bf[0:96, 0:1],
                                BB[2][0:96, t0:t0 + TC], start=True, stop=True)
            sg = P.op("tensor", spec, waits=w, inc=True)
            seg = env["srow"][0:1, 0:TC]
            kseg = env["srow"][0:1, TC:TC + 1]
            g1 = P.op("scalar", lambda e, seg=seg: e.activation(seg, ps_small.t[0:1, 0:TC], AF.Copy),
                      waits=(sg, env["srow_prev"][0]), inc=True)
            ps_small.gate = g1
            g2a = P.op("vector", lambda e, seg=seg, kseg=kseg: e.reduce_max(kseg, seg, axis=AX.X),
                       waits=(g1, km_g), inc=True)
            g2b = P.op("vector", lambda e, h=h, kseg=kseg: e.tensor_max(knmax[0:1, h:h + 1], knmax[0:1, h:h + 1], kseg),
                       waits=(g2a,), inc=True)
            g3 = P.op("vector", lambda e, seg=seg: e.tensor_copy(rowbuf[0:1, 0:TC], seg),
                      waits=(g1, prev_rowbuf_dma[0]), inc=True)
            env["srow_prev"][0] = g3
            cg = P.dma("gpsimd", kconv[h, 97:98, t0:t0 + TC], rowbuf[0:1, 0:TC], waits=(g3,))
            prev_rowbuf_dma[0] = cg
            con_gates.append(cg)
        cg = P.dma("gpsimd", kconv[h, 0:97, :], kstr[0:97, 0:T], waits=(rope_gates[-1], kones_g, sq_g2))
        prev_kstr_dma[0] = cg
        con_gates.append(cg)
        wt_vd = ws.load(W["W_vdec"], 256, 64, col0=h * 64)
        ev_v = []
        for t0 in range(0, T, TC):
            pp = ps[(t0 // TC) % 2]
            w = [wt_vd.ready, kvcn_g]
            if pp.gate is not None:
                w.append(pp.gate)
            def spec(e, t0=t0, wt_vd=wt_vd, pp=pp):
                m = None
                for ic in range(2):
                    m = e.matmul(pp.t[0:64, 0:TC], wt_vd.lhsT(ic, 0, 64), kvcn[ic][:, t0:t0 + TC],
                                 start=(ic == 0), stop=(ic == 1))
                return m
            mg = P.op("tensor", spec, waits=w, inc=True)
            g1 = P.op("scalar", lambda e, t0=t0, pp=pp: e.activation(vhead[0:64, t0:t0 + TC], pp.t[0:64, 0:TC], AF.Copy),
                      waits=(mg, sq_g2), inc=True)
            pp.gate = g1
            ev_v.append(g1)
        wt_vd.release(P.TE.now)
        if upto == 124:
            con_gates.append(ev_v[-1])
            break
        for ch in range(16):
            w = [ev_v[ch // 4], vones_g]
            if ps_trv.gate is not None:
                w.append(ps_trv.gate)
            mg = P.op("tensor", lambda e, ch=ch: e.transpose(ps_trv.t[:, 0:64], vhead[0:64, ch * 128:(ch + 1) * 128],
                                                             ident[0:64, 0:64]), waits=w, inc=True)
            g1 = P.op("scalar", lambda e: e.activation(vstr[:, 0:64], ps_trv.t[:, 0:64], AF.Copy),
                      waits=(mg, prev_vstr_dma[0]), inc=True)
            ps_trv.gate = g1
            cg = P.dma("gpsimd", vconv[h, ch * 128:(ch + 1) * 128, :], vstr[:, 0:65], waits=(g1,))
            prev_vstr_dma[0] = cg
            con_gates.append(cg)

    cc1 = P.collective("AllGather", ALU.bypass, [kcon], [kg_d],
                       waits=tuple(P.dma_frontier("gpsimd")) + tuple(P.dma_frontier("sync")))
    cc2 = P.collective("AllGather", ALU.bypass, [vcon], [vg_d], waits=(cc1,))
    kmin = P.dram("a0_km_in", [1, 16], F32)
    kmout = P.dram("a0_km_out", [1, 16], F32)
    kmg1 = P.dma("gpsimd", kmin, knmax, waits=(P.VE.now,))
    cc3 = P.collective("AllReduce", ALU.max, [kmin], [kmout], waits=(kmg1,))
    kmg2 = P.dma("gpsimd", knmax, kmout, waits=(cc3,))
    kmn = rows.perm(16)
    kms = P.op("scalar", lambda e: e.activation(kmn, knmax, AF.Sqrt), waits=(kmg2,), inc=True)
    kms = P.op("vector", lambda e: e.tensor_scalar(kmn, kmn, -1.0, None, ALU.mult), waits=(kms,), inc=True)

    kgv = kg_d.rearrange("(r h f) t -> r h f t", r=4, h=16)
    vgv = vg_d.rearrange("(r h t) c -> r h t c", r=4, h=16)

    dbg2 = env.get("dbg2")
    if dbg2 is not None:
        gd0 = P.dma("gpsimd", dbg2[230:328, :], kconv[5, :, :], waits=(cc1,))
        gd0b = P.dma("gpsimd", dbg2[330:428, :], kconv[0, :, :], waits=(cc1,))
        gd0c = P.dma("gpsimd", dbg2[430:494, :], kvcn[0][0:64, 0:T], waits=(cc1,))
        gd0e = P.dma("gpsimd", dbg2[494:510, :], xn[0][0:16, 0:T], waits=(kvcn_g,))
        env["dbg2_gates2"] = (gd0e,)
        gd1 = P.dma("gpsimd", dbg2[0:98, :], kgv[3, 5, :, :], waits=(cc1,))
        gd2 = P.dma("gpsimd", dbg2[100:228, 0:65], vg_d[3 * 16 * T + 5 * T: 3 * 16 * T + 5 * T + 128, 0:65], waits=(cc2,))
        env["dbg2_gates"] = (gd1, gd2)
    mask_g1 = P.dma("gpsimd", SFB[0][:, :], W["a0_masks"][:, 0:4096], waits=(kvg[-1], sqg[-1]))
    mask_g2 = P.dma("gpsimd", SFB[1][:, :], W["a0_masks"][:, 4096:8192], waits=(rms_gates[-1],))
    def mask_ap(r, w):
        off = (r * 4 + w) * TC
        buf = SFB[off // 4096]
        return buf[:, off % 4096: off % 4096 + TC]

    qc = [BB[i] for i in range(4)]
    qcg = []
    linear_chunked(P, ws, W["W_q"], None, D, 512,
                   lambda ic, t0, tw: xn[ic][:, t0:t0 + tw], T, C.onesrow,
                   epi_act(P, lambda oc, t0, tw: qc[oc][:, t0:t0 + tw], AF.Copy, gates=qcg),
                   ps, extra_waits=(lng, P.GE.now))
    qc_g = qcg[-1]

    attnq = [env["BA"][6][:, i * TC:(i + 1) * TC] for i in range(4)] +             [env["BA"][7][:, i * TC:(i + 1) * TC] for i in range(4)]
    qstrip = lambda h: BA[h // 4][:, (h % 4) * TC:(h % 4) * TC + TC]
    nbuf = P.sbuf("negb", [1, TC], BF16)
    lr_prev = [None]
    ktile = [env["small6"][1][:, 0:TC], env["small6"][1][:, TC:2 * TC]]
    vtile = [env["small6"][2][:, 0:72], env["small6"][2][:, 72:144]]
    ptile = [env["small6"][3][:, 0:TC], env["small6"][3][:, TC:2 * TC]]
    lb = env["small6"][4][:, 0:TC]

    res_gates = []
    for J in range(4):
        qcols = (J * TC, TC)
        qs_g = {}
        for h in range(16):
            wt_qd = ws.load(W["W_qdec"], 512, 64, col0=h * 64)
            wt_qr = ws.load(W["W_qrope"], 512, 32, col0=h * 32)
            pp = ps[h % 2]
            w = [wt_qd.ready, wt_qr.ready, qc_g, cc2] + list(env.get("dbg2_gates2", ()))
            if pp.gate is not None:
                w.append(pp.gate)
            def spec(e, h=h, wt_qd=wt_qd, wt_qr=wt_qr, pp=pp, J=J):
                for ic in range(4):
                    e.matmul(pp.t[0:64, 0:TC], wt_qd.lhsT(ic, 0, 64), qc[ic][:, J * TC:(J + 1) * TC],
                             start=(ic == 0), stop=(ic == 3))
                m = None
                for ic in range(4):
                    m = e.matmul(pp.t[64:96, 0:TC], wt_qr.lhsT(ic, 0, 32), qc[ic][:, J * TC:(J + 1) * TC],
                                 start=(ic == 0), stop=(ic == 3))
                return m
            mg = P.op("tensor", spec, waits=w, inc=True)
            wt_qd.release(P.TE.now)
            wt_qr.release(P.TE.now)
            g1 = P.op("scalar", lambda e, h=h, pp=pp: e.activation(qstrip(h)[0:64, :], pp.t[0:64, 0:TC], AF.Copy),
                      waits=(mg,), inc=True)
            g2 = P.op("scalar", lambda e, h=h, pp=pp: e.activation(scr1[64:96, 0:TC], pp.t[64:96, 0:TC], AF.Copy),
                      waits=(g1,), inc=True)
            pp.gate = g2
            mg2 = P.op("tensor", lambda e, pp=pp: e.matmul(pp.t[64:96, 0:TC], perm[64:96, 0:32],
                                                           scr1[64:96, 0:TC], start=True, stop=True),
                       waits=(g2,), inc=True)
            v1 = P.op("vector", lambda e, pp=pp, J=J: e.tensor_mul(scr2[64:96, 0:TC], pp.t[64:96, 0:TC],
                                                                   SFB[2][64:96, 2048 + J * TC:2048 + (J + 1) * TC]),
                      waits=(mg2,), inc=True)
            v2 = P.op("vector", lambda e, h=h, J=J: e.tensor_mul(qstrip(h)[64:96, :], scr1[64:96, 0:TC],
                                                                 SFB[2][64:96, J * TC:(J + 1) * TC]),
                      waits=(v1,), inc=True)
            v3 = P.op("vector", lambda e, h=h: e.tensor_add(qstrip(h)[64:96, :], qstrip(h)[64:96, :],
                                                            scr2[64:96, 0:TC]),
                      waits=(v2,), inc=True)
            pp.gate = v1
            sq1 = P.op("scalar", lambda e, h=h: e.activation(scr2[0:96, 2048:2048 + TC], qstrip(h)[0:96, :], AF.Square),
                       waits=(v3,), inc=True)
            w2 = [sq1]
            if ps_small.gate is not None:
                w2.append(ps_small.gate)
            sg = P.op("tensor", lambda e: e.matmul(ps_small.t[0:1, 0:TC], C.ones128_bf[0:96, 0:1],
                                                   scr2[0:96, 2048:2048 + TC], start=True, stop=True),
                      waits=w2, inc=True)
            seg = env["srow"][0:1, 0:TC]
            b1 = P.op("scalar", lambda e, seg=seg: e.activation(seg, ps_small.t[0:1, 0:TC], AF.Sqrt),
                      waits=(sg, env["srow_prev"][0]), inc=True)
            ps_small.gate = b1
            b2 = P.op("vector", lambda e, seg=seg, h=h: e.tensor_scalar(seg, seg, kmn[0:1, h:h + 1], 96.0 ** -0.5, ALU.mult, ALU.mult),
                      waits=(b1, kms), inc=True)
            b3 = P.op("vector", lambda e, seg=seg: e.tensor_copy(nbuf[0:1, 0:TC], seg),
                      waits=(b2, qs_g.get(h - 1)), inc=True)
            env["srow_prev"][0] = b3
            qs_g[h] = P.dma("gpsimd", BA[h // 4][96:97, (h % 4) * TC:(h % 4) * TC + TC], nbuf[0:1, 0:TC], waits=(b3,))
        wo_in_g = []
        kt_free = [None, None]
        vt_free = [None, None]
        for h in range(16):
            nkt = 4 * (J + 1)
            for m_i in range(nkt):
                u, r = m_i // 4, m_i % 4
                kt = ktile[m_i % 2]
                vt = vtile[m_i % 2]
                kd = P.dma("sync", kt[0:97, :], kgv[r, h, 0:97, u * TC:(u + 1) * TC],
                           waits=(cc1, qs_g[h], kt_free[m_i % 2]) if m_i == 0 else (cc1, kt_free[m_i % 2]))
                pvg = None
                for w_ in range(4):
                    vd = P.dma("sync", vt[:, 0:65], vgv[r, h, u * TC + w_ * 128:u * TC + (w_ + 1) * 128, :],
                               waits=(cc2, vt_free[m_i % 2] if w_ == 0 else pvg))
                    pp = ps[(m_i * 4 + w_) % 2]
                    ww = [kd, qs_g[h]]
                    if pp.gate is not None:
                        ww.append(pp.gate)
                    mg = P.op("tensor", lambda e, h=h, w_=w_, kt=kt, pp=pp: e.matmul(
                        pp.t[:, 0:TC], kt[0:97, w_ * 128:(w_ + 1) * 128], BA[h // 4][0:97, (h % 4) * TC:(h % 4 + 1) * TC],
                        start=True, stop=True), waits=ww, inc=True)
                    pt = ptile[(m_i * 4 + w_) % 2]
                    eg = P.op("scalar", lambda e, pt=pt, pp=pp: e.activation(pt, pp.t[:, 0:TC], AF.Exp, scale=96.0 ** -0.5),
                              waits=(mg,), inc=True)
                    pp.gate = eg
                    if u == J:
                        eg = P.op("vector", lambda e, pt=pt, r=r, w_=w_: e.tensor_mul(pt, pt, mask_ap(r, w_)),
                                  waits=(eg, mask_g1, mask_g2), inc=True)
                    w3 = [eg, vd]
                    if m_i == 0 and w_ == 0:
                        w3.append(ps_acc.gate)
                    pvg = P.op("tensor", lambda e, vt=vt, pt=pt, first=(m_i == 0 and w_ == 0),
                               last=(m_i == nkt - 1 and w_ == 3): e.matmul(
                        ps_acc.t[0:65, 0:TC], vt[:, 0:65], pt, start=first, stop=last), waits=w3, inc=True)
                    vt_free[m_i % 2] = pvg
                kt_free[m_i % 2] = pvg
            lr = rows.t[64:65, 0:TC]
            g1 = P.op("vector", lambda e, lr=lr: e.reciprocal(lr, ps_acc.t[64:65, 0:TC]),
                      waits=(pvg, lr_prev[0]), inc=True)
            bg = P.op("tensor", lambda e, lr=lr: e.matmul(ps_misc.t[0:64, 0:TC], C.ones_col_f32[64:65, 0:64], lr,
                                                          start=True, stop=True),
                      waits=((g1,) + ((ps_misc.gate,) if ps_misc.gate else ())), inc=True)
            lr_prev[0] = bg
            g2 = P.op("scalar", lambda e: e.activation(lb[0:64, 0:TC], ps_misc.t[0:64, 0:TC], AF.Copy),
                      waits=(bg,), inc=True)
            ps_misc.gate = g2
            dst = attnq[h // 2][(h % 2) * 64:(h % 2) * 64 + 64, :]
            g3 = P.op("vector", lambda e, dst=dst: e.tensor_mul(dst, ps_acc.t[0:64, 0:TC], lb[0:64, 0:TC]),
                      waits=(g2,), inc=True)
            ps_acc.gate = g3
            wo_in_g.append(g3)
        def xin_att(ic, t0, tw):
            return attnq[ic][:, t0:t0 + tw]
        rg = []
        def epi_res(P_, pp, oc, t0, tw, mmg):
            g = P.op("vector", lambda e, pp=pp, oc=oc, t0=t0, tw=tw, J=J: e.tensor_add(
                x_tiles[oc][:, J * TC + t0:J * TC + t0 + tw], x_tiles[oc][:, J * TC + t0:J * TC + t0 + tw],
                pp.t[:, 0:tw]), waits=(mmg,), inc=True)
            pp.gate = g
            rg.append(g)
        linear_chunked(P, ws, W["W_o"], None, D, D, xin_att, TC, C.onesrow, epi_res, ps,
                       extra_waits=(wo_in_g[-1],))
        res_gates.append(rg[-1])
    return res_gates



def emit_F1(P, C, W, x_tiles, env, x_ready=()):
    BA, BB, SF = env["BA"], env["BB"], env["SF"]
    SFB = [SF[i][:, 0:T].bitcast(BF16) for i in range(4)]
    scr = env["SMALL"]
    small = env["small6"]
    rows = env["rows"]
    ws = env["ws"]
    ps_misc, ps, ps_small, ps_acc, ps_trv = env["ps_misc"], env["ps_pair"], env["ps_small"], env["ps_acc"], env["ps_trv"]
    ident = env["ident"]

    if "f1_consts" not in env:
        cc = {}
        small_ = env["small6"]
        cc["sel"] = P.sbuf("f1selsb", [64, 16], BF16)
        cc["mask"] = small_[3][:, 0:1024]
        cc["mask0"] = small_[4][:, 0:1024]
        cc["bo"] = P.sbuf("f1bosb", [128, 32], BF16)
        cc["bot"] = P.sbuf("f1botsb", [8, 512], BF16)
        cc["perm"] = P.sbuf("f1permsb", [128, 128], BF16)
        cc["qsc"] = P.sbuf("f1qscsb", [128, 2], F32)
        cc["ksc"] = P.sbuf("f1kscsb", [128, 2], F32)
        cc["lng"] = P.sbuf("f1lngsb", [128, 16], F32)
        cc["lnb"] = P.sbuf("f1lnbsb", [128, 16], F32)
        cc["nrm"] = env["rows"].t[32:40, 0:TC]
        cc["mcol"] = P.sbuf("f1mcol", [128, 2], F32)
        cc["tab"] = [env["SMALL"][0], env["SMALL"][1]]
        cc["etok"] = small_[0][0:16, 0:1024]
        cc["egt"] = small_[1][0:64, 0:1024]
        cc["htok"] = small_[2][0:16, 0:1024]
        cc["hfm"] = small_[3][0:128, 520:584]
        cc["ecs"] = small_[3][0:128, 600:616]
        gs = []
        gs.append(P.dma("sync", cc["sel"][:, :], W["f1_sel"]))
        gs.append(P.dma("sync", cc["bo"][:, :], W["f1_bo"]))
        gs.append(P.dma("sync", cc["bot"][:, :], W["f1_bot"]))
        gs.append(P.dma("sync", cc["perm"][:, :], W["f1_perm"]))
        for l in range(2):
            gs.append(P.dma("sync", cc["qsc"][:, l:l + 1], W["f1_qsc"][l].rearrange("(p u) -> p u", u=1)))
            gs.append(P.dma("sync", cc["ksc"][:, l:l + 1], W["f1_ksc"][l].rearrange("(p u) -> p u", u=1)))
            gs.append(P.dma("sync", cc["lng"][:, l * 8:(l + 1) * 8], W["f1_lngt"][l]))
            gs.append(P.dma("sync", cc["lnb"][:, l * 8:(l + 1) * 8], W["f1_lnbt"][l]))
        cc["ready"] = tuple(gs)
        env["f1_consts"] = cc
    cc = env["f1_consts"]
    ccready = cc["ready"]

    if "f1_xspill" not in env:
        env["f1_xspill"] = P.dram("f1xspill", [D, T], F32)
        env["f1_ecd"] = P.dram("f1ec", [16, 1024], BF16)
        env["f1_egd"] = P.dram("f1eg", [64, 1024], BF16)
        env["f1_kcon"] = P.dram("f1kcon", [512, T], BF16)
        env["f1_vcon"] = P.dram("f1vcon", [T, 520], BF16)
        env["f1_kgd"] = P.dram("f1kg", [2048, T], BF16)
        env["f1_vgd"] = P.dram("f1vg", [4 * T, 520], BF16)
    xsp, ecd, egd = env["f1_xspill"], env["f1_ecd"], env["f1_egd"]
    kcon, vcon, kgd, vgd = env["f1_kcon"], env["f1_vcon"], env["f1_kgd"], env["f1_vgd"]
    kgv = kgd.rearrange("(r f) t -> r f t", r=4)
    vgv = vgd.rearrange("(r t) c -> r t c", r=4)

    sp_g = [P.dma("gpsimd", xsp[i * 128:(i + 1) * 128, :], x_tiles[i][:, 0:T],
                  waits=tuple(x_ready) if i == 0 else ()) for i in range(ND)]
    ln_g = emit_ln_inplace(P, C, rows, x_tiles, BA, scr, ps_misc, x_ready=tuple(sp_g))
    cur_g = ln_g[-1]

    def endcol_gather(curn_tiles, ready):
        evg = None
        for i in range(ND):
            g1 = P.op("vector", lambda e, i=i: e.tensor_copy(
                cc["ecs"].rearrange("p (c u) -> p c u", u=1),
                curn_tiles[i][:, 0:T].rearrange("p (c w) -> p c w", w=128)[:, :, 127:128]),
                waits=tuple(ready) + ((evg,) if evg else ()), inc=True)
            g2 = P.op("tensor", lambda e: e.transpose(ps_trv.t[0:16, 0:128], cc["ecs"], ident[:, :]),
                      waits=(g1,) + ((ps_trv.gate,) if ps_trv.gate else ()), inc=True)
            evg = P.op("scalar", lambda e, i=i: e.activation(cc["etok"][:, i * 128:(i + 1) * 128], ps_trv.t[0:16, 0:128], AF.Copy),
                       waits=(g2,), inc=True)
            ps_trv.gate = evg
        d1 = P.dma("gpsimd", ecd, cc["etok"], waits=(evg,))
        ccg = P.collective("AllGather", ALU.bypass, [ecd], [egd], waits=(d1,))
        d2 = P.dma("gpsimd", cc["egt"], egd, waits=(ccg,))
        pe = []
        for half in range(2):
            m1 = P.op("tensor", lambda e, half=half: e.matmul(ps_small.t[0:16, 0:TC], cc["sel"][:, :],
                                                              cc["egt"][:, half * TC:(half + 1) * TC], start=True, stop=True),
                      waits=(d2, ccready[0]) + ((ps_small.gate,) if ps_small.gate else ()), inc=True)
            e1 = P.op("scalar", lambda e, half=half: e.activation(cc["htok"][:, half * TC:(half + 1) * TC],
                                                                  ps_small.t[0:16, 0:TC], AF.Copy), waits=(m1,), inc=True)
            ps_small.gate = e1
            pe.append(e1)
        hg = None
        for i in range(4, 8):
            g2 = P.op("tensor", lambda e, i=i: e.transpose(ps_trv.t[0:128, 0:16], cc["htok"][:, i * 128:(i + 1) * 128], ident[0:16, 0:16]),
                      waits=tuple(pe) + ((ps_trv.gate,) if ps_trv.gate else ()), inc=True)
            hg = P.op("scalar", lambda e, i=i: e.activation(cc["hfm"][:, (i - 4) * 16:(i - 3) * 16], ps_trv.t[0:128, 0:16], AF.Copy),
                      waits=(g2,), inc=True)
            ps_trv.gate = hg
        return cc["hfm"], hg

    def hshift(curn_tiles, dst_tiles, halo_fm, halo_g, ready):
        gates = []
        for i in range(4):
            g1 = P.op("vector", lambda e, i=i: e.tensor_copy(dst_tiles[i][:, 1:T], curn_tiles[4 + i][:, 0:T - 1]),
                      waits=tuple(ready) + ((halo_g,) if i == 0 else ()), inc=True)
            g2 = P.op("vector", lambda e, i=i: e.tensor_copy(
                dst_tiles[i][:, 0:T].rearrange("p (c w) -> p c w", w=128)[:, :, 0:1],
                halo_fm[:, i * 16:(i + 1) * 16].rearrange("p (c u) -> p c u", u=1)),
                waits=(g1, halo_g), inc=True)
            gates.append(g2)
        return gates

    res_gates = []
    for l in range(2):
        curn = BA
        halo_fm, halo_g = endcol_gather(curn, (cur_g,))
        hs_g = hshift(curn, [BB[i] for i in range(4)], halo_fm, halo_g, (cur_g,))
        qt = [SFB[2][:, 0:T], SFB[2][:, T:2 * T], SFB[3][:, 0:T], SFB[3][:, T:2 * T]]
        kt = [BB[4][:, 0:T], BB[5][:, 0:T], BB[6][:, 0:T], BB[7][:, 0:T]]
        vt = [SFB[0][:, 0:T], SFB[0][:, T:2 * T], SFB[1][:, 0:T], SFB[1][:, T:2 * T]]
        qkv_dst = qt + kt + vt

        def xin_h(ic, t0, tw):
            return (curn[ic][:, t0:t0 + tw] if ic < 4 else BB[ic - 4][:, t0:t0 + tw])
        qkvg = []
        step = (ws.max_cols // 8) // 128 * 128
        first = True
        for col0 in range(0, 1536, step):
            wt = ws.load(W["wqkv"][l], D, step, col0=col0)
            def epi(P_, pp, oc, t0, tw, mmg, col0=col0):
                dst = qkv_dst[col0 // 128 + oc]
                g = P.op("scalar", lambda e, pp=pp, dst=dst, t0=t0, tw=tw: e.activation(dst[:, t0:t0 + tw], pp.t[:, 0:tw], AF.Copy),
                         waits=(mmg,), inc=True)
                pp.gate = g
                qkvg.append(g)
            emit_linear(P, wt, xin_h, step, T, [], epi, ps,
                        extra_waits=tuple(hs_g) + (cur_g,) if first else ())
            first = False
            wt.release(P.TE.now)
        qkv_done = qkvg[-1]

        def lnorm_rope(tiles, sc_col, extra_mul, tab_q, done_gate):
            gates = []
            sqs = small[4][:, 0:TC]
            for t0 in range(0, T, TC):
                prevg = mg = None
                for ic in range(4):
                    w0 = [done_gate] if ic == 0 else [mg]
                    if ic == 0 and gates:
                        w0.append(gates[-1])
                    sqg = P.op("scalar", lambda e, ic=ic, t0=t0: e.activation(sqs, tiles[ic][:, t0:t0 + TC], AF.Square),
                               waits=tuple(w0), inc=True)
                    w1 = [sqg]
                    if ic == 0 and ps_small.gate is not None:
                        w1.append(ps_small.gate)
                    mg = P.op("tensor", lambda e, ic=ic: e.matmul(ps_small.t[0:8, 0:TC], cc["bo"][:, ic * 8:(ic + 1) * 8],
                                                                 sqs, start=(ic == 0), stop=(ic == 3)),
                              waits=tuple(w1), inc=True)
                n1 = P.op("scalar", lambda e: e.activation(cc["nrm"][0:8, 0:TC], ps_small.t[0:8, 0:TC], AF.Sqrt),
                          waits=(mg,), inc=True)
                ps_small.gate = n1
                n2 = P.op("vector", lambda e: e.tensor_scalar(cc["nrm"][0:8, 0:TC], cc["nrm"][0:8, 0:TC], 1e-12, None, ALU.max),
                          waits=(n1,), inc=True)
                n3 = P.op("vector", lambda e: e.reciprocal(cc["nrm"][0:8, 0:TC], cc["nrm"][0:8, 0:TC]), waits=(n2,), inc=True)
                if extra_mul != 1.0:
                    n3 = P.op("vector", lambda e, m=extra_mul: e.tensor_scalar(cc["nrm"][0:8, 0:TC], cc["nrm"][0:8, 0:TC], m, None, ALU.mult),
                              waits=(n3,), inc=True)
                nb = env["SMALL"][2][0:8, 0:TC]
                n3 = P.op("vector", lambda e, nb=nb: e.tensor_copy(nb, cc["nrm"][0:8, 0:TC]), waits=(n3,), inc=True)
                for ic in range(4):
                    bg = P.op("tensor", lambda e, ic=ic, nb=nb: e.matmul(ps_misc.t[:, 0:TC], cc["bot"][:, ic * 128:(ic + 1) * 128],
                                                                        nb, start=True, stop=True),
                              waits=(n3,) + ((ps_misc.gate,) if ps_misc.gate else ()), inc=True)
                    ps_misc.gate = None
                    a1 = P.op("vector", lambda e, ic=ic, t0=t0: e.scalar_tensor_tensor(
                        tiles[ic][:, t0:t0 + TC], tiles[ic][:, t0:t0 + TC], sc_col, ps_misc.t[:, 0:TC], ALU.mult, ALU.mult),
                        waits=(bg, ccready[-1]), inc=True)
                    tw0 = (done_gate, gates[-1] if gates else None)
                    tcg1 = P.dma("sync", cc["tab"][0][:, :], W["f1_tabs"][tab_q * 1024 + ic * 128: tab_q * 1024 + (ic + 1) * 128, t0:t0 + TC], waits=tw0)
                    tcg2 = P.dma("sync", cc["tab"][1][:, :], W["f1_tabs"][tab_q * 1024 + 512 + ic * 128: tab_q * 1024 + 512 + (ic + 1) * 128, t0:t0 + TC], waits=tw0)
                    rg = P.op("tensor", lambda e, ic=ic, t0=t0: e.matmul(ps_misc.t[:, TC:2 * TC], cc["perm"][:, :],
                                                                        tiles[ic][:, t0:t0 + TC], start=True, stop=True),
                              waits=(a1,), inc=True)
                    v1 = P.op("vector", lambda e: e.tensor_mul(small[5][:, 0:TC], ps_misc.t[:, TC:2 * TC], cc["tab"][1][:, :]),
                              waits=(rg, tcg2, tcg1), inc=True)
                    v2 = P.op("vector", lambda e, ic=ic, t0=t0: e.tensor_mul(tiles[ic][:, t0:t0 + TC], tiles[ic][:, t0:t0 + TC], cc["tab"][0][:, :]),
                              waits=(rg, v1), inc=True)
                    v3 = P.op("vector", lambda e, ic=ic, t0=t0: e.tensor_add(tiles[ic][:, t0:t0 + TC], tiles[ic][:, t0:t0 + TC], small[5][:, 0:TC]),
                              waits=(v2,), inc=True)
                    ps_misc.gate = v3
                    gates.append(v3)
            return gates

        qg = lnorm_rope(qt, cc["qsc"][:, l:l + 1], 8.0, 0, qkv_done)
        kg = lnorm_rope(kt, cc["ksc"][:, l:l + 1], 1.0, 1, qg[-1])

        con_g = []
        for i in range(4):
            con_g.append(P.dma("gpsimd", kcon[i * 128:(i + 1) * 128, :], kt[i], waits=(kg[-1],)))
        vstr = small[3][:, 0:520]
        vz = P.op("vector", lambda e: e.memset(vstr.rearrange("p (h c) -> p h c", c=65)[:, :, 64:65], 1.0),
                  waits=(qg[-1],), inc=True)
        pvd = [None]
        for chk in range(16):
            evg = None
            for hh in range(8):
                g2 = P.op("tensor", lambda e, chk=chk, hh=hh, b=(hh % 2) * 64: e.transpose(
                    ps_trv.t[:, hh * 64:(hh + 1) * 64],
                    vt[hh // 2][b:b + 64, chk * 128:(chk + 1) * 128], ident[b:b + 64, b:b + 64]),
                    waits=(qkv_done,) + ((ps_trv.gate,) if ps_trv.gate else ()), inc=True)
                evg = P.op("scalar", lambda e, hh=hh: e.activation(vstr[:, hh * 65:hh * 65 + 64], ps_trv.t[:, hh * 64:(hh + 1) * 64], AF.Copy),
                           waits=(g2, vz, pvd[0]), inc=True)
                ps_trv.gate = evg
            d = P.dma("gpsimd", vcon[chk * 128:(chk + 1) * 128, :], vstr, waits=(evg,))
            pvd[0] = d
            con_g.append(d)
        ccg = P.collective("AllGather", ALU.bypass, [kcon], [kgd], waits=tuple(P.dma_frontier("gpsimd")))
        ccg2 = P.collective("AllGather", ALU.bypass, [vcon], [vgd], waits=(ccg,))
        mk_g1 = P.dma("sync", cc["mask"], W["f1_mask"], waits=(pvd[0],))
        mk_g2 = P.dma("sync", cc["mask0"], W["f1_mask0"], waits=(qg[-1],))

        kwin = [BB[7][:, 0:1024], BB[7][:, 1024:2048], small[0][:, 0:1024], small[1][:, 0:1024]]
        vwin = [BB[4][:, 0:520], BB[4][:, 520:1040], BB[4][:, 1040:1560],
                BB[5][:, 0:520], BB[5][:, 520:1040], BB[5][:, 1040:1560],
                BB[6][:, 0:520], BB[6][:, 520:1040]]
        ptil = small[2][:, 0:1024]
        negm = cc["mcol"]
        o_done = []
        kw_free = [None]
        vw_free = [None]
        lrp = [None]
        for j in range(NCH):
            c0 = max(j - 1, 0) * 128
            kd = []
            for i in range(4):
                kd.append(P.dma("sync", kwin[i].rearrange("p (s c) -> p s c", s=4),
                                kgv[:, i * 128:(i + 1) * 128, c0:c0 + 256].rearrange("s p c -> p s c"),
                                waits=(ccg, kg[-1], kw_free[0])))
            vd = []
            for s4 in range(4):
                for c2 in range(2):
                    vd.append(P.dma("sync", vwin[s4 * 2 + c2],
                                    vgv[s4, c0 + c2 * 128:c0 + (c2 + 1) * 128, :],
                                    waits=(ccg2, vw_free[0])))
            kw_free[0] = None
            pvg = None
            for h in range(8):
                w = [qg[-1], kd[h // 2], vd[-1]]
                if ps_misc.gate is not None:
                    w.append(ps_misc.gate)
                def smspec(e, h=h, j=j):
                    q_ap = qt[h // 2][(h % 2) * 64:(h % 2) * 64 + 64, j * 128:(j + 1) * 128]
                    e.matmul(ps_misc.t[:, 0:TC], q_ap, kwin[h // 2][(h % 2) * 64:(h % 2) * 64 + 64, 0:TC],
                             start=True, stop=True)
                    return e.matmul(ps_misc.t[:, TC:2 * TC], q_ap, kwin[h // 2][(h % 2) * 64:(h % 2) * 64 + 64, TC:1024],
                                    start=True, stop=True)
                smg = P.op("tensor", smspec, waits=w, inc=True)
                mk = P.op("vector", lambda e, j=j: e.tensor_add(ps_misc.t[:, 0:1024], ps_misc.t[:, 0:1024],
                                                                cc["mask0"] if j == 0 else cc["mask"]),
                          waits=(smg, mk_g1, mk_g2), inc=True)
                mx = P.op("vector", lambda e: e.reduce_max(negm[:, 0:1], ps_misc.t[:, 0:1024], axis=AX.X),
                          waits=(mk,), inc=True)
                ng = P.op("vector", lambda e: e.tensor_scalar(negm[:, 1:2], negm[:, 0:1], -1.0, None, ALU.mult),
                          waits=(mx,), inc=True)
                ex = P.op("scalar", lambda e: e.activation(ptil, ps_misc.t[:, 0:1024], AF.Exp, bias=negm[:, 1:2]),
                          waits=(ng, pvg), inc=True)
                ps_misc.gate = ex
                trg = None
                for tix in range(8):
                    trg = P.op("tensor", lambda e, tix=tix: e.transpose(ps_trv.t[:, tix * 128:(tix + 1) * 128],
                                                                       ptil[:, tix * 128:(tix + 1) * 128], ident[:, :]),
                               waits=(ex,) + ((ps_trv.gate,) if tix == 0 and ps_trv.gate else ()), inc=True)
                pev = P.op("scalar", lambda e: e.activation(ptil, ps_trv.t[:, 0:1024], AF.Copy),
                           waits=(trg,), inc=True)
                ps_trv.gate = pev
                for tix in range(8):
                    w2 = [pev, vd[tix]]
                    if tix == 0 and ps_acc.gate is not None:
                        w2.append(ps_acc.gate)
                    pvg = P.op("tensor", lambda e, tix=tix, h=h: e.matmul(
                        ps_acc.t[0:65, 0:128], vwin[tix][:, h * 65:(h + 1) * 65], ptil[:, tix * 128:(tix + 1) * 128],
                        start=(tix == 0), stop=(tix == 7)), waits=tuple(w2), inc=True)
                if h == 7:
                    kw_free[0] = pvg
                    vw_free[0] = pvg
                lr = rows.t[64:65, 0:128]
                g1 = P.op("vector", lambda e, lr=lr: e.reciprocal(lr, ps_acc.t[64:65, 0:128]), waits=(pvg, lrp[0]), inc=True)
                bg = P.op("tensor", lambda e, lr=lr: e.matmul(ps_small.t[0:64, 0:128], C.ones_col_f32[64:65, 0:64], lr,
                                                             start=True, stop=True),
                          waits=((g1,) + ((ps_small.gate,) if ps_small.gate else ())), inc=True)
                lrp[0] = bg
                g2 = P.op("scalar", lambda e: e.activation(small[5][0:64, 512:640], ps_small.t[0:64, 0:128], AF.Copy),
                          waits=(bg,), inc=True)
                ps_small.gate = g2
                dst = BA[h // 2][(h % 2) * 64:(h % 2) * 64 + 64, j * 128:(j + 1) * 128]
                g3 = P.op("vector", lambda e, dst=dst: e.tensor_mul(dst, ps_acc.t[0:64, 0:128], small[5][0:64, 512:640]),
                          waits=(g2,), inc=True)
                ps_acc.gate = g3
                o_done.append(g3)

        wog = []
        step = (ws.max_cols // 4) // 128 * 128
        first = True
        for col0 in range(0, D, step):
            wt = ws.load(W["wo"][l], 512, step, col0=col0)
            def epi(P_, pp, oc, t0, tw, mmg, col0=col0):
                g = P.op("vector", lambda e, pp=pp, oc=col0 // 128 + oc, t0=t0, tw=tw: e.tensor_add(
                    x_tiles[oc][:, t0:t0 + tw], x_tiles[oc][:, t0:t0 + tw], pp.t[:, 0:tw]), waits=(mmg,), inc=True)
                pp.gate = g
                wog.append(g)
            emit_linear(P, wt, lambda ic, t0, tw: BA[ic][:, t0:t0 + tw], step, T, [], epi, ps,
                        extra_waits=(o_done[-1],) if first else ())
            first = False
            wt.release(P.TE.now)
        x1_done = wog[-1]

        c2g = []
        for i in range(ND):
            c2g.append(P.op("vector", lambda e, i=i: e.tensor_copy(BA[i][:, 0:T], x_tiles[i][:, 0:T]),
                            waits=(x1_done,) if i == 0 else (), inc=True))
        halo2, halo2_g = endcol_gather(BA, (c2g[-1],))
        hs2 = hshift(BA, [BB[i] for i in range(4)], halo2, halo2_g, (c2g[-1],))
        hten = [BA[0][:, 0:T], BA[1][:, 0:T], BA[2][:, 0:T], BA[3][:, 0:T],
                BB[0][:, 0:T], BB[1][:, 0:T], BB[2][:, 0:T], BB[3][:, 0:T]]
        hn = [BB[4][:, 0:T], BB[5][:, 0:T], BB[6][:, 0:T], BB[7][:, 0:T],
              SFB[0][:, 0:T], SFB[0][:, T:2 * T], SFB[1][:, 0:T], SFB[1][:, T:2 * T]]
        mu = rows.seg(TC)
        m2r = rows.seg(TC)
        rsr = rows.seg(TC)
        hng = []
        for t0 in range(0, T, TC):
            w0 = [hs2[-1], c2g[-1]]
            if ps_misc.gate is not None:
                w0.append(ps_misc.gate)
                ps_misc.gate = None
            def spec2(e, t0=t0):
                m = None
                for i in range(ND):
                    m = e.matmul(ps_misc.t[0:1, 0:TC], C.ones128_bf[:, 0:1], hten[i][:, t0:t0 + TC],
                                 start=(i == 0), stop=(i == ND - 1))
                return m
            stat1 = P.op("tensor", spec2, waits=tuple(w0), inc=True)
            sq_done = None
            for i in range(ND):
                sq_done = P.op("scalar", lambda e, i=i, t0=t0: e.activation(small[4][:, 0:TC], hten[i][:, t0:t0 + TC], AF.Square),
                               waits=((hs2[-1], hng[-1] if hng else None) if i == 0 else (sq_done,)), inc=True)
                sq_done = P.op("tensor", lambda e, i=i: e.matmul(ps_misc.t[0:1, TC:2 * TC], C.ones128_bf[:, 0:1], small[4][:, 0:TC],
                                                                start=(i == 0), stop=(i == ND - 1)), waits=(sq_done,), inc=True)
            g1 = P.op("scalar", lambda e: e.activation(mu, ps_misc.t[0:1, 0:TC], AF.Copy, scale=1.0 / D), waits=(stat1, sq_done), inc=True)
            g2 = P.op("scalar", lambda e: e.activation(m2r, mu, AF.Square), waits=(g1,), inc=True)
            g3 = P.op("scalar", lambda e: e.activation(rsr, ps_misc.t[0:1, TC:2 * TC], AF.Copy, scale=1.0 / D), waits=(g2,), inc=True)
            g4 = P.op("vector", lambda e: e.tensor_sub(rsr, rsr, m2r), waits=(g3,), inc=True)
            g5 = P.op("scalar", lambda e: e.activation(m2r, rsr, AF.Ln, bias=1e-5), waits=(g4,), inc=True)
            g6 = P.op("scalar", lambda e: e.activation(rsr, m2r, AF.Exp, scale=-0.5), waits=(g5,), inc=True)
            g7 = P.op("vector", lambda e: e.tensor_mul(m2r, mu, rsr), waits=(g6,), inc=True)
            b1g = emit_bcast(P, ps_misc, 0, TC, rsr, C, waits=(g6,))
            b2g = emit_bcast(P, ps_misc, TC, TC, m2r, C, waits=(g7, b1g))
            for i in range(ND):
                a1 = P.op("vector", lambda e, i=i, t0=t0: e.tensor_mul(hn[i][:, t0:t0 + TC], hten[i][:, t0:t0 + TC], ps_misc.t[:, 0:TC]),
                          waits=(b1g,), inc=True)
                a2 = P.op("vector", lambda e, i=i, t0=t0: e.tensor_sub(hn[i][:, t0:t0 + TC], hn[i][:, t0:t0 + TC], ps_misc.t[:, TC:2 * TC]),
                          waits=(b2g, a1), inc=True)
                a3 = P.op("vector", lambda e, i=i, t0=t0, l=l: e.tensor_scalar(hn[i][:, t0:t0 + TC], hn[i][:, t0:t0 + TC],
                                                                              cc["lng"][:, l * 8 + i:l * 8 + i + 1],
                                                                              cc["lnb"][:, l * 8 + i:l * 8 + i + 1],
                                                                              ALU.mult, ALU.add),
                          waits=(a2, ccready[-1]), inc=True)
                hng.append(a3)
            ps_misc.gate = hng[-1]
        hn_done = hng[-1]

        NOCH = 22
        h2slots = [BA[i // 2][:, (i % 2) * 1024:(i % 2) * 1024 + 1024] for i in range(16)] + \
                  [BB[i][:, 0:1024] for i in range(4)] + [BB[0][:, 1024:2048], BB[1][:, 1024:2048]]
        ffg = []
        w2_last = [None]
        for h0 in (0, 1024):
            h2_done = []
            for och in range(NOCH):
                aw = 128 if och < 21 else 42
                wta = ws.load(W["w1"][l], D, aw, col0=och * 128, bias_dram=W["b1"][l])
                wtg = ws.load(W["w1"][l], D, aw, col0=2730 + och * 128, bias_dram=W["b1"][l])
                for t0 in range(0, 1024, TC):
                    pa, pg_ = ps[0], ps[1]
                    wl = [wta.ready, wtg.ready, hn_done]
                    if pa.gate is not None:
                        wl.append(pa.gate)
                    if pg_.gate is not None:
                        wl.append(pg_.gate)
                    def specag(e, t0=t0, h0=h0, aw=aw, wta=wta, wtg=wtg):
                        for ic in range(8):
                            e.matmul(pa.t[0:aw, 0:TC], wta.lhsT(ic, 0, aw), hn[ic][:, h0 + t0:h0 + t0 + TC],
                                     start=(ic == 0), stop=False)
                        e.matmul(pa.t[0:aw, 0:TC], ws.brow(wta, 0, aw), C.ones_row_bf[0:1, 0:TC], start=False, stop=True)
                        for ic in range(8):
                            e.matmul(pg_.t[0:aw, 0:TC], wtg.lhsT(ic, 0, aw), hn[ic][:, h0 + t0:h0 + t0 + TC],
                                     start=(ic == 0), stop=False)
                        return e.matmul(pg_.t[0:aw, 0:TC], ws.brow(wtg, 0, aw), C.ones_row_bf[0:1, 0:TC], start=False, stop=True)
                    mg = P.op("tensor", specag, waits=tuple(wl), inc=True)
                    ga = P.op("scalar", lambda e, aw=aw: e.activation(small[5][0:aw, 0:TC], pa.t[0:aw, 0:TC], AF.Copy),
                              waits=(mg, h2_done[-1] if h2_done else None), inc=True)
                    gg = P.op("scalar", lambda e, aw=aw: e.activation(small[5][0:aw, TC:2 * TC], pg_.t[0:aw, 0:TC], AF.Gelu),
                              waits=(ga,), inc=True)
                    pa.gate = gg
                    pg_.gate = gg
                    hv = P.op("vector", lambda e, och=och, t0=t0, aw=aw: e.tensor_mul(
                        h2slots[och][0:aw, t0:t0 + TC], small[5][0:aw, 0:TC], small[5][0:aw, TC:2 * TC]),
                        waits=(gg, w2_last[0]), inc=True)
                    h2_done.append(hv)
                wta.release(P.TE.now)
                wtg.release(P.TE.now)
            h2g = h2_done[-1]
            for oc in range(8):
                for t0 in range(0, 1024, TC):
                    pp = ps[(oc * 2 + t0 // TC) % 2]
                    mg = None
                    for och in range(NOCH):
                        aw = 128 if och < 21 else 42
                        wtr = ws.load_rows(W["w2"][l], och * 128, aw, 128, col0=oc * 128,
                                           bias_dram=(W["b2"][l] if och == 0 else None))
                        wl2 = [wtr.ready, h2g]
                        if och == 0 and pp.gate is not None:
                            wl2.append(pp.gate)
                        def spec(e, och=och, t0=t0, aw=aw, wtr=wtr, h0=h0, pp=pp):
                            m = e.matmul(pp.t[:, 0:TC], wtr.buf[0:aw, 0:128], h2slots[och][0:aw, t0:t0 + TC],
                                         start=(och == 0), stop=(och == NOCH - 1))
                            if och == 0:
                                m = e.matmul(pp.t[:, 0:TC], ws.brow(wtr, 0, 128), C.ones_row_bf[0:1, 0:TC],
                                             start=False, stop=False)
                            return m
                        mg = P.op("tensor", spec, waits=tuple(wl2), inc=True)
                        wtr.release(P.TE.now)
                    w2_last[0] = mg
                    g = P.op("vector", lambda e, oc=oc, t0=t0, h0=h0, pp=pp: e.tensor_add(
                        x_tiles[oc][:, h0 + t0:h0 + t0 + TC], x_tiles[oc][:, h0 + t0:h0 + t0 + TC],
                        pp.t[:, 0:TC]), waits=(mg,), inc=True)
                    pp.gate = g
                    ffg.append(g)
        ffn_done = ffg[-1]
        if l == 0:
            for i in range(ND):
                cur_g = P.op("vector", lambda e, i=i: e.tensor_copy(BA[i][:, 0:T], x_tiles[i][:, 0:T]),
                             waits=(ffn_done,) if i == 0 else (), inc=True)

    rg = []
    prev = [None, None]
    for i in range(ND):
        d = P.dma("sync", SF[2 + (i % 2)][:, 0:T], xsp[i * 128:(i + 1) * 128, :],
                  waits=(ffn_done, prev[i % 2]))
        g = P.op("vector", lambda e, i=i: e.scalar_tensor_tensor(
            x_tiles[i][:, 0:T], x_tiles[i][:, 0:T], 0.5, SF[2 + (i % 2)][:, 0:T], ALU.mult, ALU.add),
            waits=(d,), inc=True)
        prev[i % 2] = g
        rg.append(g)
    return rg



def emit_A1(P, C, W, x_tiles, env, x_ready=(), upto=None):
    BA, BB, SF = env["BA"], env["BB"], env["SF"]
    SFB = [SF[i][:, 0:T].bitcast(BF16) for i in range(4)]
    small = env["small6"]
    rows = env["rows"]
    ws = env["ws"]
    ps_misc, ps, ps_small, ps_acc, ps_trv = env["ps_misc"], env["ps_pair"], env["ps_small"], env["ps_acc"], env["ps_trv"]
    ident = env["ident"]
    NG_ = 16

    if "a1_alloc" not in env:
        al = {}
        a1c = {"cur": 0}
        def pf(n, rows_=128):
            c0 = a1c["cur"]
            a1c["cur"] = c0 + n
            assert a1c["cur"] <= rows.reserve
            return rows.t[0:rows_, c0:c0 + n]
        tp0, tp1 = env["tmp_tt"]
        al["tmaa"] = pf(48)
        al["hdb"] = pf(4, 16)
        al["gst"] = pf(48)
        al["pct"] = pf(16, 64)
        al["ecol"] = pf(64)
        al["eb"] = pf(128).bitcast(BF16)[0:16, 0:256]
        al["ebs"] = pf(8).bitcast(BF16)[:, 0:16]
        al["dbcs"] = pf(16)
        al["gcol"] = pf(8)
        al["bcol"] = pf(8)
        al["srow_s"] = pf(16, 1)
        SF_ = env["SF"]
        al["coloc"] = SF_[3][0:16, 0:T]
        al["cg"] = SF_[2][0:16, 1024:1280]
        al["z16"] = SF_[2][0:16, 1280:1408]
        al["pc"] = SF_[2][0:16, 1408:1472]
        al["gcs"] = SF_[2][0:16, 1472:1544]
        al["pfx"] = SF_[2][0:16, 1544:1560]
        al["A"] = SF_[2][0:64, 1024:2048]
        al["As"] = SF_[1][0:64, 1024:2048]
        al["BlF"] = SF_[2][64:128, 1024:1536].bitcast(BF16)
        al["BlS"] = SF_[1][64:128, 1024:1536].bitcast(BF16)
        al["y"] = SF_[0][:, 1024:2048]
        al["lsc"] = SF_[0][0:64, 1024:2048]
        al["alpha"] = tp0.bitcast(BF16)[:, 0:1024]
        al["wmroll"] = tp1.bitcast(BF16)[0:16, 0:1024]
        al["mut"] = tp1[0:16, 0:128]
        al["rst"] = tp1[0:16, 128:256]
        al["ysc"] = P.sbuf("a1ysc", [128, 1024], BF16)
        env["a1_alloc"] = al
        env["a1_sels"] = {
            "a1_selp": pf(8).bitcast(BF16)[:, 0:16],
            "a1_seln": pf(8).bitcast(BF16)[:, 0:16],
            "a1_bo16": pf(64).bitcast(BF16)[:, 0:128],
            "a1_expb": ws.bf[0][0:16, 0:1024],
            "a1_expbf": ws.f32[0][0:16, 0:1024],
            "a1_selpre": pf(16, 64),
            "a1_selgrp": pf(16, 64),
        }
        env["identf"] = pf(128)
        sg_ = [P.dma("sync", env["identf"][:, :], env["a1_sel_dram"]["identf"])]
        for k2 in ("a1_selp", "a1_seln", "a1_bo16", "a1_selpre", "a1_selgrp"):
            sg_.append(P.dma("sync", env["a1_sels"][k2][:, :], env["a1_sel_dram"][k2]))
        env["a1_selg"] = tuple(sg_)
        env["a1_dram"] = {
            "ktc": P.dram("a1ktc", [T, D], BF16), "ktg": P.dram("a1ktg", [4 * T, D], BF16),
            "vtc": P.dram("a1vtc", [T, D], BF16), "vtg": P.dram("a1vtg", [4 * T, D], BF16),
            "wmc": P.dram("a1wmc", [16, T], BF16), "wmg": P.dram("a1wmg", [64, T], BF16),
            "r": P.dram("a1r", [T, D], BF16), "g": P.dram("a1g", [D, T], BF16),
            "bf": P.dram("a1bf", [NG_ * 128, 1024], BF16), "bs": P.dram("a1bs", [NG_ * 128, 1024], BF16),
            "ec": P.dram("a1ec", [32, 1024], BF16), "eg": P.dram("a1eg", [128, 1024], BF16),
            "dbg": env.get("a1_dbg_out") if env.get("a1_dbg_out") is not None else P.dram("a1dbg", [128, 4096], F32),
        }
    al = env["a1_alloc"]
    dr = env["a1_dram"]

    ln_g = emit_ln(P, C, rows, x_tiles, BA, env["SMALL"], ps_misc, x_ready=x_ready)
    lng = ln_g[-1]
    xn = BA

    colg = []
    for vi, key in enumerate(("time_maa_x", "time_maa_w", "time_maa_k", "time_maa_v", "time_maa_r", "time_maa_g")):
        vec = W[key].rearrange("a b d -> (a b d)") if len(W[key].shape) == 3 else W[key]
        for i in range(8):
            colg.append(P.dma("sync", al["tmaa"][:, vi * 8 + i:vi * 8 + i + 1],
                              vec[i * 128:(i + 1) * 128].rearrange("(p u) -> p u", u=1)))
    hg1 = P.dma("sync", al["hdb"][:, 0:1], W["head_decay_bias"].rearrange("(p u) -> p u", u=1))
    hg2 = P.dma("sync", al["hdb"][:, 1:2], W["slow_scale"].rearrange("(p u) -> p u", u=1))
    hg3 = P.op("scalar", lambda e: e.activation(al["hdb"][:, 2:3], al["hdb"][:, 0:1], AF.Exp), waits=(hg1,), inc=True)
    hg4 = P.op("vector", lambda e: e.tensor_scalar(al["hdb"][:, 2:3], al["hdb"][:, 2:3], -1.0 / 64, None, ALU.mult),
               waits=(hg3,), inc=True)
    hg5 = P.op("scalar", lambda e: e.activation(al["hdb"][:, 1:2], al["hdb"][:, 1:2], AF.Sigmoid), waits=(hg2,), inc=True)

    arow = rows.seg(1024)
    ag1 = P.dma("sync", arow, W["decay_mix"].rearrange("(u n) -> u n", u=1), waits=(lng,))
    ag2 = P.op("scalar", lambda e: e.activation(arow, arow, AF.Sigmoid), waits=(ag1,), inc=True)
    ab1 = emit_bcast(P, ps_misc, 0, TC, arow[0:1, 0:TC], C, waits=(ag2,))
    ab2 = emit_bcast(P, ps_misc, TC, TC, arow[0:1, TC:1024], C, waits=(ag2, ab1))
    ag3 = P.op("scalar", lambda e: e.activation(al["alpha"][:, 0:TC], ps_misc.t[:, 0:TC], AF.Copy), waits=(ab1,), inc=True)
    ag4 = P.op("scalar", lambda e: e.activation(al["alpha"][:, TC:1024], ps_misc.t[:, TC:2 * TC], AF.Copy), waits=(ab2,), inc=True)
    ps_misc.gate = ag4
    gbl = None
    for oc in range(8):
        gbl = P.dma("sync", al["gcol"][:, oc:oc + 1],
                    W["ln_gamma"][oc * 128:(oc + 1) * 128].rearrange("(p u) -> p u", u=1), waits=(gbl,))
        gbl = P.dma("sync", al["bcol"][:, oc:oc + 1],
                    W["ln_beta"][oc * 128:(oc + 1) * 128].rearrange("(p u) -> p u", u=1), waits=(gbl,))

    ecs = al["ecol"][:, 0:8].bitcast(BF16)[:, 0:16]
    est = small[0][0:48, 0:1024]
    evg = None
    for i in range(ND):
        g1 = P.op("vector", lambda e, i=i: e.tensor_copy(ecs.rearrange("p (c u) -> p c u", u=1),
                                                         xn[i][:, 0:T].rearrange("p (c w) -> p c w", w=128)[:, :, 127:128]),
                  waits=(lng,) + ((evg,) if evg else ()), inc=True)
        g2 = P.op("tensor", lambda e: e.transpose(ps_trv.t[0:16, 0:128], ecs, ident[:, :]),
                  waits=(g1,) + ((ps_trv.gate,) if ps_trv.gate else ()), inc=True)
        evg = P.op("scalar", lambda e, i=i: e.activation(est[0:16, i * 128:(i + 1) * 128], ps_trv.t[0:16, 0:128], AF.Copy),
                   waits=(g2,), inc=True)
        ps_trv.gate = evg
        g1b = P.op("vector", lambda e, i=i: e.tensor_copy(ecs.rearrange("p (c u) -> p c u", u=1),
                                                          xn[i][:, 0:T].rearrange("p (c w) -> p c w", w=128)[:, :, 0:1]),
                   waits=(evg,), inc=True)
        g2b = P.op("tensor", lambda e: e.transpose(ps_trv.t[0:16, 0:128], ecs, ident[:, :]),
                   waits=(g1b,) + ((ps_trv.gate,) if ps_trv.gate else ()), inc=True)
        evg = P.op("scalar", lambda e, i=i: e.activation(est[32:48, i * 128:(i + 1) * 128], ps_trv.t[0:16, 0:128], AF.Copy),
                   waits=(g2b,), inc=True)
        ps_trv.gate = evg
    d1a = P.dma("gpsimd", dr["ec"][0:16, :], est[0:16, :], waits=(evg,))
    d1 = P.dma("gpsimd", dr["ec"][16:32, :], est[32:48, :], waits=(evg, d1a))
    cce = P.collective("AllGather", ALU.bypass, [dr["ec"]], [dr["eg"]], waits=(d1,))
    egt = small[1][0:128, 0:1024]
    d2 = P.dma("gpsimd", egt, dr["eg"], waits=(cce,))
    halos = {}
    for nm, selk in (("prev", "a1_selp"), ("next", "a1_seln")):
        htok = small[2][0:16, 0:1024]
        he1 = None
        for half in range(2):
            m1 = P.op("tensor", lambda e, half=half, selk=selk: e.matmul(
                ps_small.t[0:16, 0:TC], env["a1_sels"][selk][:, :], egt[:, half * TC:(half + 1) * TC],
                start=True, stop=True), waits=(d2,) + env["a1_selg"] + ((ps_small.gate,) if ps_small.gate else ()),
                inc=True)
            he1 = P.op("scalar", lambda e, half=half: e.activation(htok[:, half * TC:(half + 1) * TC],
                                                                   ps_small.t[0:16, 0:TC], AF.Copy), waits=(m1,), inc=True)
            ps_small.gate = he1
        hfm = al["ysc"][:, 0:128] if nm == "prev" else al["ysc"][:, 128:256]
        hg = None
        for i in range(ND):
            g2 = P.op("tensor", lambda e, i=i: e.transpose(ps_trv.t[0:128, 0:16], htok[:, i * 128:(i + 1) * 128], ident[0:16, 0:16]),
                      waits=(he1,) + ((ps_trv.gate,) if ps_trv.gate else ()), inc=True)
            hg = P.op("scalar", lambda e, i=i, hfm=hfm: e.activation(hfm[:, i * 16:(i + 1) * 16], ps_trv.t[0:128, 0:16], AF.Copy),
                      waits=(g2,), inc=True)
            ps_trv.gate = hg
        halos[nm] = (hfm, hg)

    dbh1 = P.dma("gpsimd", dr["dbg"][:, 2048:2560], egt.bitcast(F32)[:, 0:512], waits=(halos["prev"][1], halos["next"][1]))
    dbh2 = P.dma("gpsimd", dr["dbg"][:, 2560:2624], halos["prev"][0].bitcast(F32)[:, 0:64], waits=(dbh1,))
    dbh3 = P.dma("gpsimd", dr["dbg"][:, 2624:2688], halos["next"][0].bitcast(F32)[:, 0:64], waits=(dbh2,))
    dxg = []
    for i in range(ND):
        hp, hpg = halos["prev"]
        hn_, hng_ = halos["next"]
        g1 = P.op("vector", lambda e, i=i: e.tensor_copy(
            BB[i][:, 0:T].rearrange("p (c w) -> p c w", w=128)[:, :, 1:128],
            xn[i][:, 0:T].rearrange("p (c w) -> p c w", w=128)[:, :, 0:127]),
                  waits=(lng, hng_), inc=True)
        g2 = P.op("vector", lambda e, i=i, hp=hp: e.tensor_copy(
            BB[i][:, 0:T].rearrange("p (c w) -> p c w", w=128)[:, :, 0:1],
            hp[:, i * 16:(i + 1) * 16].rearrange("p (c u) -> p c u", u=1)), waits=(g1,), inc=True)
        g3 = P.op("vector", lambda e, i=i: e.tensor_add(
            BB[i][:, 0:T].rearrange("p (c w) -> p c w", w=128)[:, :, 0:127],
            BB[i][:, 0:T].rearrange("p (c w) -> p c w", w=128)[:, :, 0:127],
            xn[i][:, 0:T].rearrange("p (c w) -> p c w", w=128)[:, :, 1:128]),
                  waits=(g2,), inc=True)
        g4 = P.op("vector", lambda e, i=i, hn_=hn_: e.tensor_add(
            BB[i][:, 0:T].rearrange("p (c w) -> p c w", w=128)[:, :, 127:128],
            BB[i][:, 0:T].rearrange("p (c w) -> p c w", w=128)[:, :, 127:128],
            hn_[:, i * 16:(i + 1) * 16].rearrange("p (c u) -> p c u", u=1)), waits=(g3,), inc=True)
        g5 = P.op("vector", lambda e, i=i: e.scalar_tensor_tensor(BB[i][:, 0:T], BB[i][:, 0:T], 0.5, xn[i][:, 0:T],
                                                                 ALU.mult, ALU.subtract), waits=(g4,), inc=True)
        dxg.append(g5)
    dx_done = dxg[-1]

    dbg7 = P.dma("gpsimd", dr["dbg"][:, 1464:1976], BB[0].bitcast(F32)[:, 0:512], waits=(dx_done,))
    con_g_dbg = dbg7
    xxx = [SFB[i // 2][:, (i % 2) * T:(i % 2) * T + T] for i in range(8)]
    xxg = []
    for i in range(ND):
        xxg.append(P.op("vector", lambda e, i=i: e.scalar_tensor_tensor(
            xxx[i][:, 0:T], BB[i][:, 0:T], al["tmaa"][:, i:i + 1], xn[i][:, 0:T], ALU.mult, ALU.add),
            waits=(dx_done, colg[-1]), inc=True))
    xx_done = xxg[-1]
    mixA = [small[0][:, 0:1024], small[1][:, 0:1024]]
    mixB = [small[2][0:64, 0:1024], small[3][0:64, 0:1024]]
    wt1 = ws.load(W["time_maa_w1"], D, 128)
    wt1b = ws.load(W["time_maa_w1"], D, 32, col0=128)
    mx_done = None
    for t0 in range(0, T, TC):
        pp = ps[(t0 // TC) % 2]
        wl = [wt1.ready, wt1b.ready, xx_done]
        if pp.gate is not None:
            wl.append(pp.gate)
        def spec2(e, t0=t0, pp=pp, wt1=wt1, wt1b=wt1b):
            for ic in range(8):
                e.matmul(pp.t[0:128, 0:TC], wt1.lhsT(ic, 0, 128), xxx[ic][:, t0:t0 + TC],
                         start=(ic == 0), stop=(ic == 7))
            m = None
            for ic in range(8):
                m = e.matmul(ps_small.t[0:32, 0:TC], wt1b.lhsT(ic, 0, 32), xxx[ic][:, t0:t0 + TC],
                             start=(ic == 0), stop=(ic == 7))
            return m
        wl2 = list(wl)
        if ps_small.gate is not None:
            wl2.append(ps_small.gate)
        mg = P.op("tensor", spec2, waits=tuple(wl2), inc=True)
        th = t0 // 1024
        tc2 = (t0 % 1024)
        e1 = P.op("scalar", lambda e, th=th, tc2=tc2, pp=pp: e.activation(mixA[th][0:32, tc2:tc2 + TC], pp.t[0:32, 0:TC], AF.Tanh), waits=(mg,), inc=True)
        e2 = P.op("scalar", lambda e, th=th, tc2=tc2, pp=pp: e.activation(mixA[th][32:64, tc2:tc2 + TC], pp.t[32:64, 0:TC], AF.Tanh), waits=(e1,), inc=True)
        e3 = P.op("scalar", lambda e, th=th, tc2=tc2, pp=pp: e.activation(mixA[th][64:96, tc2:tc2 + TC], pp.t[64:96, 0:TC], AF.Tanh), waits=(e2,), inc=True)
        e4 = P.op("scalar", lambda e, th=th, tc2=tc2, pp=pp: e.activation(mixB[th][0:32, tc2:tc2 + TC], pp.t[96:128, 0:TC], AF.Tanh), waits=(e3,), inc=True)
        e5 = P.op("scalar", lambda e, th=th, tc2=tc2: e.activation(mixB[th][32:64, tc2:tc2 + TC], ps_small.t[0:32, 0:TC], AF.Tanh), waits=(e4,), inc=True)
        pp.gate = e4
        ps_small.gate = e5
        mx_done = e5
    wt1.release(P.TE.now)
    wt1b.release(P.TE.now)

    def mix_ap(f, th):
        if f < 3:
            return mixA[th][f * 32:(f + 1) * 32, 0:1024]
        return mixB[th][(f - 3) * 32:(f - 2) * 32, 0:1024]
    MIX_BASE = [0, 32, 64, 0, 32]
    xt = xxx
    scrb = al["ysc"]
    tgt_W = {"w": None, "k": "W_k", "v": "W_v", "r": "W_r", "g": "W_g"}
    tgt_dst = {"g": dr["g"]}
    last_xt = [mx_done]
    con_g = []
    prev_scr_dma = [None]
    prev_tok_dma = [None, None]
    for f, tgt in enumerate(("w", "k", "v", "r", "g")):
        wt2f = ws.load_rows(W["time_maa_w2"][f], 0, 32, 1024, base=MIX_BASE[f])
        xtg = []
        for oc in range(8):
            for t0 in range(0, T, TC):
                th, tc2 = t0 // 1024, t0 % 1024
                pp = ps[(oc + t0 // TC) % 2]
                wl = [wt2f.ready, mx_done]
                if pp.gate is not None:
                    wl.append(pp.gate)
                mg = P.op("tensor", lambda e, f=f, oc=oc, th=th, tc2=tc2, pp=pp, wt2f=wt2f, b=MIX_BASE[f]: e.matmul(
                    pp.t[:, 0:TC], wt2f.buf[b:b + 32, oc * 128:(oc + 1) * 128], mix_ap(f, th)[:, tc2:tc2 + TC],
                    start=True, stop=True), waits=tuple(w for w in wl if w), inc=True)
                v1 = P.op("vector", lambda e, f=f, oc=oc, t0=t0, pp=pp: e.scalar_tensor_tensor(
                    scrb[:, 0:TC], pp.t[:, 0:TC], al["tmaa"][:, (f + 1) * 8 + oc:(f + 1) * 8 + oc + 1],
                    BB[oc][:, t0:t0 + TC], ALU.add, ALU.mult), waits=(mg, xtg[-1] if xtg else None), inc=True)
                v2 = P.op("vector", lambda e, f=f, oc=oc, t0=t0: e.tensor_add(
                    xt[oc][:, t0:t0 + TC], scrb[:, 0:TC],
                    xn[oc][:, t0:t0 + TC]), waits=(v1,), inc=True)
                pp.gate = v2
                xtg.append(v2)
        wt2f.release(P.TE.now)
        xt_done = xtg[-1]
        if tgt == "w":
            w1bf = small[5][:, 0:512]
            w1f = small[5].bitcast(F32)[:, 256:512]
            wsrc = W["time_decay_w1"].rearrange("(ic p) o -> p ic o", p=128)
            wg_ = None
            w1g = None
            for rnd in range(2):
                dmaw = P.dma("sync", w1f.rearrange("p (ic o) -> p ic o", o=64), wsrc[:, rnd * 4:(rnd + 1) * 4, :],
                             waits=(w1g,))
                w1g = P.op("vector", lambda e, rnd=rnd: e.tensor_copy(w1bf[:, rnd * 256:(rnd + 1) * 256],
                                                                      w1f[:, 0:256]), waits=(dmaw,), inc=True)
            td_flat = W["time_decay"].rearrange("a b d -> (a b d)")
            wm_done = None
            wm_dma = [None, None]
            for t0 in range(0, T, TC):
                pp = ps[(t0 // TC) % 2]
                wl = [w1g, xt_done]
                if pp.gate is not None:
                    wl.append(pp.gate)
                mg = P.op("tensor", lambda e, t0=t0, pp=pp: [
                    e.matmul(pp.t[0:64, 0:TC], w1bf[:, ic * 64:(ic + 1) * 64], xt[ic][:, t0:t0 + TC],
                             start=(ic == 0), stop=(ic == 7)) for ic in range(8)][-1],
                    waits=tuple(wl), inc=True)
                th_g = P.op("scalar", lambda e, pp=pp: e.activation(scrb[0:64, 512:1024], pp.t[0:64, 0:TC], AF.Tanh),
                            waits=(mg, wm_done), inc=True)
                pp.gate = th_g
                sum_g = None
                for oc in range(8):
                    wtd2 = ws.load_rows(W["time_decay_w2"], 0, 64, 128, col0=oc * 128, bias_dram=td_flat)
                    pp2 = ps[(oc + 1) % 2]
                    wl2 = [wtd2.ready, th_g]
                    if pp2.gate is not None:
                        wl2.append(pp2.gate)
                    mg2 = P.op("tensor", lambda e, oc=oc, pp2=pp2, wtd2=wtd2: (
                        e.matmul(pp2.t[:, 0:TC], wtd2.buf[0:64, 0:128], scrb[0:64, 512:1024], start=True, stop=False),
                        e.matmul(pp2.t[:, 0:TC], ws.brow(wtd2, 0, 128), C.ones_row_bf[0:1, 0:TC], start=False, stop=True))[-1],
                        waits=tuple(wl2), inc=True)
                    wtd2.release(P.TE.now)
                    ex_g = P.op("scalar", lambda e, pp2=pp2: e.activation(scrb[:, 0:TC], pp2.t[:, 0:TC], AF.Exp),
                                waits=(mg2, sum_g), inc=True)
                    pp2.gate = ex_g
                    wl3 = [ex_g]
                    if oc == 0:
                        wl3 += list(env["a1_selg"])
                        if ps_small.gate is not None:
                            wl3.append(ps_small.gate)
                    sum_g = P.op("tensor", lambda e, oc=oc: e.matmul(ps_small.t[0:16, 0:TC], env["a1_sels"]["a1_bo16"][:, oc * 16:(oc + 1) * 16],
                                                                    scrb[:, 0:TC], start=(oc == 0), stop=(oc == 7)),
                                 waits=tuple(wl3), inc=True)
                half = (t0 // TC) % 2
                g1 = P.op("vector", lambda e, half=half: e.tensor_scalar(
                    al["wmroll"][0:16, half * TC:half * TC + TC], ps_small.t[0:16, 0:TC],
                    al["hdb"][0:16, 2:3], None, ALU.mult),
                          waits=(sum_g, hg4, wm_dma[half]), inc=True)
                ps_small.gate = g1
                wm_dma[half] = P.dma("gpsimd", dr["wmc"][0:16, t0:t0 + TC],
                                     al["wmroll"][0:16, half * TC:half * TC + TC], waits=(g1,))
                wm_done = wm_dma[half]
            con_g.append(wm_done)
            wm_all = tuple(w for w in wm_dma if w)
        else:
            wkey = tgt_W[tgt]
            if tgt in ("k", "v", "r"):
                for t0 in range(0, T, TC):
                    tev = None
                    for oc in range(8):
                        wt = ws.load(W[wkey], D, 128, col0=oc * 128)
                        pp = ps[(oc + t0 // TC) % 2]
                        wl = [wt.ready, xt_done]
                        if pp.gate is not None:
                            wl.append(pp.gate)
                        mg = P.op("tensor", lambda e, t0=t0, pp=pp, wt=wt: [
                            e.matmul(pp.t[:, 0:TC], wt.lhsT(ic, 0, 128), xt[ic][:, t0:t0 + TC],
                                     start=(ic == 0), stop=(ic == 7)) for ic in range(8)][-1],
                            waits=tuple(wl), inc=True)
                        sc_ = 0.125 if tgt == "r" else 1.0
                        ev = P.op("scalar", lambda e, pp=pp, sc_=sc_: e.activation(scrb[:, 0:TC], pp.t[:, 0:TC], AF.Copy,
                                                                                  scale=sc_),
                                  waits=(mg, prev_scr_dma[0]), inc=True)
                        pp.gate = ev
                        prev_scr_dma[0] = None
                        trg = None
                        for q in range(4):
                            trg = P.op("tensor", lambda e, q=q: e.transpose(ps_trv.t[:, q * 128:(q + 1) * 128],
                                                                           scrb[:, q * 128:(q + 1) * 128], ident[:, :]),
                                       waits=(ev,) + ((ps_trv.gate,) if q == 0 and ps_trv.gate else ()), inc=True)
                        dstt = {"k": dr["ktc"], "v": dr["vtc"], "r": dr["r"]}[tgt]
                        for q in range(4):
                            sl = small[4][:, (q % 2) * 128:(q % 2) * 128 + 128]
                            tev = P.op("scalar", lambda e, q=q, sl=sl: e.activation(
                                sl, ps_trv.t[:, q * 128:(q + 1) * 128], AF.Copy),
                                waits=(trg, prev_tok_dma[q % 2]), inc=True)
                            dg2 = P.dma("gpsimd", dstt[t0 + q * 128:t0 + (q + 1) * 128, oc * 128:(oc + 1) * 128],
                                        sl, waits=(tev,))
                            prev_tok_dma[q % 2] = dg2
                            con_g.append(dg2)
                        ps_trv.gate = tev
                        wt.release(P.TE.now)
            else:
                for col0 in range(0, D, 128):
                    wt = ws.load(W[wkey], D, 128, col0=col0)
                    oc = col0 // 128
                    for t0 in range(0, T, TC):
                        pp = ps[(oc + t0 // TC) % 2]
                        wl = [wt.ready, xt_done]
                        if pp.gate is not None:
                            wl.append(pp.gate)
                        mg = P.op("tensor", lambda e, t0=t0, pp=pp, wt=wt: [
                            e.matmul(pp.t[:, 0:TC], wt.lhsT(ic, 0, 128), xt[ic][:, t0:t0 + TC],
                                     start=(ic == 0), stop=(ic == 7)) for ic in range(8)][-1],
                            waits=tuple(wl), inc=True)
                        fn = AF.Silu if tgt == "g" else AF.Copy
                        ev = P.op("scalar", lambda e, pp=pp, fn=fn: e.activation(scrb[:, 0:TC], pp.t[:, 0:TC], fn),
                                  waits=(mg, prev_scr_dma[0]), inc=True)
                        pp.gate = ev
                        dg = P.dma("gpsimd", tgt_dst[tgt][oc * 128:(oc + 1) * 128, t0:t0 + TC], scrb[:, 0:TC], waits=(ev,))
                        prev_scr_dma[0] = dg
                        con_g.append(dg)
                    wt.release(P.TE.now)
    proj_done = con_g[-1]

    front = tuple(P.dma_frontier("gpsimd")) + tuple(P.dma_frontier("sync"))
    cc1 = P.collective("AllGather", ALU.bypass, [dr["ktc"]], [dr["ktg"]], waits=front)
    cc2 = P.collective("AllGather", ALU.bypass, [dr["vtc"]], [dr["vtg"]], waits=(cc1,))
    cc4 = P.collective("AllGather", ALU.bypass, [dr["wmc"]], [dr["wmg"]], waits=(cc2,))

    def g_(pt):
        return (pt.gate,) if pt.gate is not None else ()

    SEL = env["a1_sels"]
    identf = env["identf"]
    selG = env["a1_selg"]
    if upto == "proj":
        return [proj_done, cc4]
    exg1 = P.dma("sync", SEL["a1_expbf"], env["a1_sel_dram"]["a1_expbf"], waits=(proj_done,))
    zg = P.op("vector", lambda e: e.memset(al["z16"][:, :], 0.0), waits=(proj_done,), inc=True)
    exg2 = P.dma("sync", SEL["a1_expb"], env["a1_sel_dram"]["a1_expb"], waits=(proj_done,))

    sgc = al["ecol"][:, 0:16]
    nsgc = al["ecol"][:, 16:32]
    lamF = al["ecol"][:, 32:48]
    lamS = al["ecol"][:, 48:64]
    srow_s = al["srow_s"]
    s0 = P.op("tensor", lambda e: e.transpose(ps_small.t[0:1, 0:16], al["hdb"][0:16, 1:2], identf[0:16, 0:16]),
              waits=(hg5,) + selG + g_(ps_small), inc=True)
    s1 = P.op("scalar", lambda e: e.activation(srow_s, ps_small.t[0:1, 0:16], AF.Copy), waits=(s0,), inc=True)
    s2 = P.op("tensor", lambda e: e.matmul(ps_small.t[:, 16:32], C.onescol(0), srow_s, start=True, stop=True),
              waits=(s1,), inc=True)
    s3 = P.op("vector", lambda e: e.tensor_copy(sgc, ps_small.t[:, 16:32]), waits=(s2,), inc=True)
    s4 = P.op("vector", lambda e: e.tensor_scalar(nsgc, sgc, -1.0, None, ALU.mult), waits=(s3,), inc=True)
    ps_small.gate = s3

    cgb = al["cg"].bitcast(BF16)
    wmgv = dr["wmg"].rearrange("(a h) t -> a h t", a=4)
    sum_done = zg
    for c in range(GCH):
        slab, c4 = c % 4, c // 4
        dgt = P.dma("sync", cgb[0:16, 0:128], wmgv[slab][:, c4 * 128:(c4 + 1) * 128], waits=(cc4, sum_done))
        sg = P.op("vector", lambda e: e.tensor_tensor_scan(al["cg"][0:16, 128:256], cgb[0:16, 0:128],
                                                           al["z16"][0:16, 0:128], 0.0, ALU.add, ALU.add),
                  waits=(dgt, zg), inc=True)
        sum_done = P.op("vector", lambda e, c=c: e.tensor_copy(al["gcs"][0:16, c:c + 1], al["cg"][0:16, 255:256]),
                        waits=(sg,), inc=True)
        if c == 0:
            sum_done = P.dma("gpsimd", dr["dbg"][0:16, 1208:1464], al["cg"][0:16, 0:256], waits=(sum_done,))
    pc_g = P.op("vector", lambda e: e.tensor_tensor_scan(al["pc"][0:16, 0:64], al["gcs"][0:16, 0:64],
                                                         al["z16"][0:16, 0:64], 0.0, ALU.add, ALU.add),
                waits=(sum_done,), inc=True)
    tr1 = P.op("tensor", lambda e: e.transpose(ps_small.t[0:64, 32:48], al["pc"][0:16, 0:64], identf[0:16, 0:16]),
               waits=(pc_g,) + g_(ps_small), inc=True)
    tr1e = P.op("scalar", lambda e: e.activation(al["pct"][0:64, 0:16], ps_small.t[0:64, 32:48], AF.Copy),
                waits=(tr1,), inc=True)
    pm1 = P.op("tensor", lambda e: e.matmul(ps_small.t[0:16, 0:16], al["pct"][0:64, 0:16], SEL["a1_selpre"][:, :],
                                            start=True, stop=True), waits=(tr1e,), inc=True)
    pm1e = P.op("vector", lambda e: e.tensor_copy(al["pfx"][0:16, 0:16], ps_small.t[0:16, 0:16]), waits=(pm1,), inc=True)
    pm2 = P.op("tensor", lambda e: e.matmul(ps_small.t[0:16, 16:32], al["pct"][0:64, 0:16], SEL["a1_selgrp"][:, :],
                                            start=True, stop=True), waits=(pm1e,), inc=True)
    pm2e = P.op("vector", lambda e: e.tensor_sub(al["pfx"][0:16, 0:16], al["pfx"][0:16, 0:16], ps_small.t[0:16, 16:32]),
                waits=(pm2,), inc=True)
    ps_small.gate = pm2e
    cl_done = pm2e
    for j in range(NCH):
        dwj = P.dma("sync", cgb[0:16, 0:128], dr["wmc"][0:16, j * 128:(j + 1) * 128],
                    waits=(cl_done,) + wm_all)
        cl_done = P.op("vector", lambda e, j=j: e.tensor_tensor_scan(
            al["coloc"][:, j * 128:(j + 1) * 128], cgb[0:16, 0:128], al["z16"][0:16, 0:128],
            al["pfx"][0:16, j:j + 1], ALU.add, ALU.add), waits=(dwj,), inc=True)
    act = SF[2][:, 0:1024]
    carry = al["gcs"][0:16, 64:65]
    act_done = None
    prev_cp = cl_done
    prev_tg = None
    for c in range(GCH):
        slab, c4 = c % 4, c // 4
        dgt = P.dma("sync", cgb[0:16, 0:128], wmgv[slab][:, c4 * 128:(c4 + 1) * 128], waits=(prev_cp,))
        ini = al["z16"][0:16, 0:1] if c % 4 == 0 else carry
        sg = P.op("vector", lambda e, ini=ini: e.tensor_tensor_scan(al["cg"][0:16, 128:256], cgb[0:16, 0:128],
                                                                    al["z16"][0:16, 0:128], ini, ALU.add, ALU.add),
                  waits=(dgt, prev_tg), inc=True)
        prev_cp = P.op("vector", lambda e: e.tensor_copy(carry, al["cg"][0:16, 255:256]), waits=(sg,), inc=True)
        prev_tg = P.op("tensor", lambda e: e.transpose(ps_small.t[:, 32:48], al["cg"][0:16, 128:256],
                                                       identf[0:16, 0:16]), waits=(sg,) + g_(ps_small), inc=True)
        act_done = P.op("scalar", lambda e, c=c: e.activation(act[:, c * 16:(c + 1) * 16], ps_small.t[:, 32:48],
                                                              AF.Copy), waits=(prev_tg,), inc=True)
        ps_small.gate = act_done

    dbg1 = P.dma("gpsimd", dr["dbg"][:, 0:1024], act, waits=(act_done,))
    dbg2 = P.dma("gpsimd", dr["dbg"][0:16, 1024:1096], al["gcs"][0:16, 0:72], waits=(act_done, prev_cp, dbg1))
    dbg3 = P.dma("gpsimd", dr["dbg"][0:16, 1096:1160], al["pc"][0:16, 0:64], waits=(pc_g, dbg2))
    dbg4 = P.dma("gpsimd", dr["dbg"][0:64, 1160:1176], al["pct"][0:64, 0:16], waits=(tr1e, dbg3))
    dbg5 = P.dma("gpsimd", dr["dbg"][0:16, 1176:1192], al["pfx"][0:16, 0:16], waits=(pm2e, dbg4))
    dbg6 = P.dma("gpsimd", dr["dbg"][0:16, 1192:1208], al["coloc"][0:16, 0:16] if False else al["coloc"][0:16, 0:16], waits=(cl_done, dbg5))

    if upto == "C":
        return [dbg6, con_g_dbg, prev_cp]
    ktch = [BB[0][:, 0:1024], BB[0][:, 1024:2048], BB[1][:, 0:1024], BB[1][:, 1024:2048]]
    vtch = [BB[2][:, 0:1024], BB[2][:, 1024:2048], BB[3][:, 0:1024], BB[3][:, 1024:2048]]
    ktf = [BB[4][:, 0:1024], BB[4][:, 1024:2048]]
    kfr = [BB[5][0:64, 0:T], BB[6][0:64, 0:T], BB[7][0:64, 0:T], BB[5][64:128, 0:T]]
    rt64 = BB[4]
    rtok = small[4]
    AbF = SFB[0][0:64, 0:1024]
    AbS = SFB[0][0:64, 1024:2048]
    BlF = al["BlF"]
    BlS = al["BlS"]
    Ast = [al["A"][0:64, 0:1024], al["As"][0:64, 0:1024]]
    ebscr = al["ebs"]
    ebt = al["eb"]
    lsc = al["lsc"]
    expb = SEL["a1_expb"]
    expbf = SEL["a1_expbf"]
    ysb = al["ysc"]
    ynb = al["ysc"]
    ktgv = dr["ktg"].rearrange("(a t) d -> a t d", a=4)
    vtgv = dr["vtg"].rearrange("(a t) d -> a t d", a=4)

    def dma_group(G, waits):
        gs = []
        for r in range(4):
            gs.append(P.dma("sync", ktch[r], ktgv[r][G * 128:(G + 1) * 128, :], waits=waits))
            gs.append(P.dma("sync", vtch[r], vtgv[r][G * 128:(G + 1) * 128, :], waits=waits))
        return gs

    dbc_state = {"gate": None, "users": [], "dsub": None}

    def emit_dbc(G):
        drow = rows.seg(16)
        drow2 = rows.seg(16)
        pv = (dbc_state["dsub"],) if dbc_state["dsub"] else ()
        if G > 0:
            rg1 = P.dma("gpsimd", drow, al["pct"][4 * G - 1:4 * G, 0:16], waits=(tr1e,) + pv)
        else:
            rg1 = P.op("vector", lambda e, drow=drow: e.memset(drow, 0.0), waits=pv, inc=True)
        rg2 = P.dma("gpsimd", drow2, al["pct"][4 * G + 3:4 * G + 4, 0:16], waits=(tr1e,) + pv)
        dsub = P.op("vector", lambda e, drow=drow, drow2=drow2: e.tensor_sub(drow, drow2, drow),
                    waits=(rg1, rg2), inc=True)
        dbc_state["dsub"] = dsub
        mmd = P.op("tensor", lambda e, drow=drow: e.matmul(ps_small.t[:, 16:32], C.onescol(0), drow,
                                                           start=True, stop=True),
                   waits=(dsub,) + g_(ps_small), inc=True)
        w = tuple(dbc_state["users"]) + ((dbc_state["gate"],) if dbc_state["gate"] else ())
        dbc = P.op("vector", lambda e: e.tensor_copy(al["dbcs"], ps_small.t[:, 16:32]), waits=(mmd,) + w, inc=True)
        ps_small.gate = dbc
        dbc_state["gate"] = dbc
        dbc_state["users"] = []
        return dbc

    def emit_lam(dbc, lam_users):
        l1 = P.op("scalar", lambda e: e.activation(lamF, al["dbcs"], AF.Exp),
                  waits=(dbc,) + tuple(lam_users), inc=True)
        l2 = P.op("vector", lambda e: e.tensor_mul(al["gst"][:, 16:32], al["dbcs"], sgc),
                  waits=(dbc, s3), inc=True)
        l3 = P.op("scalar", lambda e: e.activation(lamS, al["gst"][:, 16:32], AF.Exp),
                  waits=(l2,) + tuple(lam_users), inc=True)
        dbc_state["users"] += [l1, l2]
        return (l1, l3)

    st = {"last_t3": None, "ktf_user": None}

    def state_pass(G, direction, dg, dbc, lamg, upd_extra=(), first_waits=()):
        upd = []
        for s in range(2):
            mlast = None
            for r in range(4):
                c = 4 * G + r
                ew = (act_done,) + (first_waits if (s == 0 and r == 0) else ())
                lt3 = (st["last_t3"],) if st["last_t3"] else ()
                if direction == "bwd":
                    if s == 0:
                        t2 = P.op("scalar", lambda e, c=c: e.activation(ebscr, act[:, c * 16:(c + 1) * 16], AF.Exp),
                                  waits=ew + lt3, inc=True)
                    else:
                        t1 = P.op("vector", lambda e, c=c: e.tensor_mul(al["gst"][:, 0:16],
                                                                        act[:, c * 16:(c + 1) * 16], sgc),
                                  waits=ew + (s3,), inc=True)
                        t2 = P.op("scalar", lambda e: e.activation(ebscr, al["gst"][:, 0:16], AF.Exp),
                                  waits=(t1,) + lt3, inc=True)
                else:
                    t1 = P.op("vector", lambda e, c=c: e.tensor_sub(al["gst"][:, 0:16], al["dbcs"],
                                                                    act[:, c * 16:(c + 1) * 16]),
                              waits=ew + (dbc,), inc=True)
                    dbc_state["users"].append(t1)
                    if s == 1:
                        t1 = P.op("vector", lambda e: e.tensor_mul(al["gst"][:, 0:16], al["gst"][:, 0:16], sgc),
                                  waits=(t1, s3), inc=True)
                    t2 = P.op("scalar", lambda e: e.activation(ebscr, al["gst"][:, 0:16], AF.Exp),
                              waits=(t1,) + lt3, inc=True)
                t3 = P.op("tensor", lambda e: e.transpose(ps_trv.t[0:16, 0:128], ebscr, ident[:, :]),
                          waits=(t2,) + g_(ps_trv), inc=True)
                st["last_t3"] = t3
                t4 = P.op("scalar", lambda e: e.activation(ebt[:, 0:128], ps_trv.t[0:16, 0:128], AF.Copy),
                          waits=(t3,), inc=True)
                ps_trv.gate = t4
                x1 = P.op("tensor", lambda e: e.matmul(ps[0].t[:, 0:TC], ebt[:, 0:128], expb[:, 0:TC],
                                                       start=True, stop=True), waits=(t4, exg2) + g_(ps[0]), inc=True)
                x2 = P.op("tensor", lambda e: e.matmul(ps[1].t[:, 0:TC], ebt[:, 0:128], expb[:, TC:1024],
                                                       start=True, stop=True), waits=(t4, exg2) + g_(ps[1]), inc=True)
                kw = (dg[2 * r],) + ((st["ktf_user"],) if st["ktf_user"] else ())
                k1 = P.op("vector", lambda e, r=r, s=s: e.tensor_mul(ktf[s][:, 0:TC], ktch[r][:, 0:TC],
                                                                     ps[0].t[:, 0:TC]), waits=(x1,) + kw, inc=True)
                k2 = P.op("vector", lambda e, r=r, s=s: e.tensor_mul(ktf[s][:, TC:1024], ktch[r][:, TC:1024],
                                                                     ps[1].t[:, 0:TC]), waits=(x2, k1), inc=True)
                ps[0].gate = k1
                ps[1].gate = k2
                def mspec(e, r=r, s=s):
                    m = None
                    for h in range(16):
                        m = e.matmul(ps_misc.t[0:64, h * 64:(h + 1) * 64], ktf[s][:, h * 64:(h + 1) * 64],
                                     vtch[r][:, h * 64:(h + 1) * 64],
                                     start=(r == 0 and h % 8 == 0), stop=(r == 3 and h % 8 == 7))
                    return m
                wl = [k1, k2, dg[2 * r + 1]]
                if s == 0 and r == 0:
                    wl += list(g_(ps_misc))
                mlast = P.op("tensor", mspec, waits=tuple(wl), inc=True)
                st["ktf_user"] = mlast
            lam_src = lamF if s == 0 else lamS
            lb = P.op("vector", lambda e, lam_src=lam_src: e.tensor_copy(ebscr, lam_src),
                      waits=(lamg[s], st["last_t3"]), inc=True)
            l4 = P.op("tensor", lambda e: e.transpose(ps_trv.t[0:16, 0:128], ebscr, ident[:, :]),
                      waits=(lb,) + g_(ps_trv), inc=True)
            st["last_t3"] = l4
            l5 = P.op("scalar", lambda e: e.activation(ebt[:, 128:256], ps_trv.t[0:16, 0:128], AF.Copy),
                      waits=(l4,), inc=True)
            ps_trv.gate = l5
            y1 = P.op("tensor", lambda e: e.matmul(ps[0].t[:, 0:TC], ebt[:, 128:256], expb[:, 0:TC],
                                                   start=True, stop=True), waits=(l5, exg2) + g_(ps[0]), inc=True)
            y2 = P.op("tensor", lambda e: e.matmul(ps[1].t[:, 0:TC], ebt[:, 128:256], expb[:, TC:1024],
                                                   start=True, stop=True), waits=(l5, exg2) + g_(ps[1]), inc=True)
            v1 = P.op("vector", lambda e, s=s: e.tensor_mul(lsc[:, 0:TC], Ast[s][:, 0:TC], ps[0].t[0:64, 0:TC]),
                      waits=(y1,) + upd_extra, inc=True)
            v2 = P.op("vector", lambda e, s=s: e.tensor_mul(lsc[:, TC:1024], Ast[s][:, TC:1024], ps[1].t[0:64, 0:TC]),
                      waits=(y2, v1), inc=True)
            ps[0].gate = v1
            ps[1].gate = v2
            v3 = P.op("vector", lambda e, s=s: e.tensor_add(Ast[s][:, 0:1024], lsc[:, 0:1024],
                                                            ps_misc.t[0:64, 0:1024]), waits=(v2, mlast), inc=True)
            ps_misc.gate = v3
            upd.append(v3)
        return upd

    z1 = P.op("vector", lambda e: e.memset(al["A"][0:64, 0:1024], 0.0),
              waits=(proj_done, act_done, prev_cp, cl_done, exg1, exg2, dbg6, con_g_dbg), inc=True)
    z2 = P.op("vector", lambda e: e.memset(al["As"][0:64, 0:1024], 0.0), waits=(z1,), inc=True)
    prev_step = (z2,)
    prev_d = (z2,)
    bwd_d = (cc2,)
    for G in range(NCH - 1, -1, -1):
        dg = dma_group(G, prev_step + (cc2,))
        sb1 = P.op("vector", lambda e: e.tensor_copy(BlF, Ast[0][:, :]), waits=prev_step + prev_d, inc=True)
        sb2 = P.op("vector", lambda e: e.tensor_copy(BlS, Ast[1][:, :]), waits=(sb1,), inc=True)
        d1 = P.dma("gpsimd", dr["bf"][G * 64:(G + 1) * 64, :], BlF, waits=(sb1,))
        d2 = P.dma("gpsimd", dr["bs"][G * 64:(G + 1) * 64, :], BlS, waits=(sb2,))
        dbc = emit_dbc(G)
        lamg = emit_lam(dbc, prev_step)
        upd = state_pass(G, "bwd", dg, dbc, lamg, upd_extra=(sb2,))
        prev_step = tuple(upd)
        prev_d = (d1, d2)
        bwd_d = (d1, d2)
    if upto == "bwd":
        return list(prev_step) + [dbg6, con_g_dbg]
    z3 = P.op("vector", lambda e: e.memset(al["A"][0:64, 0:1024], 0.0), waits=prev_step, inc=True)
    z4 = P.op("vector", lambda e: e.memset(al["As"][0:64, 0:1024], 0.0), waits=(z3,), inc=True)

    ynb = al["y"].bitcast(BF16)[:, 0:1024]
    SB1 = SF[1].bitcast(BF16)
    eFN = SB1[:, 0:128]
    eFNs = SB1[:, 128:256]
    stF = SB1[:, 256:384]
    stS = SB1[:, 384:512]
    Dt = SB1[:, 512:640]
    Wt = SB1[:, 640:768]
    ddf = SF[1][:, 384:512]
    kfr3l = al["ysc"]
    kfr3h = SF[1][:, 512:1024].bitcast(BF16)
    BlF2 = SFB[0][64:128, 0:1024]
    BlS2 = SFB[0][64:128, 1024:2048]
    gnscr = SF[1][:, 0:512]

    def kfr_ap(r2, h):
        if r2 < 3:
            return BB[5 + r2][:, h * 128:(h + 1) * 128]
        if h < 8:
            return kfr3l[:, h * 128:(h + 1) * 128]
        return kfr3h[:, (h - 8) * 128:(h - 7) * 128]

    def _kfr_lo(r2, h):
        return kfr_ap(r2, h)[0:64, :]

    zz = []
    for tens, sl in ((BB[5], None), (BB[6], None), (BB[7], None)):
        zz.append(P.op("vector", lambda e, tens=tens: e.memset(tens[64:128, 0:T], 0.0),
                       waits=(z4,) + ((zz[-1],) if zz else ()), inc=True))
    zz.append(P.op("vector", lambda e: e.memset(al["ysc"][64:128, 0:1024], 0.0), waits=(z4, zz[-1]), inc=True))
    zz.append(P.op("vector", lambda e: e.memset(SF[1][64:128, 512:1024], 0.0), waits=(z4, zz[-1]), inc=True))
    z5 = zz[-1]

    fwd_prev = (z5,)
    prev_bl = bwd_d
    psc_state = {"users": [], "gate": None}
    prev_q = ()
    prev_gn = [None]
    ba_gates = []
    NG_FWD = NCH if not (isinstance(upto, tuple) and upto[0] == "fwd") else upto[1]
    for G in range(NG_FWD):
        dg = dma_group(G, fwd_prev + (cc2,))
        rt_d = P.dma("sync", rtok[:, 0:1024], dr["r"][G * 128:(G + 1) * 128, :], waits=fwd_prev)
        bl1 = P.dma("sync", BlF2, dr["bf"][G * 64:(G + 1) * 64, :], waits=fwd_prev + prev_bl)
        bl2 = P.dma("sync", BlS2, dr["bs"][G * 64:(G + 1) * 64, :], waits=fwd_prev + prev_bl)
        prev_bl = ()
        ab1 = P.op("vector", lambda e: e.tensor_copy(AbF, Ast[0][:, :]), waits=fwd_prev, inc=True)
        ab2 = P.op("vector", lambda e: e.tensor_copy(AbS, Ast[1][:, :]), waits=(ab1,), inc=True)
        dbc = emit_dbc(G)
        lamg = emit_lam(dbc, fwd_prev)
        if upto == "fwdq0":
            return [lamg[1], bl2, rt_d, ab2, dbg6, con_g_dbg] + list(dg)
        trS = None
        trV = None
        for h in range(16):
            ta = P.op("tensor", lambda e, h=h: e.transpose(ps_trv.t[0:64, 0:128], rtok[:, h * 64:(h + 1) * 64],
                                                           ident[:, :]), waits=(rt_d,) + g_(ps_trv), inc=True)
            tb = P.op("tensor", lambda e, h=h: e.transpose(ps_trv.t[64:128, 0:128], rtok[:, h * 64:(h + 1) * 64],
                                                           ident[:, :]), waits=(ta,), inc=True)
            trS = P.op("scalar", lambda e, h=h: e.activation(rt64[0:64, h * 128:(h + 1) * 128],
                                                             ps_trv.t[0:64, 0:128], AF.Copy), waits=(ta,), inc=True)
            trV = P.op("vector", lambda e, h=h: e.tensor_copy(rt64[64:128, h * 128:(h + 1) * 128],
                                                              ps_trv.t[64:128, 0:128]), waits=(tb, trS), inc=True)
            ps_trv.gate = trV
        for r in range(4):
            for h in range(16):
                ta = P.op("tensor", lambda e, r=r, h=h: e.transpose(ps_trv.t[0:64, 0:128],
                                                                    ktch[r][:, h * 64:(h + 1) * 64], ident[:, :]),
                          waits=(dg[2 * r],) + g_(ps_trv), inc=True)
                if h % 2 == 0:
                    trS = P.op("scalar", lambda e, r=r, h=h: e.activation(_kfr_lo(r, h),
                                                                          ps_trv.t[0:64, 0:128], AF.Copy),
                               waits=(ta,), inc=True)
                    ps_trv.gate = trS
                else:
                    trV = P.op("vector", lambda e, r=r, h=h: e.tensor_copy(_kfr_lo(r, h), ps_trv.t[0:64, 0:128]),
                               waits=(ta,), inc=True)
                    ps_trv.gate = trV
        tr_done = (trS, trV)
        if upto == "fwdq1":
            return [trS, trV, lamg[1], bl2, ab2, dbg6, con_g_dbg]
        last_yf = None
        last_ys = None
        for h in range(16):
            pw = tuple(psc_state["users"]) + ((psc_state["gate"],) if psc_state["gate"] else ())
            psc = P.op("tensor", lambda e, G=G, h=h: (
                e.matmul(ps_acc.t[0:64, 0:128], SEL["a1_expbf"][:, h * 64:(h + 1) * 64],
                         al["coloc"][:, G * 128:(G + 1) * 128], start=True, stop=True),
                e.matmul(ps_acc.t[64:128, 0:128], SEL["a1_expbf"][:, h * 64:(h + 1) * 64],
                         al["coloc"][:, G * 128:(G + 1) * 128], start=True, stop=True))[-1],
                waits=(cl_done, exg1) + pw, inc=True)
            psc_state["gate"] = psc
            psc_state["users"] = []
            pq = prev_q[0] if prev_q else None
            e1 = P.op("scalar", lambda e: e.activation(eFN[0:64, :], ps_acc.t[0:64, 0:128], AF.Exp),
                      waits=(psc, pq), inc=True)
            scr_e = SF[1][64:128, 1536:1664]
            e2a = P.op("vector", lambda e, h=h: e.tensor_scalar(scr_e, ps_acc.t[64:128, 0:128],
                                                                al["dbcs"][64:128, h:h + 1], None, ALU.subtract),
                       waits=(psc, dbc, pq), inc=True)
            e2 = P.op("scalar", lambda e: e.activation(eFN[64:128, :], scr_e, AF.Exp, scale=-1.0),
                      waits=(e2a, e1), inc=True)
            e3 = P.op("scalar", lambda e, h=h: e.activation(eFNs[0:64, :], ps_acc.t[0:64, 0:128], AF.Exp,
                                                            scale=sgc[0:64, h:h + 1]), waits=(psc, e2, s3), inc=True)
            e4 = P.op("scalar", lambda e, h=h: e.activation(eFNs[64:128, :], scr_e, AF.Exp,
                                                            scale=nsgc[64:128, h:h + 1]),
                      waits=(e2a, e3, s4), inc=True)
            psc_state["users"] += [e1, e2a, e2, e3, e4]
            dbc_state["users"].append(e2a)
            if upto == "h_e" and h == 0:
                return [e2, e4, dbg6, con_g_dbg]
            sw = tr_done + (e4,)
            sp1 = P.op("vector", lambda e, h=h: e.tensor_mul(stF[0:64, :], rt64[0:64, h * 128:(h + 1) * 128],
                                                             eFN[0:64, :]), waits=sw + (pq,), inc=True)
            sp2 = P.op("vector", lambda e, h=h: e.tensor_mul(stF[64:128, :], rt64[64:128, h * 128:(h + 1) * 128],
                                                             eFN[64:128, :]), waits=(sp1, e2), inc=True)
            sp3 = P.op("vector", lambda e, h=h: e.tensor_mul(stS[0:64, :], rt64[0:64, h * 128:(h + 1) * 128],
                                                             eFNs[0:64, :]), waits=(sp2,), inc=True)
            sp4 = P.op("vector", lambda e, h=h: e.tensor_mul(stS[64:128, :], rt64[64:128, h * 128:(h + 1) * 128],
                                                             eFNs[64:128, :]), waits=(sp3, e4), inc=True)
            if upto == "h_sp" and h == 0:
                return [sp4, dbg6, con_g_dbg]
            def smm_spec(e, h=h):
                m = None
                for r2 in range(4):
                    m = e.matmul(ps_small.t[:, r2 * 128:(r2 + 1) * 128], kfr_ap(r2, h),
                                 rt64[:, h * 128:(h + 1) * 128], start=(r2 == 0), stop=(r2 == 3))
                return m
            smm = P.op("tensor", smm_spec, waits=sw + (z5,) + g_(ps_small), inc=True)
            if upto == "h_smm" and h == 0:
                return [smm, dbg6, con_g_dbg]
            yh_f = ps_misc.t[:, h * 64:(h + 1) * 64]
            yh_s = ps[h // 8].t[:, (h % 8) * 64:(h % 8 + 1) * 64]
            mw = [sp2, ab2, bl1]
            if h == 0:
                mw += list(g_(ps_misc))
            ypf = P.op("tensor", lambda e, h=h, yh_f=yh_f: e.matmul(yh_f, stF, SFB[0][:, h * 64:(h + 1) * 64],
                                                                    start=(h % 8 == 0), stop=False),
                       waits=tuple(mw), inc=True)
            mw2 = [sp4, ab2, bl2]
            if h == 0:
                mw2 += list(g_(ps[0])) + list(g_(ps[1]))
            yps = P.op("tensor", lambda e, h=h, yh_s=yh_s: e.matmul(yh_s, stS,
                                                                    SFB[0][:, 1024 + h * 64:1024 + (h + 1) * 64],
                                                                    start=(h % 8 == 0), stop=False),
                       waits=tuple(mw2), inc=True)
            if upto == "h_ypf" and h == 0:
                return [ypf, yps, dbg6, con_g_dbg]
            prev_dd = None
            prev_mlf = None
            prev_mls = None
            for r in range(4):
                c = 4 * G + r
                dd = P.op("vector", lambda e, c=c, h=h: e.tensor_scalar(ddf, ps_acc.t[:, 0:128],
                                                                        act[:, c * 16 + h:c * 16 + h + 1],
                                                                        None, ALU.subtract),
                          waits=(psc, act_done, prev_dd,
                                 prev_gn[0] if (h == 0 and r == 0) else None), inc=True)
                dd = P.op("scalar", lambda e: e.activation(ddf, ddf, AF.Abs), waits=(dd,), inc=True)
                psc_state["users"].append(dd)
                df = P.op("scalar", lambda e: e.activation(Dt, ddf, AF.Exp, scale=-1.0), waits=(dd,), inc=True)
                wf = P.op("vector", lambda e, r=r: e.tensor_mul(Wt, ps_small.t[:, r * 128:(r + 1) * 128], Dt),
                          waits=(df, smm, prev_mlf), inc=True)
                mlf = P.op("tensor", lambda e, r=r, h=h, yh_f=yh_f: e.matmul(yh_f, Wt,
                                                                             vtch[r][:, h * 64:(h + 1) * 64],
                                                                             start=False,
                                                                             stop=(r == 3 and h % 8 == 7)),
                           waits=(wf, dg[2 * r + 1], ypf), inc=True)
                ds = P.op("scalar", lambda e, h=h: e.activation(Dt, ddf, AF.Exp, scale=nsgc[:, h:h + 1]),
                          waits=(wf, s4), inc=True)
                ws2 = P.op("vector", lambda e, r=r: e.tensor_mul(Wt, ps_small.t[:, r * 128:(r + 1) * 128], Dt),
                           waits=(ds, mlf, prev_mls), inc=True)
                mls = P.op("tensor", lambda e, r=r, h=h, yh_s=yh_s: e.matmul(yh_s, Wt,
                                                                             vtch[r][:, h * 64:(h + 1) * 64],
                                                                             start=False,
                                                                             stop=(r == 3 and h % 8 == 7)),
                           waits=(ws2, dg[2 * r + 1], yps), inc=True)
                prev_dd = ws2
                prev_mlf, prev_mls = mlf, mls
                last_yf, last_ys = mlf, mls
            if upto == "h_loc" and h == 0:
                return [last_yf, last_ys, dbg6, con_g_dbg]
            ps_small.gate = prev_dd
            prev_q = (last_yf, last_ys)
        if upto == "fwdq2":
            return [last_yf, last_ys, dbg6, con_g_dbg]
        ev5 = None
        for hf in range(2):
            c0, c1 = hf * TC, (hf + 1) * TC
            e1v = P.op("vector", lambda e, hf=hf, c0=c0: e.tensor_copy(al["y"][:, c0:c0 + TC], ps[hf].t[:, 0:TC]),
                       waits=(last_yf, last_ys, ev5), inc=True)
            if hf == 0:
                ps[0].gate = e1v
            else:
                ps[1].gate = e1v
            e2v = P.op("vector", lambda e, c0=c0: e.tensor_sub(gnscr, ps_misc.t[:, c0:c0 + TC], al["y"][:, c0:c0 + TC]),
                       waits=(e1v,), inc=True)
            e3v = P.op("vector", lambda e, c0=c0: e.tensor_mul(gnscr, gnscr, al["alpha"][:, c0:c0 + TC]),
                       waits=(e2v, ag4), inc=True)
            ev5 = P.op("vector", lambda e, c0=c0: e.tensor_add(al["y"][:, c0:c0 + TC], al["y"][:, c0:c0 + TC], gnscr),
                       waits=(e3v,), inc=True)
            if hf == 1:
                ps_misc.gate = e2v
        if G == 8:
            dbgy = P.dma("gpsimd", dr["dbg"][:, 2688:3712], al["y"][:, 0:1024], waits=(ev5,))
            ev5 = dbgy
        r1 = P.op("vector", lambda e: e.reduce_sum(al["gst"][:, 0:16].rearrange("p (a u) -> p a u", u=1),
                                                   al["y"][:, 0:1024].rearrange("p (a c) -> p a c", c=64),
                                                   axis=AX.X), waits=(ev5,), inc=True)
        m1g = P.op("vector", lambda e: e.tensor_scalar(al["gst"][:, 0:16], al["gst"][:, 0:16], 1.0 / 64, None,
                                                       ALU.mult), waits=(r1,), inc=True)
        r2g = None
        for hf in range(2):
            c0 = hf * TC
            sq = P.op("scalar", lambda e, c0=c0: e.activation(gnscr, al["y"][:, c0:c0 + TC], AF.Square),
                      waits=(ev5, m1g if hf == 0 else r2g), inc=True)
            r2g = P.op("vector", lambda e, hf=hf: e.reduce_sum(
                al["gst"][:, 16 + hf * 8:24 + hf * 8].rearrange("p (a u) -> p a u", u=1),
                gnscr.rearrange("p (a c) -> p a c", c=64), axis=AX.X), waits=(sq,), inc=True)
        q1 = P.op("vector", lambda e: e.tensor_scalar(al["gst"][:, 16:32], al["gst"][:, 16:32], 1.0 / 64, None,
                                                      ALU.mult), waits=(r2g,), inc=True)
        mu2 = P.op("scalar", lambda e: e.activation(al["gst"][:, 32:48], al["gst"][:, 0:16], AF.Square),
                   waits=(m1g,), inc=True)
        vv = P.op("vector", lambda e: e.tensor_sub(al["gst"][:, 16:32], al["gst"][:, 16:32], al["gst"][:, 32:48]),
                  waits=(q1, mu2), inc=True)
        ln1 = P.op("scalar", lambda e: e.activation(al["gst"][:, 32:48], al["gst"][:, 16:32], AF.Ln, bias=6.4e-4),
                   waits=(vv,), inc=True)
        rsg = P.op("scalar", lambda e: e.activation(al["gst"][:, 32:48], al["gst"][:, 32:48], AF.Exp, scale=-0.5),
                   waits=(ln1,), inc=True)
        mtt = P.op("tensor", lambda e: e.transpose(ps_small.t[0:16, 0:128], al["gst"][:, 0:16], identf[:, :]),
                   waits=(mu2,) + g_(ps_small), inc=True)
        mte = P.op("scalar", lambda e: e.activation(al["mut"], ps_small.t[0:16, 0:128], AF.Copy),
                   waits=(mtt,), inc=True)
        rtt = P.op("tensor", lambda e: e.transpose(ps_small.t[0:16, 128:256], al["gst"][:, 32:48], identf[:, :]),
                   waits=(rsg, mte), inc=True)
        rte = P.op("scalar", lambda e: e.activation(al["rst"], ps_small.t[0:16, 128:256], AF.Copy),
                   waits=(rtt,), inc=True)
        ps_small.gate = rte
        mmu = P.op("tensor", lambda e: (e.matmul(ps_misc.t[:, 0:TC], al["mut"], expbf[:, 0:TC],
                                                 start=True, stop=True),
                                        e.matmul(ps_misc.t[:, TC:1024], al["mut"], expbf[:, TC:1024],
                                                 start=True, stop=True))[-1], waits=(mte, exg1) + g_(ps_misc), inc=True)
        mrs = P.op("tensor", lambda e: (e.matmul(ps[0].t[:, 0:TC], al["rst"], expbf[:, 0:TC],
                                                 start=True, stop=True),
                                        e.matmul(ps[1].t[:, 0:TC], al["rst"], expbf[:, TC:1024],
                                                 start=True, stop=True))[-1],
                   waits=(rte, exg1) + g_(ps[0]) + g_(ps[1]), inc=True)
        v2b = None
        for hf in range(2):
            c0 = hf * TC
            v1q = P.op("vector", lambda e, c0=c0: e.tensor_sub(gnscr, al["y"][:, c0:c0 + TC],
                                                               ps_misc.t[:, c0:c0 + TC]),
                       waits=(mmu, r2g if hf == 0 else v2b), inc=True)
            v2b = P.op("vector", lambda e, hf=hf, c0=c0: e.tensor_mul(ynb[:, c0:c0 + TC],
                                                                      gnscr, ps[hf].t[:, 0:TC]),
                       waits=(mrs, v1q), inc=True)
        ps_misc.gate = v1q
        ps[0].gate = v2b
        ps[1].gate = v2b
        prev_gn[0] = v2b
        gw = None
        for q in range(8):
            gdm = P.dma("sync", rtok[:, q * 128:(q + 1) * 128],
                        dr["g"][q * 128:(q + 1) * 128, G * 128:(G + 1) * 128], waits=tr_done + (gw,))
            tq = P.op("tensor", lambda e, q=q: e.transpose(ps_trv.t[:, q * 128:(q + 1) * 128],
                                                           ynb[:, q * 128:(q + 1) * 128], ident[:, :]),
                      waits=(v2b,) + ((gw,) if gw else g_(ps_trv)), inc=True)
            gw1 = P.op("vector", lambda e, q=q: e.tensor_scalar(SB1[:, 0:128],
                                                                ps_trv.t[:, q * 128:(q + 1) * 128],
                                                                al["gcol"][:, q:q + 1], None, ALU.mult),
                       waits=(tq, gbl), inc=True)
            gw = P.op("vector", lambda e, q=q, G=G: e.scalar_tensor_tensor(
                BA[q][:, G * 128:(G + 1) * 128], SB1[:, 0:128],
                al["bcol"][:, q:q + 1], rtok[:, q * 128:(q + 1) * 128], ALU.add, ALU.mult),
                      waits=(gw1, gdm), inc=True)
            ba_gates.append(gw)
        ps_trv.gate = gw
        if upto == "fwdq3":
            return [gw, dbg6, con_g_dbg]
        upd = state_pass(G, "fwd", dg, dbc, lamg, upd_extra=(e1v, gw), first_waits=(v1q, ev5))
        fwd_prev = tuple(upd) + (gw,)
        prev_q = (gw, gw)
    if isinstance(upto, tuple) and upto[0] == "fwd":
        return list(fwd_prev) + [dbg6, con_g_dbg]
    def xin_att(ic, t0, tw):
        return BA[ic][:, t0:t0 + tw]
    rg = []
    def epi_res(P_, pp, oc, t0, tw, mmg):
        g = P.op("vector", lambda e, pp=pp, oc=oc, t0=t0, tw=tw: e.tensor_add(
            x_tiles[oc][:, t0:t0 + tw], x_tiles[oc][:, t0:t0 + tw], pp.t[:, 0:tw]), waits=(mmg,), inc=True)
        pp.gate = g
        rg.append(g)
    linear_chunked(P, ws, W["W_o"], None, D, D, xin_att, T, C.onesrow, epi_res, ps,
                   extra_waits=tuple(fwd_prev), load_waits=tuple(fwd_prev))
    return [rg[-1]]



def make_env(P):
    env = {}
    env["x"] = [P.sbuf("x", [128, T], F32) for _ in range(ND)]
    env["BA"] = [P.sbuf("BA", [128, T], BF16) for _ in range(ND)]
    env["BB"] = [P.sbuf("BB", [128, T], BF16) for _ in range(ND)]
    env["SF"] = [P.sbuf("SF", [128, T], F32) for _ in range(4)]
    small = [P.sbuf("scr", [128, 2 * TC], BF16) for _ in range(6)]
    env["SMALL"] = [env["BB"][i][:, 0:TC] for i in range(8)] + [env["BB"][i][:, TC:2 * TC] for i in range(8)]
    env["t2"] = [small[i][:, 0:1024] for i in range(4)]
    env["t3"] = [small[4 + i][:, 0:1024] for i in range(2)]
    env["tmp_tt"] = [P.sbuf("tmptt", [128, TC], F32) for _ in range(2)]
    env["ps_misc"] = P.psum("psmisc", [128, 1024])
    env["ps_pair"] = [P.psum("pspa", [128, TC]), P.psum("pspb", [128, TC])]
    env["ps_small"] = P.psum("pssm", [128, TC])
    env["small6"] = small
    env["ps_acc"] = P.psum("psacc", [128, TC])
    env["ps_trv"] = PsumTile(P.es.enter_context(P.nc.psum_tensor(P.uname("pstrv"), [128, 1024], BF16)))
    env["ws"] = WStream(P, max_cols=1024, nbuf=2, bias_cols=256)
    env["rows"] = Rows(P, width=2432, reserve=640)
    env["srow"] = env["rows"].perm(TC + 8)
    env["srow_prev"] = [None]
    return env


def emit_load_x(P, env, x_in):
    gates = []
    for i in range(ND):
        gates.append(P.dma("sync", env["x"][i][:, 0:T], x_in[i * 128:(i + 1) * 128, 0:T]))
    return gates


def emit_store_x(P, env, x_out, waits):
    gates = []
    for i in range(ND):
        gates.append(P.dma("gpsimd", x_out[i * 128:(i + 1) * 128, 0:T], env["x"][i][:, 0:T],
                           waits=waits if i == 0 else ()))
    return gates


def finish(P, store_gates):
    def fn(e, gates=tuple(store_gates)):
        for (s, v) in gates:
            e.wait_ge(s.h, v)
    P.on("gpsimd", fn)


def emit_ident(P, env, ident_dram):
    env["ident"] = P.sbuf("identsb", [128, 128], BF16)
    env["ident_g"] = P.dma("sync", env["ident"][:, :], ident_dram)


def host_a0_consts(p):
    import numpy as np
    masks = np.zeros((128, 8192), dtype=np.float32)
    kk = np.arange(128)[:, None]
    qq = np.arange(128)[None, :]
    tri = (kk <= qq).astype(np.float32)
    for r in range(4):
        for w in range(4):
            off = (r * 4 + w) * 512
            ck = 4 * w + r
            for i in range(4):
                cq = 4 * i + p
                blk = masks[:, off + 128 * i: off + 128 * (i + 1)]
                if cq > ck:
                    blk[:] = 1.0
                elif cq == ck:
                    blk[:] = tri
    idx = own_token_index(p)
    cos, sin = rope_tables_host(SEQ, 32)
    tabs = np.concatenate([cos[idx].T, sin[idx].T], axis=0)
    permM = np.zeros((128, 32), dtype=np.float32)
    for c in range(16):
        permM[64 + c + 16, c] = -1.0
    for c in range(16, 32):
        permM[64 + c - 16, c] = 1.0
    return (masks.astype(NP_BF16), tabs.astype(NP_BF16), permM.astype(NP_BF16),
            np.eye(128).astype(NP_BF16))


def rope_tables_host(Tlen, dim):
    import numpy as np
    inv_freq = 1.0 / 10000.0 ** (np.arange(0, dim, 2, dtype=np.float32) / dim)
    t = np.arange(Tlen, dtype=np.float32)
    freqs = np.outer(t, inv_freq)
    emb = np.concatenate([freqs, freqs], -1)
    return np.cos(emb).astype(np.float32), np.sin(emb).astype(np.float32)


A0_WSHAPES = {
    "W_kv": (D, 256), "g_kv": (256,), "W_kdec": (256, D), "W_vdec": (256, D),
    "W_q": (D, 512), "W_qdec": (512, D), "W_krope": (D, 512), "W_qrope": (512, 512),
    "W_o": (D, D),
}


def build_a0_program(upto=None):
    nc = bass.Bass("TRN2", target_bir_lowering=False, debug=False)
    es = contextlib.ExitStack()
    with es:
        P = Prog(nc, es)
        x_in = nc.dram_tensor("x_in", [D, T], F32, kind="ExternalInput").ap()
        x_out = nc.dram_tensor("x_out", [D, T], F32, kind="ExternalOutput").ap()
        W = {k: nc.dram_tensor("W_" + k, list(shp), F32, kind="ExternalInput").ap()
             for k, shp in A0_WSHAPES.items()}
        W["a0_masks"] = nc.dram_tensor("a0_masks", [128, 8192], BF16, kind="ExternalInput").ap()
        W["a0_tabs"] = nc.dram_tensor("a0_tabs", [64, T], BF16, kind="ExternalInput").ap()
        W["perm32"] = nc.dram_tensor("perm32", [128, 32], BF16, kind="ExternalInput").ap()
        ident_in = nc.dram_tensor("ident", [128, 128], BF16, kind="ExternalInput").ap()
        C = emit_consts(P)
        env = make_env(P)
        emit_ident(P, env, ident_in)
        ld = emit_load_x(P, env, x_in)
        if upto == "dbg":
            env["dbg2"] = nc.dram_tensor("dbg2", [512, T], BF16, kind="ExternalOutput").ap()
            upto = None
        gates = emit_A0(P, C, W, env["x"], env, x_ready=tuple(ld) + (C.ready,), upto=upto)
        st = emit_store_x(P, env, x_out, waits=tuple(gates[-1:]))
        finish(P, st)
        P.finalize()
    return nc


F1_WSHAPES = {
    "wqkv": (2, D, 1536), "wo": (2, 512, D), "w1": (2, D, 5460), "b1": (2, 5460),
    "w2": (2, 2730, D), "b2": (2, D),
}


def host_f1_consts(p, d):
    import numpy as np
    sel = np.zeros((64, 16), dtype=np.float32)
    for j in range(16):
        g = 4 * j + p
        prev = g - 1
        if prev >= 0:
            sel[(prev % 4) * 16 + prev // 4, j] = 1.0
    def mk_mask(j0):
        m = np.zeros((128, 1024), dtype=np.float32)
        qq = np.arange(128)
        for r in range(4):
            for c2 in range(2):
                ck = 4 * (c2 + (0 if j0 == 0 else j0 - 1)) + r
                cq = 4 * j0 + p
                kk = np.arange(128)
                dchunk = 128 * (ck - cq)
                dd = dchunk + kk[None, :] - qq[:, None]
                blk = np.where((dd <= 0) & (dd >= -512), 0.0, -30000.0).astype(np.float32)
                m[:, (r * 2 + c2) * 128:(r * 2 + c2 + 1) * 128] = blk
        return m
    mask = mk_mask(1)
    mask0 = mk_mask(0)
    idx = own_token_index(p)
    DH, WIN = 64, 512
    half = np.arange(0, DH, 2, dtype=np.float32)
    inv_freq = 1.0 / 10000.0 ** (half / DH)
    tt = np.arange(SEQ, dtype=np.float32)
    freqs = tt[:, None] * inv_freq[None, :]
    freqs = np.concatenate([freqs, freqs], axis=-1)
    base = (half + 0.4 * DH) / (1.4 * DH)
    power = (tt - SEQ // 2) / float(WIN // 2)
    sc = base[None, :] ** power[:, None]
    sc = np.concatenate([sc, sc], axis=-1)
    cosf, sinf = np.cos(freqs), np.sin(freqs)
    qc = (cosf * sc)[idx].T
    qs = (sinf * sc)[idx].T
    kc = (cosf / sc)[idx].T
    ks = (sinf / sc)[idx].T
    tabs = np.concatenate([np.tile(a, (8, 1)) for a in (qc, qs, kc, ks)], axis=0)
    bo = np.zeros((128, 32), dtype=np.float32)
    for ic in range(4):
        for r in range(128):
            bo[r, ic * 8 + 2 * ic + r // 64] = 1.0
    bot = np.zeros((8, 512), dtype=np.float32)
    for rr in range(512):
        bot[rr // 64, rr] = 1.0
    perm = np.zeros((128, 128), dtype=np.float32)
    for blk in range(2):
        o = blk * 64
        for c in range(32):
            perm[o + c + 32, o + c] = -1.0
        for c in range(32, 64):
            perm[o + c - 32, o + c] = 1.0
    out = {
        "f1_sel": sel.astype(NP_BF16), "f1_mask": mask.astype(NP_BF16), "f1_mask0": mask0.astype(NP_BF16),
        "f1_tabs": tabs.astype(NP_BF16), "f1_bo": bo.astype(NP_BF16), "f1_bot": bot.astype(NP_BF16),
        "f1_perm": perm.astype(NP_BF16),
        "f1_qsc": np.ascontiguousarray(np.tile(np.asarray(d["L1_ffn_q_scale"]), (1, 2)).astype(np.float32)),
        "f1_ksc": np.ascontiguousarray(np.tile(np.asarray(d["L1_ffn_k_scale"]), (1, 2)).astype(np.float32)),
        "f1_lngt": np.ascontiguousarray(np.asarray(d["L1_ffn_ln_g"]).reshape(2, 8, 128).transpose(0, 2, 1).astype(np.float32)),
        "f1_lnbt": np.ascontiguousarray(np.asarray(d["L1_ffn_ln_b"]).reshape(2, 8, 128).transpose(0, 2, 1).astype(np.float32)),
    }
    return out


F1_CONST_SHAPES = {
    "f1_sel": ((64, 16), BF16), "f1_mask": ((128, 1024), BF16), "f1_mask0": ((128, 1024), BF16),
    "f1_tabs": ((2048, T), BF16), "f1_bo": ((128, 32), BF16), "f1_bot": ((8, 512), BF16),
    "f1_perm": ((128, 128), BF16), "f1_qsc": ((2, 128), F32), "f1_ksc": ((2, 128), F32),
    "f1_lngt": ((2, 128, 8), F32), "f1_lnbt": ((2, 128, 8), F32),
}


def build_f1_program():
    nc = bass.Bass("TRN2", target_bir_lowering=False, debug=False)
    es = contextlib.ExitStack()
    with es:
        P = Prog(nc, es)
        x_in = nc.dram_tensor("x_in", [D, T], F32, kind="ExternalInput").ap()
        x_out = nc.dram_tensor("x_out", [D, T], F32, kind="ExternalOutput").ap()
        W = {k: nc.dram_tensor("W_" + k, list(shp), F32, kind="ExternalInput").ap()
             for k, shp in F1_WSHAPES.items()}
        for k, (shp, dt) in F1_CONST_SHAPES.items():
            W[k] = nc.dram_tensor(k, list(shp), dt, kind="ExternalInput").ap()
        ident_in = nc.dram_tensor("identin", [128, 128], BF16, kind="ExternalInput").ap()
        C = emit_consts(P)
        env = make_env(P)
        emit_ident(P, env, ident_in)
        ld = emit_load_x(P, env, x_in)
        gates = emit_F1(P, C, W, env["x"], env, x_ready=tuple(ld) + (C.ready,))
        st = emit_store_x(P, env, x_out, waits=tuple(gates))
        finish(P, st)
        P.finalize()
    return nc


def build_f0_program(Wkeys):
    nc = bass.Bass("TRN2", target_bir_lowering=False, debug=False)
    es = contextlib.ExitStack()
    with es:
        P = Prog(nc, es)
        x_in = nc.dram_tensor("x_in", [D, T], F32, kind="ExternalInput").ap()
        x_out = nc.dram_tensor("x_out", [D, T], F32, kind="ExternalOutput").ap()
        W = {k: nc.dram_tensor("W_" + k, list(shp), F32, kind="ExternalInput").ap()
             for k, shp in Wkeys.items()}
        C = emit_consts(P)
        env = make_env(P)
        ld = emit_load_x(P, env, x_in)
        gates = emit_F0(P, C, W, env["x"], env, x_ready=tuple(ld) + (C.ready,))
        st = emit_store_x(P, env, x_out, waits=tuple(gates[-1:]))
        finish(P, st)
        P.finalize()
    return nc


F0_WSHAPES = {
    "cn_w1": (D, D), "cn_b1": (D,), "cn_w2": (D, 512), "cn_b2": (512,),
    "cn_w3": (512, 256), "cn_b3": (256,), "cn_w4": (256, 1), "cn_b4": (1,),
    "gc_w1": (D, 512), "gc_b1": (512,), "gc_w2": (512, 256), "gc_b2": (256,),
    "gc_w3": (256, 1), "gc_b3": (1,), "adaptive_range": (1,),
}


A1_WSHAPES = {
    "W_r": (D, D), "W_k": (D, D), "W_v": (D, D), "W_g": (D, D), "W_o": (D, D),
    "decay_mix": (D,), "head_decay_bias": (16,), "ln_beta": (D,), "ln_gamma": (D,),
    "slow_scale": (16,), "time_decay": (1, 1, D), "time_decay_w1": (D, 64), "time_decay_w2": (64, D),
    "time_maa_g": (1, 1, D), "time_maa_k": (1, 1, D), "time_maa_r": (1, 1, D), "time_maa_v": (1, 1, D),
    "time_maa_w": (1, 1, D), "time_maa_w1": (D, 160), "time_maa_w2": (5, 32, D), "time_maa_x": (1, 1, D),
}

A1_CONST_SHAPES = {
    "identf": ((128, 128), F32), "a1_selp": ((128, 16), BF16), "a1_seln": ((128, 16), BF16),
    "a1_bo16": ((128, 128), BF16), "a1_expb": ((16, 1024), BF16),
    "a1_expbf": ((16, 1024), F32), "a1_selpre": ((64, 16), F32), "a1_selgrp": ((64, 16), F32),
}


def host_a1_consts(p):
    import ml_dtypes
    bf = ml_dtypes.bfloat16
    d = {}
    d["identf"] = np.eye(128, dtype=np.float32)
    selp = np.zeros((128, 16), dtype=np.float32)
    seln = np.zeros((128, 16), dtype=np.float32)
    for j in range(NCH):
        c = 4 * j + p
        if c - 1 >= 0:
            selp[((c - 1) % 4) * 32 + (c - 1) // 4, j] = 1.0
        if c + 1 < GCH:
            seln[((c + 1) % 4) * 32 + 16 + (c + 1) // 4, j] = 1.0
    d["a1_selp"] = selp.astype(bf)
    d["a1_seln"] = seln.astype(bf)
    bo = np.zeros((128, 128), dtype=np.float32)
    for oc in range(8):
        bo[0:64, oc * 16 + 2 * oc] = 1.0
        bo[64:128, oc * 16 + 2 * oc + 1] = 1.0
    d["a1_bo16"] = bo.astype(bf)
    expb = np.zeros((16, 1024), dtype=np.float32)
    for h in range(16):
        expb[h, h * 64:(h + 1) * 64] = 1.0
    d["a1_expb"] = expb.astype(bf)
    d["a1_expbf"] = expb.copy()
    spre = np.zeros((64, 16), dtype=np.float32)
    sgrp = np.zeros((64, 16), dtype=np.float32)
    for j in range(NCH):
        if 4 * j + p - 1 >= 0:
            spre[4 * j + p - 1, j] = 1.0
        if 4 * j - 1 >= 0:
            sgrp[4 * j - 1, j] = 1.0
    d["a1_selpre"] = spre
    d["a1_selgrp"] = sgrp
    return d


def build_a1_program(upto=None):
    nc = bass.Bass("TRN2", target_bir_lowering=False, debug=False)
    es = contextlib.ExitStack()
    with es:
        P = Prog(nc, es)
        x_in = nc.dram_tensor("x_in", [D, T], F32, kind="ExternalInput").ap()
        x_out = nc.dram_tensor("x_out", [D, T], F32, kind="ExternalOutput").ap()
        W = {k: nc.dram_tensor("W_" + k, list(shp), F32, kind="ExternalInput").ap()
             for k, shp in A1_WSHAPES.items()}
        ident_in = nc.dram_tensor("identin", [128, 128], BF16, kind="ExternalInput").ap()
        C = emit_consts(P)
        env = make_env(P)
        env["a1_dbg_out"] = nc.dram_tensor("dbg_out", [128, 4096], F32, kind="ExternalOutput").ap()
        env["a1_sel_dram"] = {k: nc.dram_tensor(k, list(shp), dt, kind="ExternalInput").ap()
                              for k, (shp, dt) in A1_CONST_SHAPES.items()}
        emit_ident(P, env, ident_in)
        ld = emit_load_x(P, env, x_in)
        gates = emit_A1(P, C, W, env["x"], env, x_ready=tuple(ld) + (C.ready,), upto=upto)
        st = emit_store_x(P, env, x_out, waits=tuple(gates))
        finish(P, st)
        P.finalize()
    return nc


def own_token_index(p):
    j = np.arange(NCH)
    return ((4 * j[:, None] + p) * 128 + np.arange(128)[None, :]).reshape(-1)


def kernel(**inputs):
    import ml_dtypes
    x = np.asarray(inputs["L0_attn_x"], dtype=np.float32)
    pref = lambda pre: {k[len(pre):]: np.ascontiguousarray(np.asarray(v, dtype=np.float32))
                        for k, v in inputs.items() if k.startswith(pre)}
    pF0 = pref("L0_ffn_")
    pA0 = {k: v for k, v in pref("L0_attn_").items() if k != "x"}
    pF1 = pref("L1_ffn_")
    pA1 = pref("L1_attn_")
    idxs = [own_token_index(p) for p in range(4)]
    xc = [np.ascontiguousarray(x[c // 4][idxs[c % 4]].T) for c in range(8)]

    ncF0 = build_f0_program({k: v.shape for k, v in pF0.items()})
    ncA0 = build_a0_program()
    ncF1 = build_f1_program()
    ncA1 = build_a1_program()

    a0c = [host_a0_consts(p) for p in range(4)]
    f1c = [host_f1_consts(p, inputs) for p in range(4)]
    a1c = [host_a1_consts(p) for p in range(4)]
    eye_bf = np.eye(128, dtype=ml_dtypes.bfloat16)

    def launch(nc, mk):
        res = run_bass_kernel_spmd(nc, [mk(c) for c in range(8)], core_ids=list(range(8)))
        return [np.ascontiguousarray(res.results[c]["x_out"]) for c in range(8)]

    def mk_f0(c):
        m = {"x_in": xc[c]}
        for k, v in pF0.items():
            m["W_" + k] = v
        return m

    def mk_a0(c):
        masks, tabs, perm, ident = a0c[c % 4]
        m = {"x_in": xc[c], "a0_masks": masks, "a0_tabs": tabs, "perm32": perm, "ident": ident}
        for k, v in pA0.items():
            m["W_" + k] = v
        return m

    def mk_f1(c):
        m = {"x_in": xc[c], "identin": eye_bf}
        m.update(f1c[c % 4])
        for k in F1_WSHAPES:
            m["W_" + k] = pF1[k]
        return m

    def mk_a1(c):
        m = {"x_in": xc[c], "identin": eye_bf}
        m.update(a1c[c % 4])
        for k in A1_WSHAPES:
            m["W_" + k] = pA1[k]
        return m

    xc = launch(ncF0, mk_f0)
    xc = launch(ncA0, mk_a0)
    xc = launch(ncF0, mk_f0)
    xc = launch(ncF1, mk_f1)
    xc = launch(ncA1, mk_a1)
    xc = launch(ncF1, mk_f1)

    out = np.empty((2, SEQ, D), dtype=np.float32)
    for c in range(8):
        out[c // 4][idxs[c % 4]] = xc[c].T
    return out


# version 03 (final)
# speedup vs baseline: 1.0020x; 1.0020x over previous
import contextlib
import numpy as np
import concourse.bass as bass
import concourse.mybir as mybir
from concourse.bass_utils import run_bass_kernel_spmd

try:
    import ml_dtypes
    NP_BF16 = ml_dtypes.bfloat16
except Exception:
    NP_BF16 = None

F32 = mybir.dt.float32
BF16 = mybir.dt.bfloat16
AF = mybir.ActivationFunctionType
ALU = mybir.AluOpType
AX = mybir.AxisListType

D = 1024
T = 2048
NCH = 16
SEQ = 8192
GCH = 64
TC = 512
ND = 8

ENGINES = ("sync", "vector", "scalar", "tensor", "gpsimd")
RG = [[0, 1, 2, 3], [4, 5, 6, 7]]


class Sem:
    __slots__ = ("h", "n", "name")

    def __init__(self, h, name):
        self.h = h
        self.n = 0
        self.name = name

    def will(self, k=1):
        self.n += k
        return self.n

    @property
    def now(self):
        return (self, self.n)


class PsumTile:
    __slots__ = ("t", "gate")

    def __init__(self, t):
        self.t = t
        self.gate = None


class Prog:
    def __init__(self, nc, es):
        self.nc = nc
        self.es = es
        self.q = {e: [] for e in ENGINES}
        self._names = {}
        self.LD = self.sem("LD")
        self.GP = self.sem("GP")
        self.TE = self.sem("TE")
        self.VE = self.sem("VE")
        self.SE = self.sem("SE")
        self.GE = self.sem("GE")

    def sem_for(self, eng):
        return {"vector": self.VE, "scalar": self.SE, "tensor": self.TE, "gpsimd": self.GE}[eng]

    def sem(self, name):
        h = self.es.enter_context(self.nc.semaphore(self.uname(name)))
        return Sem(h, name)

    def uname(self, base):
        i = self._names.get(base, 0)
        self._names[base] = i + 1
        return f"{base}{i}" if i else base

    def sbuf(self, name, shape, dtype):
        return self.es.enter_context(self.nc.sbuf_tensor(self.uname(name), shape, dtype))

    def psum(self, name, shape, dtype=F32):
        return PsumTile(self.es.enter_context(self.nc.psum_tensor(self.uname(name), shape, dtype)))

    def dram(self, name, shape, dtype=F32):
        return self.nc.dram_tensor(self.uname(name), shape, dtype).ap()

    def on(self, eng, fn):
        self.q[eng].append(fn)

    def op(self, eng, spec, waits=(), inc=False):
        s = self.sem_for(eng) if inc else None
        tgt = s.will() if s else None
        waits = tuple(w for w in waits if w is not None)
        def fn(e, spec=spec, waits=waits, s=s):
            for (sem, v) in waits:
                if v > 0:
                    e.wait_ge(sem.h, v)
            ins = spec(e)
            if s is not None:
                ins.then_inc(s.h, 1)
        self.on(eng, fn)
        return (s, tgt) if s else None

    NDCH = 6

    def dma(self, queue, out, in_, waits=(), sem=None, **kw):
        if not hasattr(self, "dchan"):
            self.dchan = {}
            self.drot = {}
        if queue not in self.dchan:
            self.dchan[queue] = [self.sem(f"dc_{queue}{i}") for i in range(self.NDCH)]
            self.drot[queue] = 0
        ch = self.dchan[queue][self.drot[queue] % self.NDCH]
        self.drot[queue] += 1
        prev = ch.n
        tgt = ch.will(16)
        waits = tuple(w for w in waits if w is not None)
        def fn(e, out=out, in_=in_, waits=waits, kw=kw, prev=prev):
            if prev > 0:
                e.wait_ge(ch.h, prev)
            for (s, v) in waits:
                if v > 0:
                    e.wait_ge(s.h, v)
            e.dma_start(out=out, in_=in_, **kw).then_inc(ch.h, 16)
        self.on(queue, fn)
        return (ch, tgt)

    def dma_frontier(self, queue):
        if not hasattr(self, "dchan") or queue not in self.dchan:
            return []
        return [(ch, ch.n) for ch in self.dchan[queue]]

    def collective(self, kind, op, ins, outs, waits=()):
        cs = self.sem("cc")
        tgt = cs.will()
        waits = tuple(w for w in waits if w is not None)
        def fn(e, waits=waits):
            for (s, v) in waits:
                if v > 0:
                    e.wait_ge(s.h, v)
            e.collective_compute(kind, op, replica_groups=RG, ins=ins, outs=outs).then_inc(cs.h, 1)
        self.on("gpsimd", fn)
        return (cs, tgt)

    def finalize(self):
        with self.nc.Block() as block:
            for eng in ENGINES:
                def runner(e, _q=tuple(self.q[eng])):
                    for fn in _q:
                        fn(e)
                getattr(block, eng)(runner)


class WTile:
    def __init__(self, buf, nic, ncols, ready, ws, slot):
        self.buf, self.nic, self.ncols, self.ready, self.ws, self.slot = buf, nic, ncols, ready, ws, slot

    def lhsT(self, ic, o0, ow):
        base = ic * self.ncols + o0
        return self.buf[:, base:base + ow]

    def release(self, gate):
        self.ws.free_at[self.slot] = gate


class WStream:
    def __init__(self, P, max_cols=2048, nbuf=2, tag="w", bias_cols=512):
        self.P = P
        self.nbuf = nbuf
        self.max_cols = max_cols
        self.bias_cols = bias_cols
        tot = max_cols + bias_cols
        self.f32 = [P.sbuf(tag + "ld", [128, tot], F32) for _ in range(nbuf)]
        self.bf = [P.sbuf(tag + "bf", [128, tot], BF16) for _ in range(nbuf)]
        self.free_at = [None] * nbuf
        self.i = 0

    def load(self, w_dram, IN, ncols, col0=0, bias_dram=None, extra=()):
        P = self.P
        nic = IN // 128
        n = nic * ncols
        assert n <= self.max_cols and (bias_dram is None or ncols <= self.bias_cols), (IN, ncols)
        bf32, bbf = self.f32[self.i], self.bf[self.i]
        gate = self.free_at[self.i]
        src = w_dram[:, col0:col0 + ncols].rearrange("(ic p) o -> p ic o", p=128)
        dst = bf32[:, 0:n].rearrange("p (ic o) -> p ic o", o=ncols)
        ldg = P.dma("sync", dst, src, waits=(gate,) + tuple(extra),
                    **({"allow_slow_non_contiguous": True} if ncols < 16 else {}))
        lds = [ldg]
        if bias_dram is not None:
            bsrc = bias_dram[col0:col0 + ncols].rearrange("(u n) -> u n", u=1)
            lds.append(P.dma("sync", bf32[0:1, n:n + ncols], bsrc, waits=(gate,) + tuple(extra)))
        cast = P.op("vector", lambda e, a=bbf, b=bf32, n=n: e.tensor_copy(a[:, 0:n], b[:, 0:n]),
                    waits=tuple(lds), inc=True)
        if bias_dram is not None:
            cast = P.op("vector", lambda e, a=bbf, b=bf32, n=n, m=ncols: e.tensor_copy(a[0:1, n:n + m], b[0:1, n:n + m]),
                        waits=(cast,), inc=True)
        wt = WTile(bbf, nic, ncols, cast, self, self.i)
        wt.bias_off = n
        self.i = (self.i + 1) % self.nbuf
        return wt

    def brow(self, wt, o0, ow):
        return wt.buf[0:1, wt.bias_off + o0: wt.bias_off + o0 + ow]

    def load_rows(self, w_dram, r0, rows, ncols, col0=0, bias_dram=None, base=0):
        P = self.P
        assert ncols <= self.max_cols and (bias_dram is None or ncols <= self.bias_cols)
        bf32, bbf = self.f32[self.i], self.bf[self.i]
        gate = self.free_at[self.i]
        lds = [P.dma("sync", bf32[base:base + rows, 0:ncols], w_dram[r0:r0 + rows, col0:col0 + ncols], waits=(gate,))]
        if bias_dram is not None:
            lds.append(P.dma("sync", bf32[0:1, self.max_cols:self.max_cols + ncols],
                             bias_dram[col0:col0 + ncols].rearrange("(u n) -> u n", u=1), waits=(gate,)))
            cast = P.op("vector", lambda e, a=bbf, b=bf32, rows=rows, n=ncols, base=base:
                        e.tensor_copy(a[base:base + rows, 0:n], b[base:base + rows, 0:n]), waits=tuple(lds), inc=True)
            cast = P.op("vector", lambda e, a=bbf, b=bf32, n=ncols:
                        e.tensor_copy(a[0:1, self.max_cols:self.max_cols + n],
                                      b[0:1, self.max_cols:self.max_cols + n]), waits=(cast,), inc=True)
        else:
            cast = P.op("vector", lambda e, a=bbf, b=bf32, rows=rows, n=ncols, base=base:
                        e.tensor_copy(a[base:base + rows, 0:n], b[base:base + rows, 0:n]), waits=tuple(lds), inc=True)
        wt = WTile(bbf, 1, ncols, cast, self, self.i)
        wt.bias_off = self.max_cols
        wt.row_count = rows
        self.i = (self.i + 1) % self.nbuf
        return wt


class Rows:

    def __init__(self, P, width=2048, reserve=64):
        self.t = P.sbuf("rows", [128, width], F32)
        self.width = width
        self.reserve = reserve
        self.cur = reserve
        self.rcur = 0

    def seg(self, n, base=0):
        if self.cur + n > self.width:
            self.cur = self.reserve
        ap = self.t[base:base + 1, self.cur:self.cur + n]
        self.cur += n
        return ap

    def perm(self, n):
        assert self.rcur + n <= self.reserve
        ap = self.t[0:1, self.rcur:self.rcur + n]
        self.rcur += n
        return ap

    def permf(self, n, rows=128):
        assert self.rcur + n <= self.reserve
        ap = self.t[0:rows, self.rcur:self.rcur + n]
        self.rcur += n
        return ap


def load_row(P, dram_vec, n, tag="row"):
    r = P.sbuf(tag, [1, n], F32)
    src = dram_vec if len(dram_vec.shape) == 2 else dram_vec.rearrange("(u n) -> u n", u=1)
    g = P.dma("sync", r[:, :], src)
    rb = P.sbuf(tag + "b", [1, n], BF16)
    g2 = P.op("vector", lambda e: e.tensor_copy(rb[:, :], r[:, :]), waits=(g,), inc=True)
    return r, rb, g2


def emit_linear(P, wt, xin, OUT, Ttok, augs, epilogue, psums, o_base=0, extra_waits=(), oc_list=None):
    k = 0
    first = True
    for oc in (oc_list if oc_list is not None else range(OUT // 128)):
        for t0 in range(0, Ttok, TC):
            tw = min(TC, Ttok - t0)
            ps = psums[k % len(psums)]
            k += 1
            waits = []
            if first:
                waits += list(extra_waits)
                if wt is not None:
                    waits.append(wt.ready)
            first = False
            if ps.gate is not None:
                waits.append(ps.gate)
            def spec(e, oc=oc, t0=t0, tw=tw, wt=wt, augs=augs, ps=ps):
                m = None
                nic = wt.nic if wt else 0
                for ic in range(nic):
                    m = e.matmul(ps.t[:, 0:tw], wt.lhsT(ic, o_base + oc * 128, 128),
                                 xin(ic, t0, tw), start=(ic == 0),
                                 stop=(not augs and ic == nic - 1))
                last = len(augs) - 1
                for ai, (xrow, wrow) in enumerate(augs):
                    m = e.matmul(ps.t[:, 0:tw], wrow(o_base + oc * 128, 128),
                                 xrow(t0, tw), start=(wt is None and ai == 0), stop=(ai == last))
                return m
            mmg = P.op("tensor", spec, waits=waits, inc=True)
            epilogue(P, ps, oc, t0, tw, mmg)


def linear_chunked(P, ws, wdram, bias_dram, IN, OUT, xin, Ttok, onesrow, epilogue, psums,
                   extra_waits=(), load_waits=()):
    step = (ws.max_cols // (IN // 128)) // 128 * 128
    step = min(step, ws.bias_cols)
    first = True
    for col0 in range(0, OUT, step):
        ncols = min(step, OUT - col0)
        wt = ws.load(wdram, IN, ncols, col0=col0, bias_dram=bias_dram, extra=load_waits)
        augs = [(onesrow, lambda o0, ow, wt=wt: ws.brow(wt, o0, ow))] if bias_dram is not None else []
        def epi(P, ps, oc, t0, tw, mmg, col0=col0):
            epilogue(P, ps, col0 // 128 + oc, t0, tw, mmg)
        emit_linear(P, wt, xin, ncols, Ttok, augs, epi, psums,
                    extra_waits=(extra_waits if first else ()))
        first = False
        wt.release(P.TE.now)
    return P.TE.now


def epi_act(P_, dst, func=AF.Copy, scale=1.0, eng="scalar", gates=None):
    def epilogue(P, ps, oc, t0, tw, mmg):
        g = P.op(eng, lambda e, ps=ps, oc=oc, t0=t0, tw=tw:
                 e.activation(dst(oc, t0, tw), ps.t[:, 0:tw], func, scale=scale),
                 waits=(mmg,), inc=True)
        ps.gate = g
        if gates is not None:
            gates.append(g)
    return epilogue


def emit_bcast(P, ps, c0, n, row_ap, C, waits=()):
    w = list(waits)
    if ps.gate is not None:
        w.append(ps.gate)
        ps.gate = None
    b = row_ap.base_partition()
    return P.op("tensor", lambda e, b=b: e.matmul(ps.t[:, c0:c0 + n], C.onescol(b),
                                                  row_ap, start=True, stop=True), waits=w, inc=True)


class Consts:
    pass


FCONSTS = (1e-5, 1e-6, 6.4e-4, 1.0 / D, 1.0 / SEQ, -0.5, 1.4, -1.0, 2.0,
           1.0 / 256, 1.0 / 64, 8.0, 0.125, 64.0 ** -0.5, 96.0 ** -0.5)


def emit_consts(P):
    C = Consts()
    C.ones128_bf = P.sbuf("ones128bf", [128, 1], BF16)
    C.ones_col_f32 = P.sbuf("onescolf", [128, 128], F32)
    C.ones_row_bf = P.sbuf("onesrowbf", [1, TC], BF16)
    C.fc = P.sbuf("fconsts", [128, len(FCONSTS)], F32)
    g = None
    for tgt, val in ((C.ones128_bf, 1.0), (C.ones_col_f32, 1.0), (C.ones_row_bf, 1.0)):
        g = P.op("vector", lambda e, t=tgt, v=val: e.memset(t[:, :], v), inc=True)
    C.onescol = lambda b: C.ones_col_f32[b:b + 1, 0:128]
    for k, v in enumerate(FCONSTS):
        g = P.op("vector", lambda e, k=k, v=v: e.memset(C.fc[:, k:k + 1], v), inc=True)
        P.nc.const_aps.register(F32, v, C.fc[:, k:k + 1])
    C.ready = g
    for eng in ("scalar", "tensor", "gpsimd"):
        P.on(eng, lambda e, g=g: e.wait_ge(g[0].h, g[1]))
    C.onesrow = lambda t0, tw: C.ones_row_bf[0:1, 0:tw]
    return C


def emit_ln(P, C, rows, x_tiles, xn_tiles, scr_bf, ps_misc, Ttok=T, eps=1e-5, x_ready=()):
    mu = rows.seg(TC)
    m2 = rows.seg(TC)
    rs = rows.seg(TC)
    out_gates = []
    for t0 in range(0, Ttok, TC):
        cast_g = sq_g = None
        for i in range(ND):
            cast_g = P.op("vector", lambda e, i=i, t0=t0: e.tensor_copy(scr_bf[i][:, 0:TC], x_tiles[i][:, t0:t0 + TC]),
                          waits=tuple(x_ready) + ((ps_misc.gate,) if i == 0 and ps_misc.gate else ()), inc=True)
            sq_g = P.op("scalar", lambda e, i=i: e.activation(scr_bf[ND + i][:, 0:TC], scr_bf[i][:, 0:TC], AF.Square),
                        waits=(cast_g,), inc=True)
        w = [cast_g, sq_g]
        if ps_misc.gate is not None:
            w.append(ps_misc.gate)
            ps_misc.gate = None
        def spec(e):
            for i in range(ND):
                e.matmul(ps_misc.t[0:1, 0:TC], C.ones128_bf[:, 0:1], scr_bf[i][:, 0:TC],
                         start=(i == 0), stop=(i == ND - 1))
            m = None
            for i in range(ND):
                m = e.matmul(ps_misc.t[0:1, TC:2 * TC], C.ones128_bf[:, 0:1], scr_bf[ND + i][:, 0:TC],
                             start=(i == 0), stop=(i == ND - 1))
            return m
        stat_g = P.op("tensor", spec, waits=w, inc=True)
        g1 = P.op("scalar", lambda e: e.activation(mu, ps_misc.t[0:1, 0:TC], AF.Copy, scale=1.0 / D),
                  waits=(stat_g,), inc=True)
        g2 = P.op("scalar", lambda e: e.activation(m2, mu, AF.Square), waits=(g1,), inc=True)
        g3 = P.op("scalar", lambda e: e.activation(rs, ps_misc.t[0:1, TC:2 * TC], AF.Copy, scale=1.0 / D),
                  waits=(g2,), inc=True)
        g4 = P.op("vector", lambda e: e.tensor_sub(rs, rs, m2), waits=(g3,), inc=True)
        g5 = P.op("scalar", lambda e, eps=eps: e.activation(m2, rs, AF.Ln, bias=eps), waits=(g4,), inc=True)
        g6 = P.op("scalar", lambda e: e.activation(rs, m2, AF.Exp, scale=-0.5), waits=(g5,), inc=True)
        g7 = P.op("vector", lambda e: e.tensor_mul(m2, mu, rs), waits=(g6,), inc=True)
        b1 = emit_bcast(P, ps_misc, 0, TC, rs, C, waits=(g6,))
        b2 = emit_bcast(P, ps_misc, TC, TC, m2, C, waits=(g7,))
        last = None
        for i in range(ND):
            a1 = P.op("vector", lambda e, i=i, t0=t0: e.tensor_mul(xn_tiles[i][:, t0:t0 + TC], x_tiles[i][:, t0:t0 + TC], ps_misc.t[:, 0:TC]),
                      waits=(b1,), inc=True)
            last = P.op("vector", lambda e, i=i, t0=t0: e.tensor_sub(xn_tiles[i][:, t0:t0 + TC], xn_tiles[i][:, t0:t0 + TC], ps_misc.t[:, TC:2 * TC]),
                        waits=(b2, a1), inc=True)
        ps_misc.gate = last
        out_gates.append(last)
    return out_gates


def emit_ln_inplace(P, C, rows, x_tiles, xn_tiles, scr_bf, ps_misc, Ttok=T, eps=1e-5, x_ready=()):
    mu = rows.seg(TC)
    m2 = rows.seg(TC)
    rs = rows.seg(TC)
    out_gates = []
    for t0 in range(0, Ttok, TC):
        cast_g = sq_g = None
        for i in range(ND):
            cast_g = P.op("vector", lambda e, i=i, t0=t0: e.tensor_copy(scr_bf[i][:, 0:TC], x_tiles[i][:, t0:t0 + TC]),
                          waits=tuple(x_ready) + ((ps_misc.gate,) if i == 0 and ps_misc.gate else ()), inc=True)
            sq_g = P.op("scalar", lambda e, i=i: e.activation(scr_bf[ND + i][:, 0:TC], scr_bf[i][:, 0:TC], AF.Square),
                        waits=(cast_g,), inc=True)
        w = [cast_g, sq_g]
        if ps_misc.gate is not None:
            w.append(ps_misc.gate)
            ps_misc.gate = None
        def spec(e):
            for i in range(ND):
                e.matmul(ps_misc.t[0:1, 0:TC], C.ones128_bf[:, 0:1], scr_bf[i][:, 0:TC],
                         start=(i == 0), stop=(i == ND - 1))
            m = None
            for i in range(ND):
                m = e.matmul(ps_misc.t[0:1, TC:2 * TC], C.ones128_bf[:, 0:1], scr_bf[ND + i][:, 0:TC],
                             start=(i == 0), stop=(i == ND - 1))
            return m
        stat_g = P.op("tensor", spec, waits=w, inc=True)
        g1 = P.op("scalar", lambda e: e.activation(mu, ps_misc.t[0:1, 0:TC], AF.Copy, scale=1.0 / D),
                  waits=(stat_g,), inc=True)
        g2 = P.op("scalar", lambda e: e.activation(m2, mu, AF.Square), waits=(g1,), inc=True)
        g3 = P.op("scalar", lambda e: e.activation(rs, ps_misc.t[0:1, TC:2 * TC], AF.Copy, scale=1.0 / D),
                  waits=(g2,), inc=True)
        g4 = P.op("vector", lambda e: e.tensor_sub(rs, rs, m2), waits=(g3,), inc=True)
        g5 = P.op("scalar", lambda e, eps=eps: e.activation(m2, rs, AF.Ln, bias=eps), waits=(g4,), inc=True)
        g6 = P.op("scalar", lambda e: e.activation(rs, m2, AF.Exp, scale=-0.5), waits=(g5,), inc=True)
        g7 = P.op("vector", lambda e: e.tensor_mul(m2, mu, rs), waits=(g6,), inc=True)
        b1 = emit_bcast(P, ps_misc, 0, TC, rs, C, waits=(g6,))
        b2 = emit_bcast(P, ps_misc, TC, TC, m2, C, waits=(g7, b1))
        last = None
        for i in range(ND):
            a1 = P.op("vector", lambda e, i=i, t0=t0: e.tensor_mul(x_tiles[i][:, t0:t0 + TC], x_tiles[i][:, t0:t0 + TC], ps_misc.t[:, 0:TC]),
                      waits=(b1,), inc=True)
            a2 = P.op("vector", lambda e, i=i, t0=t0: e.tensor_sub(x_tiles[i][:, t0:t0 + TC], x_tiles[i][:, t0:t0 + TC], ps_misc.t[:, TC:2 * TC]),
                      waits=(b2, a1), inc=True)
            last = P.op("vector", lambda e, i=i, t0=t0: e.tensor_copy(xn_tiles[i][:, t0:t0 + TC], x_tiles[i][:, t0:t0 + TC]),
                        waits=(a2,), inc=True)
        ps_misc.gate = last
        out_gates.append(last)
    return out_gates


def emit_F0(P, C, W, x_tiles, env, x_ready=()):
    xn = env["BA"]
    scr = env["SMALL"]
    ps_misc = env["ps_misc"]
    ps = env["ps_pair"]
    ps_small = env["ps_small"]
    ws = env["ws"]
    SF = env["SF"]
    BB = env["BB"]

    rows = env["rows"]
    ln_g = emit_ln(P, C, rows, x_tiles, xn, scr, ps_misc, x_ready=x_ready)

    if "f0_part" not in env:
        env["f0_part"] = P.sbuf("gfpart", [128, ND], F32)
        env["f0_gsum"] = P.sbuf("gsum", [128, ND], F32)
        env["f0_gctx"] = P.sbuf("gctx", [128, ND], BF16)
        env["f0_cols"] = {}
    part = env["f0_part"]
    rg = None
    for i in range(ND):
        rg = P.op("vector", lambda e, i=i: e.reduce_sum(part[:, i:i + 1], xn[i][:, 0:T], axis=AX.X),
                  waits=(ln_g[-1],) if i == 0 else (), inc=True)
    cc_in = P.dram("gf_in", [128, ND])
    cc_out = P.dram("gf_out", [128, ND])
    g = P.dma("gpsimd", cc_in, part[:, :], waits=(rg,))
    ccg = P.collective("AllReduce", ALU.add, [cc_in], [cc_out], waits=(g,))
    gsum = env["f0_gsum"]
    g2 = P.dma("gpsimd", gsum[:, :], cc_out, waits=(ccg,))
    gctx = env["f0_gctx"]
    gc_g = P.op("scalar", lambda e: e.activation(gctx[:, :], gsum[:, :], AF.Copy, scale=1.0 / SEQ),
                waits=(g2,), inc=True)

    def col_linear(w_key, b_key, IN, OUT, xcol, xcol_g, act, tag):
        ncol = (OUT + 127) // 128
        if tag not in env["f0_cols"]:
            env["f0_cols"][tag] = P.sbuf(tag, [128, ncol], BF16)
        out_col = env["f0_cols"][tag]
        gate = None
        for oc in range(ncol):
            ow = min(128, OUT - oc * 128)
            wt = ws.load(W[w_key], IN, ow, col0=oc * 128, bias_dram=W[b_key])
            w = [wt.ready, xcol_g]
            if ps_small.gate is not None:
                w.append(ps_small.gate)
            def spec(e, oc=oc, ow=ow, wt=wt):
                for ic in range(IN // 128):
                    e.matmul(ps_small.t[0:ow, 0:1], wt.lhsT(ic, 0, ow), xcol[:, ic:ic + 1],
                             start=(ic == 0), stop=False)
                return e.matmul(ps_small.t[0:ow, 0:1], ws.brow(wt, 0, ow),
                                C.ones_row_bf[0:1, 0:1], start=False, stop=True)
            mg = P.op("tensor", spec, waits=w, inc=True)
            gate = P.op("scalar", lambda e, oc=oc, ow=ow: e.activation(out_col[0:ow, oc:oc + 1], ps_small.t[0:ow, 0:1], act),
                        waits=(mg,), inc=True)
            ps_small.gate = gate
            wt.release(gate)
        return out_col, gate

    h1, h1g = col_linear("gc_w1", "gc_b1", D, 512, gctx, gc_g, AF.Gelu, "gfh1")
    h2, h2g = col_linear("gc_w2", "gc_b2", 512, 256, h1, h1g, AF.Gelu, "gfh2")
    gfc, gfg = col_linear("gc_w3", "gc_b3", 256, 1, h2, h2g, AF.Sigmoid, "gfv")

    ar_row = rows.perm(4)[0:1, 0:1]
    ar_g = P.dma("sync", ar_row, W["adaptive_range"].rearrange("(u n) -> u n", u=1))
    sc = rows.perm(2)
    s1 = P.op("vector", lambda e: e.tensor_scalar(sc[0:1, 0:1], ar_row, 0.6, None, ALU.mult),
              waits=(ar_g,), inc=True)
    s2 = P.op("scalar", lambda e: e.activation(sc[0:1, 1:2], gfc[0:1, 0:1], AF.Copy, scale=1.4, bias=-1.0),
              waits=(gfg,), inc=True)
    s3 = P.op("vector", lambda e: e.tensor_mul(sc[0:1, 1:2], sc[0:1, 1:2], ar_row), waits=(s2, s1), inc=True)
    s4 = P.op("vector", lambda e: e.tensor_scalar(sc[0:1, 1:2], sc[0:1, 1:2], 0.1, None, ALU.add), waits=(s3,), inc=True)
    cpl_ready = s4


    res_gates = []
    H = 1024
    for h0 in (0, H):
        out_f = [SF[i // 2][:, (i % 2) * H:(i % 2) * H + H] for i in range(ND)]
        ig = None
        for i in range(ND):
            ig = P.op("vector", lambda e, i=i, h0=h0: e.tensor_copy(out_f[i][:, 0:H], xn[i][:, h0:h0 + H]),
                      waits=(ln_g[-1],) if i == 0 else (), inc=True)
        prev_gate = ig
        for cyc in range(3):
            outb = [BB[i][:, 0:H] for i in range(ND)]
            t1 = [BB[i][:, H:2 * H] for i in range(ND)]
            t2 = [env["t2"][i] for i in range(4)]
            t3 = [env["t3"][i] for i in range(2)]
            cg = None
            for i in range(ND):
                cg = P.op("vector", lambda e, i=i: e.tensor_copy(outb[i][:, 0:H], out_f[i][:, 0:H]),
                          waits=(prev_gate,) if i == 0 else (), inc=True)
            eg1, eg2, eg3 = [], [], []
            linear_chunked(P, ws, W["cn_w1"], W["cn_b1"], D, D,
                           lambda ic, t0, tw: outb[ic][:, t0:t0 + tw], H, C.onesrow,
                           epi_act(P, lambda oc, t0, tw: t1[oc][:, t0:t0 + tw], AF.Gelu, gates=eg1),
                           ps, extra_waits=(cg,))
            linear_chunked(P, ws, W["cn_w2"], W["cn_b2"], D, 512,
                           lambda ic, t0, tw: t1[ic][:, t0:t0 + tw], H, C.onesrow,
                           epi_act(P, lambda oc, t0, tw: t2[oc][:, t0:t0 + tw], AF.Gelu, gates=eg2),
                           ps, extra_waits=(eg1[-1],))
            linear_chunked(P, ws, W["cn_w3"], W["cn_b3"], 512, 256,
                           lambda ic, t0, tw: t2[ic][:, t0:t0 + tw], H, C.onesrow,
                           epi_act(P, lambda oc, t0, tw: t3[oc][:, t0:t0 + tw], AF.Gelu, gates=eg3),
                           ps, extra_waits=(eg2[-1],))
            wt4 = ws.load(W["cn_w4"], 256, 1, bias_dram=W["cn_b4"])
            tf_row = env.setdefault("f0_tf", rows.seg(TC))
            cp_row = env.setdefault("f0_cp", rows.seg(TC))
            ncp_row = env.setdefault("f0_ncp", rows.seg(TC))
            f0r_prev = env.setdefault("f0r_prev", [None])
            cyc_gates = []
            tmp = env["tmp_tt"]
            for t0 in range(0, H, TC):
                w = [wt4.ready, eg3[-1]]
                if ps_small.gate is not None:
                    w.append(ps_small.gate)
                def spec(e, t0=t0, wt4=wt4):
                    for ic in range(2):
                        e.matmul(ps_small.t[0:1, 0:TC], wt4.lhsT(ic, 0, 1), t3[ic][:, t0:t0 + TC],
                                 start=(ic == 0), stop=False)
                    return e.matmul(ps_small.t[0:1, 0:TC], ws.brow(wt4, 0, 1),
                                    C.ones_row_bf[0:1, 0:TC], start=False, stop=True)
                mg = P.op("tensor", spec, waits=w, inc=True)
                sg = P.op("scalar", lambda e, t0=t0: e.activation(tf_row, ps_small.t[0:1, 0:TC], AF.Sigmoid),
                          waits=(mg, f0r_prev[0]), inc=True)
                ps_small.gate = sg
                tfg = P.op("vector", lambda e: e.tensor_scalar(cp_row, tf_row, sc[0:1, 0:1], sc[0:1, 1:2], ALU.mult, ALU.add),
                           waits=(sg, cpl_ready), inc=True)
                ng = P.op("vector", lambda e: e.tensor_scalar(ncp_row, cp_row, -1.0, None, ALU.mult),
                          waits=(tfg,), inc=True)
                bp = emit_bcast(P, ps_misc, 0, TC, cp_row, C, waits=(tfg,))
                bn = emit_bcast(P, ps_misc, TC, TC, ncp_row, C, waits=(ng,))
                f0r_prev[0] = bn
                last = None
                for (a, b, ca, cb) in ((0, 4, 0, TC), (1, 5, 0, TC), (2, 6, TC, 0), (3, 7, TC, 0)):
                    ga = P.op("vector", lambda e, b=b, t0=t0, ca=ca: e.tensor_mul(tmp[0][:, 0:TC], out_f[b][:, t0:t0 + TC], ps_misc.t[:, ca:ca + TC]),
                              waits=(bp, bn), inc=True)
                    gb = P.op("vector", lambda e, a=a, t0=t0, cb=cb: e.tensor_mul(tmp[1][:, 0:TC], out_f[a][:, t0:t0 + TC], ps_misc.t[:, cb:cb + TC]),
                              waits=(ga,), inc=True)
                    gc = P.op("vector", lambda e, a=a, t0=t0: e.tensor_add(out_f[a][:, t0:t0 + TC], out_f[a][:, t0:t0 + TC], tmp[0][:, 0:TC]),
                              waits=(gb,), inc=True)
                    last = P.op("vector", lambda e, b=b, t0=t0: e.tensor_add(out_f[b][:, t0:t0 + TC], out_f[b][:, t0:t0 + TC], tmp[1][:, 0:TC]),
                              waits=(gc,), inc=True)
                ps_misc.gate = last
                cyc_gates.append(last)
            wt4.release(cyc_gates[-1])
            prev_gate = cyc_gates[-1]
        for i in range(ND):
            g = P.op("vector", lambda e, i=i, h0=h0: e.scalar_tensor_tensor(x_tiles[i][:, h0:h0 + H], out_f[i][:, 0:H], 0.5,
                                                                            x_tiles[i][:, h0:h0 + H], ALU.mult, ALU.add),
                     waits=(prev_gate,) if i == 0 else (), inc=True)
            res_gates.append(g)
    return res_gates



def emit_A0(P, C, W, x_tiles, env, x_ready=(), upto=None):
    xn = env["BA"]
    scr = env["SMALL"]
    ps_misc = env["ps_misc"]
    ps = env["ps_pair"]
    ps_small = env["ps_small"]
    ps_acc = env["ps_acc"]
    ps_trv = env["ps_trv"]
    ws = env["ws"]
    rows = env["rows"]
    BA, BB, SF = env["BA"], env["BB"], env["SF"]
    SFB = [SF[i][:, 0:T].bitcast(BF16) for i in range(4)]
    ident = env["ident"]

    ln_g = emit_ln(P, C, rows, x_tiles, xn, scr, ps_misc, x_ready=x_ready)
    lng = ln_g[-1]

    tabs_g1 = P.dma("gpsimd", SFB[2][64:96, 0:2048], W["a0_tabs"][0:32, :], waits=tuple(x_ready))
    tabs_g2 = P.dma("gpsimd", SFB[2][64:96, 2048:4096], W["a0_tabs"][32:64, :], waits=tuple(x_ready))
    tabs_g = (tabs_g1, tabs_g2)
    perm = P.sbuf("permsb", [128, 32], BF16)
    perm_g = P.dma("gpsimd", perm[:, :], W["perm32"])

    gcol = P.sbuf("gkvcol", [128, 2], F32)
    gcol_gs = [P.dma("sync", gcol[:, c:c + 1], W["g_kv"][c * 128:(c + 1) * 128].rearrange("(p u) -> p u", u=1))
               for c in range(2)]

    kvc = [SF[0], SF[1]]
    kvg = []
    linear_chunked(P, ws, W["W_kv"], None, D, 256,
                   lambda ic, t0, tw: xn[ic][:, t0:t0 + tw], T, C.onesrow,
                   epi_act(P, lambda oc, t0, tw: kvc[oc][:, t0:t0 + tw], AF.Copy, gates=kvg),
                   ps, extra_waits=(lng,))
    sqg = []
    for i in range(2):
        sqg.append(P.op("scalar", lambda e, i=i: e.activation(BB[2 + i][:, 0:T], kvc[i][:, 0:T], AF.Square),
                        waits=(kvg[-1],), inc=True))
    kvcn = [BB[4], BB[5]]
    rms_gates = []
    srow = env["srow"][0:1, 0:TC]
    srow_prev = env["srow_prev"]
    for t0 in range(0, T, TC):
        w = [sqg[-1]]
        if ps_small.gate is not None:
            w.append(ps_small.gate)
        def spec(e, t0=t0):
            m = None
            for i in range(2):
                m = e.matmul(ps_small.t[0:1, 0:TC], C.ones128_bf[:, 0:1], BB[2 + i][:, t0:t0 + TC],
                             start=(i == 0), stop=(i == 1))
            return m
        sg = P.op("tensor", spec, waits=w, inc=True)
        rr = srow
        g1 = P.op("scalar", lambda e, rr=rr: e.activation(rr, ps_small.t[0:1, 0:TC], AF.Ln, bias=1e-6, scale=1.0 / 256),
                  waits=(sg, srow_prev[0]), inc=True)
        g2 = P.op("scalar", lambda e, rr=rr: e.activation(rr, rr, AF.Exp, scale=-0.5), waits=(g1,), inc=True)
        ps_small.gate = g2
        bg = emit_bcast(P, ps_misc, 0, TC, rr, C, waits=(g2,))
        srow_prev[0] = bg
        last = None
        for i in range(2):
            last = P.op("vector", lambda e, i=i, t0=t0: e.scalar_tensor_tensor(
                kvcn[i][:, t0:t0 + TC], kvc[i][:, t0:t0 + TC], gcol[:, i:i + 1], ps_misc.t[:, 0:TC],
                ALU.mult, ALU.mult), waits=(bg, gcol_gs[0], gcol_gs[1]), inc=True)
        ps_misc.gate = last
        rms_gates.append(last)
    kvcn_g = rms_gates[-1]

    kcon = P.dram("a0_kcon", [16 * 98, T], BF16)
    vcon = P.dram("a0_vcon", [16 * T, 65], BF16)
    kg_d = P.dram("a0_kg", [4 * 16 * 98, T], BF16)
    vg_d = P.dram("a0_vg", [4 * 16 * T, 65], BF16)
    kconv = kcon.rearrange("(h f) t -> h f t", h=16)
    vconv = vcon.rearrange("(h t) c -> h t c", h=16)

    kstr = BB[6]
    scr1 = BB[7]
    scr2 = SFB[3]
    vhead = SFB[3]
    vstr = env["small6"][0]
    kones_g = None
    for t0_ in range(0, T, TC):
        kones_g = P.dma("gpsimd", kstr[96:97, t0_:t0_ + TC], C.ones_row_bf[0:1, 0:TC], waits=(lng, kones_g))
    vones_g = P.op("vector", lambda e: e.memset(vstr[:, 64:65], 1.0), inc=True)
    knmax = rows.perm(16)
    km_g = P.op("vector", lambda e: e.memset(knmax, -1e30), inc=True)
    rowbuf = P.sbuf("rowbuf", [1, TC], BF16)

    con_gates = []
    prev_kstr_dma = [None]
    prev_vstr_dma = [None]
    prev_rowbuf_dma = [None]
    for h in range(16):
        wt_kd = ws.load(W["W_kdec"], 256, 64, col0=h * 64)
        wt_kr = ws.load(W["W_krope"], D, 32, col0=h * 32)
        ev_c = []
        for t0 in range(0, T, TC):
            pp = ps[(t0 // TC) % 2]
            w = [wt_kd.ready, wt_kr.ready, kvcn_g]
            if pp.gate is not None:
                w.append(pp.gate)
            def spec(e, t0=t0, wt_kd=wt_kd, wt_kr=wt_kr, pp=pp):
                for ic in range(2):
                    e.matmul(pp.t[0:64, 0:TC], wt_kd.lhsT(ic, 0, 64), kvcn[ic][:, t0:t0 + TC],
                             start=(ic == 0), stop=(ic == 1))
                m = None
                for ic in range(8):
                    m = e.matmul(pp.t[64:96, 0:TC], wt_kr.lhsT(ic, 0, 32), xn[ic][:, t0:t0 + TC],
                                 start=(ic == 0), stop=(ic == 7))
                return m
            mg = P.op("tensor", spec, waits=w, inc=True)
            g1 = P.op("scalar", lambda e, t0=t0, pp=pp: e.activation(kstr[0:64, t0:t0 + TC], pp.t[0:64, 0:TC], AF.Copy),
                      waits=(mg, prev_kstr_dma[0]), inc=True)
            g2 = P.op("scalar", lambda e, t0=t0, pp=pp: e.activation(scr1[64:96, t0:t0 + TC], pp.t[64:96, 0:TC], AF.Copy),
                      waits=(g1,), inc=True)
            pp.gate = g2
            ev_c.append(g2)
        wt_kd.release(P.TE.now)
        wt_kr.release(P.TE.now)
        rope_gates = []
        for t0 in range(0, T, TC):
            pp = ps[(t0 // TC) % 2]
            w = [ev_c[-1], perm_g]
            if pp.gate is not None:
                w.append(pp.gate)
            mg = P.op("tensor", lambda e, t0=t0, pp=pp: e.matmul(pp.t[64:96, 0:TC], perm[64:96, 0:32],
                                                                scr1[64:96, t0:t0 + TC], start=True, stop=True),
                      waits=w, inc=True)
            v1 = P.op("vector", lambda e, t0=t0, pp=pp: e.tensor_mul(scr2[64:96, t0:t0 + TC], pp.t[64:96, 0:TC],
                                                                    SFB[2][64:96, 2048 + t0:2048 + t0 + TC]),
                      waits=(mg,) + tabs_g, inc=True)
            v2 = P.op("vector", lambda e, t0=t0: e.tensor_mul(kstr[64:96, t0:t0 + TC], scr1[64:96, t0:t0 + TC],
                                                              SFB[2][64:96, t0:t0 + TC]),
                      waits=(v1,), inc=True)
            v3 = P.op("vector", lambda e, t0=t0: e.tensor_add(kstr[64:96, t0:t0 + TC], kstr[64:96, t0:t0 + TC],
                                                              scr2[64:96, t0:t0 + TC]),
                      waits=(v2,), inc=True)
            pp.gate = v1
            rope_gates.append(v3)
        sq_g2 = P.op("scalar", lambda e: e.activation(BB[2][0:96, 0:T], kstr[0:96, 0:T], AF.Square),
                     waits=(rope_gates[-1], kones_g), inc=True)
        for t0 in range(0, T, TC):
            w = [sq_g2]
            if ps_small.gate is not None:
                w.append(ps_small.gate)
            def spec(e, t0=t0):
                return e.matmul(ps_small.t[0:1, 0:TC], C.ones128_bf[0:96, 0:1],
                                BB[2][0:96, t0:t0 + TC], start=True, stop=True)
            sg = P.op("tensor", spec, waits=w, inc=True)
            seg = env["srow"][0:1, 0:TC]
            kseg = env["srow"][0:1, TC:TC + 1]
            g1 = P.op("scalar", lambda e, seg=seg: e.activation(seg, ps_small.t[0:1, 0:TC], AF.Copy),
                      waits=(sg, env["srow_prev"][0]), inc=True)
            ps_small.gate = g1
            g2a = P.op("vector", lambda e, seg=seg, kseg=kseg: e.reduce_max(kseg, seg, axis=AX.X),
                       waits=(g1, km_g), inc=True)
            g2b = P.op("vector", lambda e, h=h, kseg=kseg: e.tensor_max(knmax[0:1, h:h + 1], knmax[0:1, h:h + 1], kseg),
                       waits=(g2a,), inc=True)
            g3 = P.op("vector", lambda e, seg=seg: e.tensor_copy(rowbuf[0:1, 0:TC], seg),
                      waits=(g1, prev_rowbuf_dma[0]), inc=True)
            env["srow_prev"][0] = g3
            cg = P.dma("gpsimd", kconv[h, 97:98, t0:t0 + TC], rowbuf[0:1, 0:TC], waits=(g3,))
            prev_rowbuf_dma[0] = cg
            con_gates.append(cg)
        cg = P.dma("gpsimd", kconv[h, 0:97, :], kstr[0:97, 0:T], waits=(rope_gates[-1], kones_g, sq_g2))
        prev_kstr_dma[0] = cg
        con_gates.append(cg)
        wt_vd = ws.load(W["W_vdec"], 256, 64, col0=h * 64)
        ev_v = []
        for t0 in range(0, T, TC):
            pp = ps[(t0 // TC) % 2]
            w = [wt_vd.ready, kvcn_g]
            if pp.gate is not None:
                w.append(pp.gate)
            def spec(e, t0=t0, wt_vd=wt_vd, pp=pp):
                m = None
                for ic in range(2):
                    m = e.matmul(pp.t[0:64, 0:TC], wt_vd.lhsT(ic, 0, 64), kvcn[ic][:, t0:t0 + TC],
                                 start=(ic == 0), stop=(ic == 1))
                return m
            mg = P.op("tensor", spec, waits=w, inc=True)
            g1 = P.op("scalar", lambda e, t0=t0, pp=pp: e.activation(vhead[0:64, t0:t0 + TC], pp.t[0:64, 0:TC], AF.Copy),
                      waits=(mg, sq_g2), inc=True)
            pp.gate = g1
            ev_v.append(g1)
        wt_vd.release(P.TE.now)
        if upto == 124:
            con_gates.append(ev_v[-1])
            break
        for ch in range(16):
            w = [ev_v[ch // 4], vones_g]
            if ps_trv.gate is not None:
                w.append(ps_trv.gate)
            mg = P.op("tensor", lambda e, ch=ch: e.transpose(ps_trv.t[:, 0:64], vhead[0:64, ch * 128:(ch + 1) * 128],
                                                             ident[0:64, 0:64]), waits=w, inc=True)
            g1 = P.op("scalar", lambda e: e.activation(vstr[:, 0:64], ps_trv.t[:, 0:64], AF.Copy),
                      waits=(mg, prev_vstr_dma[0]), inc=True)
            ps_trv.gate = g1
            cg = P.dma("gpsimd", vconv[h, ch * 128:(ch + 1) * 128, :], vstr[:, 0:65], waits=(g1,))
            prev_vstr_dma[0] = cg
            con_gates.append(cg)

    cc1 = P.collective("AllGather", ALU.bypass, [kcon], [kg_d],
                       waits=tuple(P.dma_frontier("gpsimd")) + tuple(P.dma_frontier("sync")))
    cc2 = P.collective("AllGather", ALU.bypass, [vcon], [vg_d], waits=(cc1,))
    kmin = P.dram("a0_km_in", [1, 16], F32)
    kmout = P.dram("a0_km_out", [1, 16], F32)
    kmg1 = P.dma("gpsimd", kmin, knmax, waits=(P.VE.now,))
    cc3 = P.collective("AllReduce", ALU.max, [kmin], [kmout], waits=(kmg1,))
    kmg2 = P.dma("gpsimd", knmax, kmout, waits=(cc3,))
    kmn = rows.perm(16)
    kms = P.op("scalar", lambda e: e.activation(kmn, knmax, AF.Sqrt), waits=(kmg2,), inc=True)
    kms = P.op("vector", lambda e: e.tensor_scalar(kmn, kmn, -1.0, None, ALU.mult), waits=(kms,), inc=True)

    kgv = kg_d.rearrange("(r h f) t -> r h f t", r=4, h=16)
    vgv = vg_d.rearrange("(r h t) c -> r h t c", r=4, h=16)

    dbg2 = env.get("dbg2")
    if dbg2 is not None:
        gd0 = P.dma("gpsimd", dbg2[230:328, :], kconv[5, :, :], waits=(cc1,))
        gd0b = P.dma("gpsimd", dbg2[330:428, :], kconv[0, :, :], waits=(cc1,))
        gd0c = P.dma("gpsimd", dbg2[430:494, :], kvcn[0][0:64, 0:T], waits=(cc1,))
        gd0e = P.dma("gpsimd", dbg2[494:510, :], xn[0][0:16, 0:T], waits=(kvcn_g,))
        env["dbg2_gates2"] = (gd0e,)
        gd1 = P.dma("gpsimd", dbg2[0:98, :], kgv[3, 5, :, :], waits=(cc1,))
        gd2 = P.dma("gpsimd", dbg2[100:228, 0:65], vg_d[3 * 16 * T + 5 * T: 3 * 16 * T + 5 * T + 128, 0:65], waits=(cc2,))
        env["dbg2_gates"] = (gd1, gd2)
    mask_g1 = P.dma("gpsimd", SFB[0][:, :], W["a0_masks"][:, 0:4096], waits=(kvg[-1], sqg[-1]))
    mask_g2 = P.dma("gpsimd", SFB[1][:, :], W["a0_masks"][:, 4096:8192], waits=(rms_gates[-1],))
    def mask_ap(r, w):
        off = (r * 4 + w) * TC
        buf = SFB[off // 4096]
        return buf[:, off % 4096: off % 4096 + TC]

    qc = [BB[i] for i in range(4)]
    qcg = []
    linear_chunked(P, ws, W["W_q"], None, D, 512,
                   lambda ic, t0, tw: xn[ic][:, t0:t0 + tw], T, C.onesrow,
                   epi_act(P, lambda oc, t0, tw: qc[oc][:, t0:t0 + tw], AF.Copy, gates=qcg),
                   ps, extra_waits=(lng, P.GE.now))
    qc_g = qcg[-1]

    attnq = [env["BA"][6][:, i * TC:(i + 1) * TC] for i in range(4)] +             [env["BA"][7][:, i * TC:(i + 1) * TC] for i in range(4)]
    qstrip = lambda h: BA[h // 4][:, (h % 4) * TC:(h % 4) * TC + TC]
    nbuf = P.sbuf("negb", [1, TC], BF16)
    lr_prev = [None]
    ktile = [env["small6"][1][:, 0:TC], env["small6"][1][:, TC:2 * TC]]
    vtile = [env["small6"][2][:, 0:72], env["small6"][2][:, 72:144]]
    ptile = [env["small6"][3][:, 0:TC], env["small6"][3][:, TC:2 * TC]]
    lb = env["small6"][4][:, 0:TC]

    res_gates = []
    for J in range(4):
        qcols = (J * TC, TC)
        qs_g = {}
        for h in range(16):
            wt_qd = ws.load(W["W_qdec"], 512, 64, col0=h * 64)
            wt_qr = ws.load(W["W_qrope"], 512, 32, col0=h * 32)
            pp = ps[h % 2]
            w = [wt_qd.ready, wt_qr.ready, qc_g, cc2] + list(env.get("dbg2_gates2", ()))
            if pp.gate is not None:
                w.append(pp.gate)
            def spec(e, h=h, wt_qd=wt_qd, wt_qr=wt_qr, pp=pp, J=J):
                for ic in range(4):
                    e.matmul(pp.t[0:64, 0:TC], wt_qd.lhsT(ic, 0, 64), qc[ic][:, J * TC:(J + 1) * TC],
                             start=(ic == 0), stop=(ic == 3))
                m = None
                for ic in range(4):
                    m = e.matmul(pp.t[64:96, 0:TC], wt_qr.lhsT(ic, 0, 32), qc[ic][:, J * TC:(J + 1) * TC],
                                 start=(ic == 0), stop=(ic == 3))
                return m
            mg = P.op("tensor", spec, waits=w, inc=True)
            wt_qd.release(P.TE.now)
            wt_qr.release(P.TE.now)
            g1 = P.op("scalar", lambda e, h=h, pp=pp: e.activation(qstrip(h)[0:64, :], pp.t[0:64, 0:TC], AF.Copy),
                      waits=(mg,), inc=True)
            g2 = P.op("scalar", lambda e, h=h, pp=pp: e.activation(scr1[64:96, 0:TC], pp.t[64:96, 0:TC], AF.Copy),
                      waits=(g1,), inc=True)
            pp.gate = g2
            mg2 = P.op("tensor", lambda e, pp=pp: e.matmul(pp.t[64:96, 0:TC], perm[64:96, 0:32],
                                                           scr1[64:96, 0:TC], start=True, stop=True),
                       waits=(g2,), inc=True)
            v1 = P.op("vector", lambda e, pp=pp, J=J: e.tensor_mul(scr2[64:96, 0:TC], pp.t[64:96, 0:TC],
                                                                   SFB[2][64:96, 2048 + J * TC:2048 + (J + 1) * TC]),
                      waits=(mg2,), inc=True)
            v2 = P.op("vector", lambda e, h=h, J=J: e.tensor_mul(qstrip(h)[64:96, :], scr1[64:96, 0:TC],
                                                                 SFB[2][64:96, J * TC:(J + 1) * TC]),
                      waits=(v1,), inc=True)
            v3 = P.op("vector", lambda e, h=h: e.tensor_add(qstrip(h)[64:96, :], qstrip(h)[64:96, :],
                                                            scr2[64:96, 0:TC]),
                      waits=(v2,), inc=True)
            pp.gate = v1
            sq1 = P.op("scalar", lambda e, h=h: e.activation(scr2[0:96, 2048:2048 + TC], qstrip(h)[0:96, :], AF.Square),
                       waits=(v3,), inc=True)
            w2 = [sq1]
            if ps_small.gate is not None:
                w2.append(ps_small.gate)
            sg = P.op("tensor", lambda e: e.matmul(ps_small.t[0:1, 0:TC], C.ones128_bf[0:96, 0:1],
                                                   scr2[0:96, 2048:2048 + TC], start=True, stop=True),
                      waits=w2, inc=True)
            seg = env["srow"][0:1, 0:TC]
            b1 = P.op("scalar", lambda e, seg=seg: e.activation(seg, ps_small.t[0:1, 0:TC], AF.Sqrt),
                      waits=(sg, env["srow_prev"][0]), inc=True)
            ps_small.gate = b1
            b2 = P.op("vector", lambda e, seg=seg, h=h: e.tensor_scalar(seg, seg, kmn[0:1, h:h + 1], 96.0 ** -0.5, ALU.mult, ALU.mult),
                      waits=(b1, kms), inc=True)
            b3 = P.op("vector", lambda e, seg=seg: e.tensor_copy(nbuf[0:1, 0:TC], seg),
                      waits=(b2, qs_g.get(h - 1)), inc=True)
            env["srow_prev"][0] = b3
            qs_g[h] = P.dma("gpsimd", BA[h // 4][96:97, (h % 4) * TC:(h % 4) * TC + TC], nbuf[0:1, 0:TC], waits=(b3,))
        wo_in_g = []
        kt_free = [None, None]
        vt_free = [None, None]
        for h in range(16):
            nkt = 4 * (J + 1)
            for m_i in range(nkt):
                u, r = m_i // 4, m_i % 4
                kt = ktile[m_i % 2]
                vt = vtile[m_i % 2]
                kd = P.dma("sync", kt[0:97, :], kgv[r, h, 0:97, u * TC:(u + 1) * TC],
                           waits=(cc1, qs_g[h], kt_free[m_i % 2]) if m_i == 0 else (cc1, kt_free[m_i % 2]))
                pvg = None
                for w_ in range(4):
                    vd = P.dma("sync", vt[:, 0:65], vgv[r, h, u * TC + w_ * 128:u * TC + (w_ + 1) * 128, :],
                               waits=(cc2, vt_free[m_i % 2] if w_ == 0 else pvg))
                    pp = ps[(m_i * 4 + w_) % 2]
                    ww = [kd, qs_g[h]]
                    if pp.gate is not None:
                        ww.append(pp.gate)
                    mg = P.op("tensor", lambda e, h=h, w_=w_, kt=kt, pp=pp: e.matmul(
                        pp.t[:, 0:TC], kt[0:97, w_ * 128:(w_ + 1) * 128], BA[h // 4][0:97, (h % 4) * TC:(h % 4 + 1) * TC],
                        start=True, stop=True), waits=ww, inc=True)
                    pt = ptile[(m_i * 4 + w_) % 2]
                    eg = P.op("scalar", lambda e, pt=pt, pp=pp: e.activation(pt, pp.t[:, 0:TC], AF.Exp, scale=96.0 ** -0.5),
                              waits=(mg,), inc=True)
                    pp.gate = eg
                    if u == J:
                        eg = P.op("vector", lambda e, pt=pt, r=r, w_=w_: e.tensor_mul(pt, pt, mask_ap(r, w_)),
                                  waits=(eg, mask_g1, mask_g2), inc=True)
                    w3 = [eg, vd]
                    if m_i == 0 and w_ == 0:
                        w3.append(ps_acc.gate)
                    pvg = P.op("tensor", lambda e, vt=vt, pt=pt, first=(m_i == 0 and w_ == 0),
                               last=(m_i == nkt - 1 and w_ == 3): e.matmul(
                        ps_acc.t[0:65, 0:TC], vt[:, 0:65], pt, start=first, stop=last), waits=w3, inc=True)
                    vt_free[m_i % 2] = pvg
                kt_free[m_i % 2] = pvg
            lr = rows.t[64:65, 0:TC]
            g1 = P.op("vector", lambda e, lr=lr: e.reciprocal(lr, ps_acc.t[64:65, 0:TC]),
                      waits=(pvg, lr_prev[0]), inc=True)
            bg = P.op("tensor", lambda e, lr=lr: e.matmul(ps_misc.t[0:64, 0:TC], C.ones_col_f32[64:65, 0:64], lr,
                                                          start=True, stop=True),
                      waits=((g1,) + ((ps_misc.gate,) if ps_misc.gate else ())), inc=True)
            lr_prev[0] = bg
            g2 = P.op("scalar", lambda e: e.activation(lb[0:64, 0:TC], ps_misc.t[0:64, 0:TC], AF.Copy),
                      waits=(bg,), inc=True)
            ps_misc.gate = g2
            dst = attnq[h // 2][(h % 2) * 64:(h % 2) * 64 + 64, :]
            g3 = P.op("vector", lambda e, dst=dst: e.tensor_mul(dst, ps_acc.t[0:64, 0:TC], lb[0:64, 0:TC]),
                      waits=(g2,), inc=True)
            ps_acc.gate = g3
            wo_in_g.append(g3)
        def xin_att(ic, t0, tw):
            return attnq[ic][:, t0:t0 + tw]
        rg = []
        def epi_res(P_, pp, oc, t0, tw, mmg):
            g = P.op("vector", lambda e, pp=pp, oc=oc, t0=t0, tw=tw, J=J: e.tensor_add(
                x_tiles[oc][:, J * TC + t0:J * TC + t0 + tw], x_tiles[oc][:, J * TC + t0:J * TC + t0 + tw],
                pp.t[:, 0:tw]), waits=(mmg,), inc=True)
            pp.gate = g
            rg.append(g)
        linear_chunked(P, ws, W["W_o"], None, D, D, xin_att, TC, C.onesrow, epi_res, ps,
                       extra_waits=(wo_in_g[-1],))
        res_gates.append(rg[-1])
    return res_gates



def emit_F1(P, C, W, x_tiles, env, x_ready=()):
    BA, BB, SF = env["BA"], env["BB"], env["SF"]
    SFB = [SF[i][:, 0:T].bitcast(BF16) for i in range(4)]
    scr = env["SMALL"]
    small = env["small6"]
    rows = env["rows"]
    ws = env["ws"]
    ps_misc, ps, ps_small, ps_acc, ps_trv = env["ps_misc"], env["ps_pair"], env["ps_small"], env["ps_acc"], env["ps_trv"]
    ident = env["ident"]

    if "f1_consts" not in env:
        cc = {}
        small_ = env["small6"]
        cc["sel"] = P.sbuf("f1selsb", [64, 16], BF16)
        cc["mask"] = small_[3][:, 0:1024]
        cc["mask0"] = small_[4][:, 0:1024]
        cc["bo"] = P.sbuf("f1bosb", [128, 32], BF16)
        cc["bot"] = P.sbuf("f1botsb", [8, 512], BF16)
        cc["perm"] = P.sbuf("f1permsb", [128, 128], BF16)
        cc["qsc"] = P.sbuf("f1qscsb", [128, 2], F32)
        cc["ksc"] = P.sbuf("f1kscsb", [128, 2], F32)
        cc["lng"] = P.sbuf("f1lngsb", [128, 16], F32)
        cc["lnb"] = P.sbuf("f1lnbsb", [128, 16], F32)
        cc["nrm"] = env["rows"].t[32:40, 0:TC]
        cc["mcol"] = P.sbuf("f1mcol", [128, 2], F32)
        cc["tab"] = [env["SMALL"][0], env["SMALL"][1]]
        cc["etok"] = small_[0][0:16, 0:1024]
        cc["egt"] = small_[1][0:64, 0:1024]
        cc["htok"] = small_[2][0:16, 0:1024]
        cc["hfm"] = small_[3][0:128, 520:584]
        cc["ecs"] = small_[3][0:128, 600:616]
        gs = []
        gs.append(P.dma("sync", cc["sel"][:, :], W["f1_sel"]))
        gs.append(P.dma("sync", cc["bo"][:, :], W["f1_bo"]))
        gs.append(P.dma("sync", cc["bot"][:, :], W["f1_bot"]))
        gs.append(P.dma("sync", cc["perm"][:, :], W["f1_perm"]))
        for l in range(2):
            gs.append(P.dma("sync", cc["qsc"][:, l:l + 1], W["f1_qsc"][l].rearrange("(p u) -> p u", u=1)))
            gs.append(P.dma("sync", cc["ksc"][:, l:l + 1], W["f1_ksc"][l].rearrange("(p u) -> p u", u=1)))
            gs.append(P.dma("sync", cc["lng"][:, l * 8:(l + 1) * 8], W["f1_lngt"][l]))
            gs.append(P.dma("sync", cc["lnb"][:, l * 8:(l + 1) * 8], W["f1_lnbt"][l]))
        cc["ready"] = tuple(gs)
        env["f1_consts"] = cc
    cc = env["f1_consts"]
    ccready = cc["ready"]

    if "f1_xspill" not in env:
        env["f1_xspill"] = P.dram("f1xspill", [D, T], F32)
        env["f1_ecd"] = P.dram("f1ec", [16, 1024], BF16)
        env["f1_egd"] = P.dram("f1eg", [64, 1024], BF16)
        env["f1_kcon"] = P.dram("f1kcon", [512, T], BF16)
        env["f1_vcon"] = P.dram("f1vcon", [T, 520], BF16)
        env["f1_kgd"] = P.dram("f1kg", [2048, T], BF16)
        env["f1_vgd"] = P.dram("f1vg", [4 * T, 520], BF16)
    xsp, ecd, egd = env["f1_xspill"], env["f1_ecd"], env["f1_egd"]
    kcon, vcon, kgd, vgd = env["f1_kcon"], env["f1_vcon"], env["f1_kgd"], env["f1_vgd"]
    kgv = kgd.rearrange("(r f) t -> r f t", r=4)
    vgv = vgd.rearrange("(r t) c -> r t c", r=4)

    sp_g = [P.dma("gpsimd", xsp[i * 128:(i + 1) * 128, :], x_tiles[i][:, 0:T],
                  waits=tuple(x_ready) if i == 0 else ()) for i in range(ND)]
    ln_g = emit_ln_inplace(P, C, rows, x_tiles, BA, scr, ps_misc, x_ready=tuple(sp_g))
    cur_g = ln_g[-1]

    def endcol_gather(curn_tiles, ready):
        evg = None
        for i in range(ND):
            g1 = P.op("vector", lambda e, i=i: e.tensor_copy(
                cc["ecs"].rearrange("p (c u) -> p c u", u=1),
                curn_tiles[i][:, 0:T].rearrange("p (c w) -> p c w", w=128)[:, :, 127:128]),
                waits=tuple(ready) + ((evg,) if evg else ()), inc=True)
            g2 = P.op("tensor", lambda e: e.transpose(ps_trv.t[0:16, 0:128], cc["ecs"], ident[:, :]),
                      waits=(g1,) + ((ps_trv.gate,) if ps_trv.gate else ()), inc=True)
            evg = P.op("scalar", lambda e, i=i: e.activation(cc["etok"][:, i * 128:(i + 1) * 128], ps_trv.t[0:16, 0:128], AF.Copy),
                       waits=(g2,), inc=True)
            ps_trv.gate = evg
        d1 = P.dma("gpsimd", ecd, cc["etok"], waits=(evg,))
        ccg = P.collective("AllGather", ALU.bypass, [ecd], [egd], waits=(d1,))
        d2 = P.dma("gpsimd", cc["egt"], egd, waits=(ccg,))
        pe = []
        for half in range(2):
            m1 = P.op("tensor", lambda e, half=half: e.matmul(ps_small.t[0:16, 0:TC], cc["sel"][:, :],
                                                              cc["egt"][:, half * TC:(half + 1) * TC], start=True, stop=True),
                      waits=(d2, ccready[0]) + ((ps_small.gate,) if ps_small.gate else ()), inc=True)
            e1 = P.op("scalar", lambda e, half=half: e.activation(cc["htok"][:, half * TC:(half + 1) * TC],
                                                                  ps_small.t[0:16, 0:TC], AF.Copy), waits=(m1,), inc=True)
            ps_small.gate = e1
            pe.append(e1)
        hg = None
        for i in range(4, 8):
            g2 = P.op("tensor", lambda e, i=i: e.transpose(ps_trv.t[0:128, 0:16], cc["htok"][:, i * 128:(i + 1) * 128], ident[0:16, 0:16]),
                      waits=tuple(pe) + ((ps_trv.gate,) if ps_trv.gate else ()), inc=True)
            hg = P.op("scalar", lambda e, i=i: e.activation(cc["hfm"][:, (i - 4) * 16:(i - 3) * 16], ps_trv.t[0:128, 0:16], AF.Copy),
                      waits=(g2,), inc=True)
            ps_trv.gate = hg
        return cc["hfm"], hg

    def hshift(curn_tiles, dst_tiles, halo_fm, halo_g, ready):
        gates = []
        for i in range(4):
            g1 = P.op("vector", lambda e, i=i: e.tensor_copy(dst_tiles[i][:, 1:T], curn_tiles[4 + i][:, 0:T - 1]),
                      waits=tuple(ready) + ((halo_g,) if i == 0 else ()), inc=True)
            g2 = P.op("vector", lambda e, i=i: e.tensor_copy(
                dst_tiles[i][:, 0:T].rearrange("p (c w) -> p c w", w=128)[:, :, 0:1],
                halo_fm[:, i * 16:(i + 1) * 16].rearrange("p (c u) -> p c u", u=1)),
                waits=(g1, halo_g), inc=True)
            gates.append(g2)
        return gates

    res_gates = []
    for l in range(2):
        curn = BA
        halo_fm, halo_g = endcol_gather(curn, (cur_g,))
        hs_g = hshift(curn, [BB[i] for i in range(4)], halo_fm, halo_g, (cur_g,))
        qt = [SFB[2][:, 0:T], SFB[2][:, T:2 * T], SFB[3][:, 0:T], SFB[3][:, T:2 * T]]
        kt = [BB[4][:, 0:T], BB[5][:, 0:T], BB[6][:, 0:T], BB[7][:, 0:T]]
        vt = [SFB[0][:, 0:T], SFB[0][:, T:2 * T], SFB[1][:, 0:T], SFB[1][:, T:2 * T]]
        qkv_dst = qt + kt + vt

        def xin_h(ic, t0, tw):
            return (curn[ic][:, t0:t0 + tw] if ic < 4 else BB[ic - 4][:, t0:t0 + tw])
        qkvg = []
        step = (ws.max_cols // 8) // 128 * 128
        first = True
        for col0 in range(0, 1536, step):
            wt = ws.load(W["wqkv"][l], D, step, col0=col0)
            def epi(P_, pp, oc, t0, tw, mmg, col0=col0):
                dst = qkv_dst[col0 // 128 + oc]
                g = P.op("scalar", lambda e, pp=pp, dst=dst, t0=t0, tw=tw: e.activation(dst[:, t0:t0 + tw], pp.t[:, 0:tw], AF.Copy),
                         waits=(mmg,), inc=True)
                pp.gate = g
                qkvg.append(g)
            emit_linear(P, wt, xin_h, step, T, [], epi, ps,
                        extra_waits=tuple(hs_g) + (cur_g,) if first else ())
            first = False
            wt.release(P.TE.now)
        qkv_done = qkvg[-1]

        def lnorm_rope(tiles, sc_col, extra_mul, tab_q, done_gate):
            gates = []
            sqs = small[4][:, 0:TC]
            for t0 in range(0, T, TC):
                prevg = mg = None
                for ic in range(4):
                    w0 = [done_gate] if ic == 0 else [mg]
                    if ic == 0 and gates:
                        w0.append(gates[-1])
                    sqg = P.op("scalar", lambda e, ic=ic, t0=t0: e.activation(sqs, tiles[ic][:, t0:t0 + TC], AF.Square),
                               waits=tuple(w0), inc=True)
                    w1 = [sqg]
                    if ic == 0 and ps_small.gate is not None:
                        w1.append(ps_small.gate)
                    mg = P.op("tensor", lambda e, ic=ic: e.matmul(ps_small.t[0:8, 0:TC], cc["bo"][:, ic * 8:(ic + 1) * 8],
                                                                 sqs, start=(ic == 0), stop=(ic == 3)),
                              waits=tuple(w1), inc=True)
                n1 = P.op("scalar", lambda e: e.activation(cc["nrm"][0:8, 0:TC], ps_small.t[0:8, 0:TC], AF.Sqrt),
                          waits=(mg,), inc=True)
                ps_small.gate = n1
                n2 = P.op("vector", lambda e: e.tensor_scalar(cc["nrm"][0:8, 0:TC], cc["nrm"][0:8, 0:TC], 1e-12, None, ALU.max),
                          waits=(n1,), inc=True)
                n3 = P.op("vector", lambda e: e.reciprocal(cc["nrm"][0:8, 0:TC], cc["nrm"][0:8, 0:TC]), waits=(n2,), inc=True)
                if extra_mul != 1.0:
                    n3 = P.op("vector", lambda e, m=extra_mul: e.tensor_scalar(cc["nrm"][0:8, 0:TC], cc["nrm"][0:8, 0:TC], m, None, ALU.mult),
                              waits=(n3,), inc=True)
                nb = env["SMALL"][2][0:8, 0:TC]
                n3 = P.op("vector", lambda e, nb=nb: e.tensor_copy(nb, cc["nrm"][0:8, 0:TC]), waits=(n3,), inc=True)
                for ic in range(4):
                    bg = P.op("tensor", lambda e, ic=ic, nb=nb: e.matmul(ps_misc.t[:, 0:TC], cc["bot"][:, ic * 128:(ic + 1) * 128],
                                                                        nb, start=True, stop=True),
                              waits=(n3,) + ((ps_misc.gate,) if ps_misc.gate else ()), inc=True)
                    ps_misc.gate = None
                    a1 = P.op("vector", lambda e, ic=ic, t0=t0: e.scalar_tensor_tensor(
                        tiles[ic][:, t0:t0 + TC], tiles[ic][:, t0:t0 + TC], sc_col, ps_misc.t[:, 0:TC], ALU.mult, ALU.mult),
                        waits=(bg, ccready[-1]), inc=True)
                    tw0 = (done_gate, gates[-1] if gates else None)
                    tcg1 = P.dma("sync", cc["tab"][0][:, :], W["f1_tabs"][tab_q * 1024 + ic * 128: tab_q * 1024 + (ic + 1) * 128, t0:t0 + TC], waits=tw0)
                    tcg2 = P.dma("sync", cc["tab"][1][:, :], W["f1_tabs"][tab_q * 1024 + 512 + ic * 128: tab_q * 1024 + 512 + (ic + 1) * 128, t0:t0 + TC], waits=tw0)
                    rg = P.op("tensor", lambda e, ic=ic, t0=t0: e.matmul(ps_misc.t[:, TC:2 * TC], cc["perm"][:, :],
                                                                        tiles[ic][:, t0:t0 + TC], start=True, stop=True),
                              waits=(a1,), inc=True)
                    v1 = P.op("vector", lambda e: e.tensor_mul(small[5][:, 0:TC], ps_misc.t[:, TC:2 * TC], cc["tab"][1][:, :]),
                              waits=(rg, tcg2, tcg1), inc=True)
                    v2 = P.op("vector", lambda e, ic=ic, t0=t0: e.tensor_mul(tiles[ic][:, t0:t0 + TC], tiles[ic][:, t0:t0 + TC], cc["tab"][0][:, :]),
                              waits=(rg, v1), inc=True)
                    v3 = P.op("vector", lambda e, ic=ic, t0=t0: e.tensor_add(tiles[ic][:, t0:t0 + TC], tiles[ic][:, t0:t0 + TC], small[5][:, 0:TC]),
                              waits=(v2,), inc=True)
                    ps_misc.gate = v3
                    gates.append(v3)
            return gates

        qg = lnorm_rope(qt, cc["qsc"][:, l:l + 1], 8.0, 0, qkv_done)
        kg = lnorm_rope(kt, cc["ksc"][:, l:l + 1], 1.0, 1, qg[-1])

        con_g = []
        for i in range(4):
            con_g.append(P.dma("gpsimd", kcon[i * 128:(i + 1) * 128, :], kt[i], waits=(kg[-1],)))
        vstr = small[3][:, 0:520]
        vz = P.op("vector", lambda e: e.memset(vstr.rearrange("p (h c) -> p h c", c=65)[:, :, 64:65], 1.0),
                  waits=(qg[-1],), inc=True)
        pvd = [None]
        for chk in range(16):
            evg = None
            for hh in range(8):
                g2 = P.op("tensor", lambda e, chk=chk, hh=hh, b=(hh % 2) * 64: e.transpose(
                    ps_trv.t[:, hh * 64:(hh + 1) * 64],
                    vt[hh // 2][b:b + 64, chk * 128:(chk + 1) * 128], ident[b:b + 64, b:b + 64]),
                    waits=(qkv_done,) + ((ps_trv.gate,) if ps_trv.gate else ()), inc=True)
                evg = P.op("scalar", lambda e, hh=hh: e.activation(vstr[:, hh * 65:hh * 65 + 64], ps_trv.t[:, hh * 64:(hh + 1) * 64], AF.Copy),
                           waits=(g2, vz, pvd[0]), inc=True)
                ps_trv.gate = evg
            d = P.dma("gpsimd", vcon[chk * 128:(chk + 1) * 128, :], vstr, waits=(evg,))
            pvd[0] = d
            con_g.append(d)
        ccg = P.collective("AllGather", ALU.bypass, [kcon], [kgd], waits=tuple(P.dma_frontier("gpsimd")))
        ccg2 = P.collective("AllGather", ALU.bypass, [vcon], [vgd], waits=(ccg,))
        mk_g1 = P.dma("sync", cc["mask"], W["f1_mask"], waits=(pvd[0],))
        mk_g2 = P.dma("sync", cc["mask0"], W["f1_mask0"], waits=(qg[-1],))

        kwin = [BB[7][:, 0:1024], BB[7][:, 1024:2048], small[0][:, 0:1024], small[1][:, 0:1024]]
        vwin = [BB[4][:, 0:520], BB[4][:, 520:1040], BB[4][:, 1040:1560],
                BB[5][:, 0:520], BB[5][:, 520:1040], BB[5][:, 1040:1560],
                BB[6][:, 0:520], BB[6][:, 520:1040]]
        ptil = small[2][:, 0:1024]
        negm = cc["mcol"]
        o_done = []
        kw_free = [None]
        vw_free = [None]
        lrp = [None]
        for j in range(NCH):
            c0 = max(j - 1, 0) * 128
            kd = []
            for i in range(4):
                kd.append(P.dma("sync", kwin[i].rearrange("p (s c) -> p s c", s=4),
                                kgv[:, i * 128:(i + 1) * 128, c0:c0 + 256].rearrange("s p c -> p s c"),
                                waits=(ccg, kg[-1], kw_free[0])))
            vd = []
            for s4 in range(4):
                for c2 in range(2):
                    vd.append(P.dma("sync", vwin[s4 * 2 + c2],
                                    vgv[s4, c0 + c2 * 128:c0 + (c2 + 1) * 128, :],
                                    waits=(ccg2, vw_free[0])))
            kw_free[0] = None
            pvg = None
            for h in range(8):
                w = [qg[-1], kd[h // 2], vd[-1]]
                if ps_misc.gate is not None:
                    w.append(ps_misc.gate)
                def smspec(e, h=h, j=j):
                    q_ap = qt[h // 2][(h % 2) * 64:(h % 2) * 64 + 64, j * 128:(j + 1) * 128]
                    e.matmul(ps_misc.t[:, 0:TC], q_ap, kwin[h // 2][(h % 2) * 64:(h % 2) * 64 + 64, 0:TC],
                             start=True, stop=True)
                    return e.matmul(ps_misc.t[:, TC:2 * TC], q_ap, kwin[h // 2][(h % 2) * 64:(h % 2) * 64 + 64, TC:1024],
                                    start=True, stop=True)
                smg = P.op("tensor", smspec, waits=w, inc=True)
                mk = P.op("vector", lambda e, j=j: e.tensor_add(ps_misc.t[:, 0:1024], ps_misc.t[:, 0:1024],
                                                                cc["mask0"] if j == 0 else cc["mask"]),
                          waits=(smg, mk_g1, mk_g2), inc=True)
                mx = P.op("vector", lambda e: e.reduce_max(negm[:, 0:1], ps_misc.t[:, 0:1024], axis=AX.X),
                          waits=(mk,), inc=True)
                ng = P.op("vector", lambda e: e.tensor_scalar(negm[:, 1:2], negm[:, 0:1], -1.0, None, ALU.mult),
                          waits=(mx,), inc=True)
                ex = P.op("scalar", lambda e: e.activation(ptil, ps_misc.t[:, 0:1024], AF.Exp, bias=negm[:, 1:2]),
                          waits=(ng, pvg), inc=True)
                ps_misc.gate = ex
                trg = None
                for tix in range(8):
                    trg = P.op("tensor", lambda e, tix=tix: e.transpose(ps_trv.t[:, tix * 128:(tix + 1) * 128],
                                                                       ptil[:, tix * 128:(tix + 1) * 128], ident[:, :]),
                               waits=(ex,) + ((ps_trv.gate,) if tix == 0 and ps_trv.gate else ()), inc=True)
                pev = P.op("scalar", lambda e: e.activation(ptil, ps_trv.t[:, 0:1024], AF.Copy),
                           waits=(trg,), inc=True)
                ps_trv.gate = pev
                for tix in range(8):
                    w2 = [pev, vd[tix]]
                    if tix == 0 and ps_acc.gate is not None:
                        w2.append(ps_acc.gate)
                    pvg = P.op("tensor", lambda e, tix=tix, h=h: e.matmul(
                        ps_acc.t[0:65, 0:128], vwin[tix][:, h * 65:(h + 1) * 65], ptil[:, tix * 128:(tix + 1) * 128],
                        start=(tix == 0), stop=(tix == 7)), waits=tuple(w2), inc=True)
                if h == 7:
                    kw_free[0] = pvg
                    vw_free[0] = pvg
                lr = rows.t[64:65, 0:128]
                g1 = P.op("vector", lambda e, lr=lr: e.reciprocal(lr, ps_acc.t[64:65, 0:128]), waits=(pvg, lrp[0]), inc=True)
                bg = P.op("tensor", lambda e, lr=lr: e.matmul(ps_small.t[0:64, 0:128], C.ones_col_f32[64:65, 0:64], lr,
                                                             start=True, stop=True),
                          waits=((g1,) + ((ps_small.gate,) if ps_small.gate else ())), inc=True)
                lrp[0] = bg
                g2 = P.op("scalar", lambda e: e.activation(small[5][0:64, 512:640], ps_small.t[0:64, 0:128], AF.Copy),
                          waits=(bg,), inc=True)
                ps_small.gate = g2
                dst = BA[h // 2][(h % 2) * 64:(h % 2) * 64 + 64, j * 128:(j + 1) * 128]
                g3 = P.op("vector", lambda e, dst=dst: e.tensor_mul(dst, ps_acc.t[0:64, 0:128], small[5][0:64, 512:640]),
                          waits=(g2,), inc=True)
                ps_acc.gate = g3
                o_done.append(g3)

        wog = []
        step = (ws.max_cols // 4) // 128 * 128
        first = True
        for col0 in range(0, D, step):
            wt = ws.load(W["wo"][l], 512, step, col0=col0)
            def epi(P_, pp, oc, t0, tw, mmg, col0=col0):
                g = P.op("vector", lambda e, pp=pp, oc=col0 // 128 + oc, t0=t0, tw=tw: e.tensor_add(
                    x_tiles[oc][:, t0:t0 + tw], x_tiles[oc][:, t0:t0 + tw], pp.t[:, 0:tw]), waits=(mmg,), inc=True)
                pp.gate = g
                wog.append(g)
            emit_linear(P, wt, lambda ic, t0, tw: BA[ic][:, t0:t0 + tw], step, T, [], epi, ps,
                        extra_waits=(o_done[-1],) if first else ())
            first = False
            wt.release(P.TE.now)
        x1_done = wog[-1]

        c2g = []
        for i in range(ND):
            c2g.append(P.op("vector", lambda e, i=i: e.tensor_copy(BA[i][:, 0:T], x_tiles[i][:, 0:T]),
                            waits=(x1_done,) if i == 0 else (), inc=True))
        halo2, halo2_g = endcol_gather(BA, (c2g[-1],))
        hs2 = hshift(BA, [BB[i] for i in range(4)], halo2, halo2_g, (c2g[-1],))
        hten = [BA[0][:, 0:T], BA[1][:, 0:T], BA[2][:, 0:T], BA[3][:, 0:T],
                BB[0][:, 0:T], BB[1][:, 0:T], BB[2][:, 0:T], BB[3][:, 0:T]]
        hn = [BB[4][:, 0:T], BB[5][:, 0:T], BB[6][:, 0:T], BB[7][:, 0:T],
              SFB[0][:, 0:T], SFB[0][:, T:2 * T], SFB[1][:, 0:T], SFB[1][:, T:2 * T]]
        mu = rows.seg(TC)
        m2r = rows.seg(TC)
        rsr = rows.seg(TC)
        hng = []
        for t0 in range(0, T, TC):
            w0 = [hs2[-1], c2g[-1]]
            if ps_misc.gate is not None:
                w0.append(ps_misc.gate)
                ps_misc.gate = None
            def spec2(e, t0=t0):
                m = None
                for i in range(ND):
                    m = e.matmul(ps_misc.t[0:1, 0:TC], C.ones128_bf[:, 0:1], hten[i][:, t0:t0 + TC],
                                 start=(i == 0), stop=(i == ND - 1))
                return m
            stat1 = P.op("tensor", spec2, waits=tuple(w0), inc=True)
            sq_done = None
            for i in range(ND):
                sq_done = P.op("scalar", lambda e, i=i, t0=t0: e.activation(small[4][:, 0:TC], hten[i][:, t0:t0 + TC], AF.Square),
                               waits=((hs2[-1], hng[-1] if hng else None) if i == 0 else (sq_done,)), inc=True)
                sq_done = P.op("tensor", lambda e, i=i: e.matmul(ps_misc.t[0:1, TC:2 * TC], C.ones128_bf[:, 0:1], small[4][:, 0:TC],
                                                                start=(i == 0), stop=(i == ND - 1)), waits=(sq_done,), inc=True)
            g1 = P.op("scalar", lambda e: e.activation(mu, ps_misc.t[0:1, 0:TC], AF.Copy, scale=1.0 / D), waits=(stat1, sq_done), inc=True)
            g2 = P.op("scalar", lambda e: e.activation(m2r, mu, AF.Square), waits=(g1,), inc=True)
            g3 = P.op("scalar", lambda e: e.activation(rsr, ps_misc.t[0:1, TC:2 * TC], AF.Copy, scale=1.0 / D), waits=(g2,), inc=True)
            g4 = P.op("vector", lambda e: e.tensor_sub(rsr, rsr, m2r), waits=(g3,), inc=True)
            g5 = P.op("scalar", lambda e: e.activation(m2r, rsr, AF.Ln, bias=1e-5), waits=(g4,), inc=True)
            g6 = P.op("scalar", lambda e: e.activation(rsr, m2r, AF.Exp, scale=-0.5), waits=(g5,), inc=True)
            g7 = P.op("vector", lambda e: e.tensor_mul(m2r, mu, rsr), waits=(g6,), inc=True)
            b1g = emit_bcast(P, ps_misc, 0, TC, rsr, C, waits=(g6,))
            b2g = emit_bcast(P, ps_misc, TC, TC, m2r, C, waits=(g7, b1g))
            for i in range(ND):
                a1 = P.op("vector", lambda e, i=i, t0=t0: e.tensor_mul(hn[i][:, t0:t0 + TC], hten[i][:, t0:t0 + TC], ps_misc.t[:, 0:TC]),
                          waits=(b1g,), inc=True)
                a2 = P.op("vector", lambda e, i=i, t0=t0: e.tensor_sub(hn[i][:, t0:t0 + TC], hn[i][:, t0:t0 + TC], ps_misc.t[:, TC:2 * TC]),
                          waits=(b2g, a1), inc=True)
                a3 = P.op("vector", lambda e, i=i, t0=t0, l=l: e.tensor_scalar(hn[i][:, t0:t0 + TC], hn[i][:, t0:t0 + TC],
                                                                              cc["lng"][:, l * 8 + i:l * 8 + i + 1],
                                                                              cc["lnb"][:, l * 8 + i:l * 8 + i + 1],
                                                                              ALU.mult, ALU.add),
                          waits=(a2, ccready[-1]), inc=True)
                hng.append(a3)
            ps_misc.gate = hng[-1]
        hn_done = hng[-1]

        NOCH = 22
        h2slots = [BA[i // 2][:, (i % 2) * 1024:(i % 2) * 1024 + 1024] for i in range(16)] + \
                  [BB[i][:, 0:1024] for i in range(4)] + [BB[0][:, 1024:2048], BB[1][:, 1024:2048]]
        ffg = []
        w2_last = [None]
        for h0 in (0, 1024):
            h2_done = []
            for och in range(NOCH):
                aw = 128 if och < 21 else 42
                wta = ws.load(W["w1"][l], D, aw, col0=och * 128, bias_dram=W["b1"][l])
                wtg = ws.load(W["w1"][l], D, aw, col0=2730 + och * 128, bias_dram=W["b1"][l])
                for t0 in range(0, 1024, TC):
                    pa, pg_ = ps[0], ps[1]
                    wl = [wta.ready, wtg.ready, hn_done]
                    if pa.gate is not None:
                        wl.append(pa.gate)
                    if pg_.gate is not None:
                        wl.append(pg_.gate)
                    def specag(e, t0=t0, h0=h0, aw=aw, wta=wta, wtg=wtg):
                        for ic in range(8):
                            e.matmul(pa.t[0:aw, 0:TC], wta.lhsT(ic, 0, aw), hn[ic][:, h0 + t0:h0 + t0 + TC],
                                     start=(ic == 0), stop=False)
                        e.matmul(pa.t[0:aw, 0:TC], ws.brow(wta, 0, aw), C.ones_row_bf[0:1, 0:TC], start=False, stop=True)
                        for ic in range(8):
                            e.matmul(pg_.t[0:aw, 0:TC], wtg.lhsT(ic, 0, aw), hn[ic][:, h0 + t0:h0 + t0 + TC],
                                     start=(ic == 0), stop=False)
                        return e.matmul(pg_.t[0:aw, 0:TC], ws.brow(wtg, 0, aw), C.ones_row_bf[0:1, 0:TC], start=False, stop=True)
                    mg = P.op("tensor", specag, waits=tuple(wl), inc=True)
                    ga = P.op("scalar", lambda e, aw=aw: e.activation(small[5][0:aw, 0:TC], pa.t[0:aw, 0:TC], AF.Copy),
                              waits=(mg, h2_done[-1] if h2_done else None), inc=True)
                    gg = P.op("scalar", lambda e, aw=aw: e.activation(small[5][0:aw, TC:2 * TC], pg_.t[0:aw, 0:TC], AF.Gelu),
                              waits=(ga,), inc=True)
                    pa.gate = gg
                    pg_.gate = gg
                    hv = P.op("vector", lambda e, och=och, t0=t0, aw=aw: e.tensor_mul(
                        h2slots[och][0:aw, t0:t0 + TC], small[5][0:aw, 0:TC], small[5][0:aw, TC:2 * TC]),
                        waits=(gg, w2_last[0]), inc=True)
                    h2_done.append(hv)
                wta.release(P.TE.now)
                wtg.release(P.TE.now)
            h2g = h2_done[-1]
            for oc in range(8):
                for t0 in range(0, 1024, TC):
                    pp = ps[(oc * 2 + t0 // TC) % 2]
                    mg = None
                    for och in range(NOCH):
                        aw = 128 if och < 21 else 42
                        wtr = ws.load_rows(W["w2"][l], och * 128, aw, 128, col0=oc * 128,
                                           bias_dram=(W["b2"][l] if och == 0 else None))
                        wl2 = [wtr.ready, h2g]
                        if och == 0 and pp.gate is not None:
                            wl2.append(pp.gate)
                        def spec(e, och=och, t0=t0, aw=aw, wtr=wtr, h0=h0, pp=pp):
                            m = e.matmul(pp.t[:, 0:TC], wtr.buf[0:aw, 0:128], h2slots[och][0:aw, t0:t0 + TC],
                                         start=(och == 0), stop=(och == NOCH - 1))
                            if och == 0:
                                m = e.matmul(pp.t[:, 0:TC], ws.brow(wtr, 0, 128), C.ones_row_bf[0:1, 0:TC],
                                             start=False, stop=False)
                            return m
                        mg = P.op("tensor", spec, waits=tuple(wl2), inc=True)
                        wtr.release(P.TE.now)
                    w2_last[0] = mg
                    g = P.op("vector", lambda e, oc=oc, t0=t0, h0=h0, pp=pp: e.tensor_add(
                        x_tiles[oc][:, h0 + t0:h0 + t0 + TC], x_tiles[oc][:, h0 + t0:h0 + t0 + TC],
                        pp.t[:, 0:TC]), waits=(mg,), inc=True)
                    pp.gate = g
                    ffg.append(g)
        ffn_done = ffg[-1]
        if l == 0:
            for i in range(ND):
                cur_g = P.op("vector", lambda e, i=i: e.tensor_copy(BA[i][:, 0:T], x_tiles[i][:, 0:T]),
                             waits=(ffn_done,) if i == 0 else (), inc=True)

    rg = []
    prev = [None, None]
    for i in range(ND):
        d = P.dma("sync", SF[2 + (i % 2)][:, 0:T], xsp[i * 128:(i + 1) * 128, :],
                  waits=(ffn_done, prev[i % 2]))
        g = P.op("vector", lambda e, i=i: e.scalar_tensor_tensor(
            x_tiles[i][:, 0:T], x_tiles[i][:, 0:T], 0.5, SF[2 + (i % 2)][:, 0:T], ALU.mult, ALU.add),
            waits=(d,), inc=True)
        prev[i % 2] = g
        rg.append(g)
    return rg



def emit_A1(P, C, W, x_tiles, env, x_ready=(), upto=None):
    BA, BB, SF = env["BA"], env["BB"], env["SF"]
    SFB = [SF[i][:, 0:T].bitcast(BF16) for i in range(4)]
    small = env["small6"]
    rows = env["rows"]
    ws = env["ws"]
    ps_misc, ps, ps_small, ps_acc, ps_trv = env["ps_misc"], env["ps_pair"], env["ps_small"], env["ps_acc"], env["ps_trv"]
    ident = env["ident"]
    NG_ = 16

    if "a1_alloc" not in env:
        al = {}
        a1c = {"cur": 0}
        def pf(n, rows_=128):
            c0 = a1c["cur"]
            a1c["cur"] = c0 + n
            assert a1c["cur"] <= rows.reserve
            return rows.t[0:rows_, c0:c0 + n]
        tp0, tp1 = env["tmp_tt"]
        al["tmaa"] = pf(48)
        al["hdb"] = pf(4, 16)
        al["gst"] = pf(48)
        al["pct"] = pf(16, 64)
        al["ecol"] = pf(64)
        al["eb"] = pf(128).bitcast(BF16)[0:16, 0:256]
        al["ebs"] = pf(8).bitcast(BF16)[:, 0:16]
        al["dbcs"] = pf(16)
        al["gcol"] = pf(8)
        al["bcol"] = pf(8)
        al["srow_s"] = pf(16, 1)
        SF_ = env["SF"]
        al["coloc"] = SF_[3][0:16, 0:T]
        al["cg"] = SF_[2][0:16, 1024:1280]
        al["z16"] = SF_[2][0:16, 1280:1408]
        al["pc"] = SF_[2][0:16, 1408:1472]
        al["gcs"] = SF_[2][0:16, 1472:1544]
        al["pfx"] = SF_[2][0:16, 1544:1560]
        al["A"] = SF_[2][0:64, 1024:2048]
        al["As"] = SF_[1][0:64, 1024:2048]
        al["BlF"] = SF_[2][64:128, 1024:1536].bitcast(BF16)
        al["BlS"] = SF_[1][64:128, 1024:1536].bitcast(BF16)
        al["y"] = SF_[0][:, 1024:2048]
        al["lsc"] = SF_[0][0:64, 1024:2048]
        al["alpha"] = tp0.bitcast(BF16)[:, 0:1024]
        al["wmroll"] = tp1.bitcast(BF16)[0:16, 0:1024]
        al["mut"] = tp1[0:16, 0:128]
        al["rst"] = tp1[0:16, 128:256]
        al["ysc"] = P.sbuf("a1ysc", [128, 1024], BF16)
        env["a1_alloc"] = al
        env["a1_sels"] = {
            "a1_selp": pf(8).bitcast(BF16)[:, 0:16],
            "a1_seln": pf(8).bitcast(BF16)[:, 0:16],
            "a1_bo16": pf(64).bitcast(BF16)[:, 0:128],
            "a1_expb": ws.bf[0][0:16, 0:1024],
            "a1_expbf": ws.f32[0][0:16, 0:1024],
            "a1_selpre": pf(16, 64),
            "a1_selgrp": pf(16, 64),
        }
        env["identf"] = pf(128)
        sg_ = [P.dma("sync", env["identf"][:, :], env["a1_sel_dram"]["identf"], waits=tuple(x_ready))]
        for k2 in ("a1_selp", "a1_seln", "a1_bo16", "a1_selpre", "a1_selgrp"):
            sg_.append(P.dma("sync", env["a1_sels"][k2][:, :], env["a1_sel_dram"][k2], waits=tuple(x_ready)))
        env["a1_selg"] = tuple(sg_)
        env["a1_dram"] = {
            "ktc": P.dram("a1ktc", [T, D], BF16), "ktg": P.dram("a1ktg", [4 * T, D], BF16),
            "vtc": P.dram("a1vtc", [T, D], BF16), "vtg": P.dram("a1vtg", [4 * T, D], BF16),
            "wmc": P.dram("a1wmc", [16, T], BF16), "wmg": P.dram("a1wmg", [64, T], BF16),
            "r": P.dram("a1r", [T, D], BF16), "g": P.dram("a1g", [D, T], BF16),
            "bf": P.dram("a1bf", [NG_ * 128, 1024], BF16), "bs": P.dram("a1bs", [NG_ * 128, 1024], BF16),
            "ec": P.dram("a1ec", [32, 1024], BF16), "eg": P.dram("a1eg", [128, 1024], BF16),
            "dbg": env.get("a1_dbg_out") if env.get("a1_dbg_out") is not None else P.dram("a1dbg", [128, 4096], F32),
        }
    al = env["a1_alloc"]
    dr = env["a1_dram"]

    ln_g = emit_ln(P, C, rows, x_tiles, BA, env["SMALL"], ps_misc, x_ready=x_ready)
    lng = ln_g[-1]
    xn = BA

    colg = []
    for vi, key in enumerate(("time_maa_x", "time_maa_w", "time_maa_k", "time_maa_v", "time_maa_r", "time_maa_g")):
        vec = W[key].rearrange("a b d -> (a b d)") if len(W[key].shape) == 3 else W[key]
        for i in range(8):
            colg.append(P.dma("sync", al["tmaa"][:, vi * 8 + i:vi * 8 + i + 1],
                              vec[i * 128:(i + 1) * 128].rearrange("(p u) -> p u", u=1),
                              waits=tuple(x_ready)))
    hg1 = P.dma("sync", al["hdb"][:, 0:1], W["head_decay_bias"].rearrange("(p u) -> p u", u=1),
                waits=tuple(x_ready))
    hg2 = P.dma("sync", al["hdb"][:, 1:2], W["slow_scale"].rearrange("(p u) -> p u", u=1),
                waits=tuple(x_ready))
    hg3 = P.op("scalar", lambda e: e.activation(al["hdb"][:, 2:3], al["hdb"][:, 0:1], AF.Exp), waits=(hg1,), inc=True)
    hg4 = P.op("vector", lambda e: e.tensor_scalar(al["hdb"][:, 2:3], al["hdb"][:, 2:3], -1.0 / 64, None, ALU.mult),
               waits=(hg3,), inc=True)
    hg5 = P.op("scalar", lambda e: e.activation(al["hdb"][:, 1:2], al["hdb"][:, 1:2], AF.Sigmoid), waits=(hg2,), inc=True)

    arow = rows.seg(1024)
    ag1 = P.dma("sync", arow, W["decay_mix"].rearrange("(u n) -> u n", u=1), waits=(lng,))
    ag2 = P.op("scalar", lambda e: e.activation(arow, arow, AF.Sigmoid), waits=(ag1,), inc=True)
    ab1 = emit_bcast(P, ps_misc, 0, TC, arow[0:1, 0:TC], C, waits=(ag2,))
    ab2 = emit_bcast(P, ps_misc, TC, TC, arow[0:1, TC:1024], C, waits=(ag2, ab1))
    ag3 = P.op("scalar", lambda e: e.activation(al["alpha"][:, 0:TC], ps_misc.t[:, 0:TC], AF.Copy), waits=(ab1,), inc=True)
    ag4 = P.op("scalar", lambda e: e.activation(al["alpha"][:, TC:1024], ps_misc.t[:, TC:2 * TC], AF.Copy), waits=(ab2,), inc=True)
    ps_misc.gate = ag4
    gbl = lng
    for oc in range(8):
        gbl = P.dma("sync", al["gcol"][:, oc:oc + 1],
                    W["ln_gamma"][oc * 128:(oc + 1) * 128].rearrange("(p u) -> p u", u=1), waits=(gbl,))
        gbl = P.dma("sync", al["bcol"][:, oc:oc + 1],
                    W["ln_beta"][oc * 128:(oc + 1) * 128].rearrange("(p u) -> p u", u=1), waits=(gbl,))

    ecs = al["ecol"][:, 0:8].bitcast(BF16)[:, 0:16]
    est = small[0][0:48, 0:1024]
    evg = None
    for i in range(ND):
        g1 = P.op("vector", lambda e, i=i: e.tensor_copy(ecs.rearrange("p (c u) -> p c u", u=1),
                                                         xn[i][:, 0:T].rearrange("p (c w) -> p c w", w=128)[:, :, 127:128]),
                  waits=(lng,) + ((evg,) if evg else ()), inc=True)
        g2 = P.op("tensor", lambda e: e.transpose(ps_trv.t[0:16, 0:128], ecs, ident[:, :]),
                  waits=(g1,) + ((ps_trv.gate,) if ps_trv.gate else ()), inc=True)
        evg = P.op("scalar", lambda e, i=i: e.activation(est[0:16, i * 128:(i + 1) * 128], ps_trv.t[0:16, 0:128], AF.Copy),
                   waits=(g2,), inc=True)
        ps_trv.gate = evg
        g1b = P.op("vector", lambda e, i=i: e.tensor_copy(ecs.rearrange("p (c u) -> p c u", u=1),
                                                          xn[i][:, 0:T].rearrange("p (c w) -> p c w", w=128)[:, :, 0:1]),
                   waits=(evg,), inc=True)
        g2b = P.op("tensor", lambda e: e.transpose(ps_trv.t[0:16, 0:128], ecs, ident[:, :]),
                   waits=(g1b,) + ((ps_trv.gate,) if ps_trv.gate else ()), inc=True)
        evg = P.op("scalar", lambda e, i=i: e.activation(est[32:48, i * 128:(i + 1) * 128], ps_trv.t[0:16, 0:128], AF.Copy),
                   waits=(g2b,), inc=True)
        ps_trv.gate = evg
    d1a = P.dma("gpsimd", dr["ec"][0:16, :], est[0:16, :], waits=(evg,))
    d1 = P.dma("gpsimd", dr["ec"][16:32, :], est[32:48, :], waits=(evg, d1a))
    cce = P.collective("AllGather", ALU.bypass, [dr["ec"]], [dr["eg"]], waits=(d1,))
    egt = small[1][0:128, 0:1024]
    d2 = P.dma("gpsimd", egt, dr["eg"], waits=(cce,))
    halos = {}
    for nm, selk in (("prev", "a1_selp"), ("next", "a1_seln")):
        htok = small[2][0:16, 0:1024]
        he1 = None
        for half in range(2):
            m1 = P.op("tensor", lambda e, half=half, selk=selk: e.matmul(
                ps_small.t[0:16, 0:TC], env["a1_sels"][selk][:, :], egt[:, half * TC:(half + 1) * TC],
                start=True, stop=True), waits=(d2,) + env["a1_selg"] + ((ps_small.gate,) if ps_small.gate else ()),
                inc=True)
            he1 = P.op("scalar", lambda e, half=half: e.activation(htok[:, half * TC:(half + 1) * TC],
                                                                   ps_small.t[0:16, 0:TC], AF.Copy), waits=(m1,), inc=True)
            ps_small.gate = he1
        hfm = al["ysc"][:, 0:128] if nm == "prev" else al["ysc"][:, 128:256]
        hg = None
        for i in range(ND):
            g2 = P.op("tensor", lambda e, i=i: e.transpose(ps_trv.t[0:128, 0:16], htok[:, i * 128:(i + 1) * 128], ident[0:16, 0:16]),
                      waits=(he1,) + ((ps_trv.gate,) if ps_trv.gate else ()), inc=True)
            hg = P.op("scalar", lambda e, i=i, hfm=hfm: e.activation(hfm[:, i * 16:(i + 1) * 16], ps_trv.t[0:128, 0:16], AF.Copy),
                      waits=(g2,), inc=True)
            ps_trv.gate = hg
        halos[nm] = (hfm, hg)

    dbh1 = P.dma("gpsimd", dr["dbg"][:, 2048:2560], egt.bitcast(F32)[:, 0:512], waits=(halos["prev"][1], halos["next"][1]))
    dbh2 = P.dma("gpsimd", dr["dbg"][:, 2560:2624], halos["prev"][0].bitcast(F32)[:, 0:64], waits=(dbh1,))
    dbh3 = P.dma("gpsimd", dr["dbg"][:, 2624:2688], halos["next"][0].bitcast(F32)[:, 0:64], waits=(dbh2,))
    dxg = []
    for i in range(ND):
        hp, hpg = halos["prev"]
        hn_, hng_ = halos["next"]
        g1 = P.op("vector", lambda e, i=i: e.tensor_copy(
            BB[i][:, 0:T].rearrange("p (c w) -> p c w", w=128)[:, :, 1:128],
            xn[i][:, 0:T].rearrange("p (c w) -> p c w", w=128)[:, :, 0:127]),
                  waits=(lng, hng_), inc=True)
        g2 = P.op("vector", lambda e, i=i, hp=hp: e.tensor_copy(
            BB[i][:, 0:T].rearrange("p (c w) -> p c w", w=128)[:, :, 0:1],
            hp[:, i * 16:(i + 1) * 16].rearrange("p (c u) -> p c u", u=1)), waits=(g1,), inc=True)
        g3 = P.op("vector", lambda e, i=i: e.tensor_add(
            BB[i][:, 0:T].rearrange("p (c w) -> p c w", w=128)[:, :, 0:127],
            BB[i][:, 0:T].rearrange("p (c w) -> p c w", w=128)[:, :, 0:127],
            xn[i][:, 0:T].rearrange("p (c w) -> p c w", w=128)[:, :, 1:128]),
                  waits=(g2,), inc=True)
        g4 = P.op("vector", lambda e, i=i, hn_=hn_: e.tensor_add(
            BB[i][:, 0:T].rearrange("p (c w) -> p c w", w=128)[:, :, 127:128],
            BB[i][:, 0:T].rearrange("p (c w) -> p c w", w=128)[:, :, 127:128],
            hn_[:, i * 16:(i + 1) * 16].rearrange("p (c u) -> p c u", u=1)), waits=(g3,), inc=True)
        g5 = P.op("vector", lambda e, i=i: e.scalar_tensor_tensor(BB[i][:, 0:T], BB[i][:, 0:T], 0.5, xn[i][:, 0:T],
                                                                 ALU.mult, ALU.subtract), waits=(g4,), inc=True)
        dxg.append(g5)
    dx_done = dxg[-1]

    dbg7 = P.dma("gpsimd", dr["dbg"][:, 1464:1976], BB[0].bitcast(F32)[:, 0:512], waits=(dx_done,))
    con_g_dbg = dbg7
    xxx = [SFB[i // 2][:, (i % 2) * T:(i % 2) * T + T] for i in range(8)]
    xxg = []
    for i in range(ND):
        xxg.append(P.op("vector", lambda e, i=i: e.scalar_tensor_tensor(
            xxx[i][:, 0:T], BB[i][:, 0:T], al["tmaa"][:, i:i + 1], xn[i][:, 0:T], ALU.mult, ALU.add),
            waits=(dx_done, colg[-1]), inc=True))
    xx_done = xxg[-1]
    mixA = [small[0][:, 0:1024], small[1][:, 0:1024]]
    mixB = [small[2][0:64, 0:1024], small[3][0:64, 0:1024]]
    wt1 = ws.load(W["time_maa_w1"], D, 128)
    wt1b = ws.load(W["time_maa_w1"], D, 32, col0=128)
    mx_done = None
    for t0 in range(0, T, TC):
        pp = ps[(t0 // TC) % 2]
        wl = [wt1.ready, wt1b.ready, xx_done]
        if pp.gate is not None:
            wl.append(pp.gate)
        def spec2(e, t0=t0, pp=pp, wt1=wt1, wt1b=wt1b):
            for ic in range(8):
                e.matmul(pp.t[0:128, 0:TC], wt1.lhsT(ic, 0, 128), xxx[ic][:, t0:t0 + TC],
                         start=(ic == 0), stop=(ic == 7))
            m = None
            for ic in range(8):
                m = e.matmul(ps_small.t[0:32, 0:TC], wt1b.lhsT(ic, 0, 32), xxx[ic][:, t0:t0 + TC],
                             start=(ic == 0), stop=(ic == 7))
            return m
        wl2 = list(wl)
        if ps_small.gate is not None:
            wl2.append(ps_small.gate)
        mg = P.op("tensor", spec2, waits=tuple(wl2), inc=True)
        th = t0 // 1024
        tc2 = (t0 % 1024)
        e1 = P.op("scalar", lambda e, th=th, tc2=tc2, pp=pp: e.activation(mixA[th][0:32, tc2:tc2 + TC], pp.t[0:32, 0:TC], AF.Tanh), waits=(mg,), inc=True)
        e2 = P.op("scalar", lambda e, th=th, tc2=tc2, pp=pp: e.activation(mixA[th][32:64, tc2:tc2 + TC], pp.t[32:64, 0:TC], AF.Tanh), waits=(e1,), inc=True)
        e3 = P.op("scalar", lambda e, th=th, tc2=tc2, pp=pp: e.activation(mixA[th][64:96, tc2:tc2 + TC], pp.t[64:96, 0:TC], AF.Tanh), waits=(e2,), inc=True)
        e4 = P.op("scalar", lambda e, th=th, tc2=tc2, pp=pp: e.activation(mixB[th][0:32, tc2:tc2 + TC], pp.t[96:128, 0:TC], AF.Tanh), waits=(e3,), inc=True)
        e5 = P.op("scalar", lambda e, th=th, tc2=tc2: e.activation(mixB[th][32:64, tc2:tc2 + TC], ps_small.t[0:32, 0:TC], AF.Tanh), waits=(e4,), inc=True)
        pp.gate = e4
        ps_small.gate = e5
        mx_done = e5
    wt1.release(P.TE.now)
    wt1b.release(P.TE.now)

    def mix_ap(f, th):
        if f < 3:
            return mixA[th][f * 32:(f + 1) * 32, 0:1024]
        return mixB[th][(f - 3) * 32:(f - 2) * 32, 0:1024]
    MIX_BASE = [0, 32, 64, 0, 32]
    xt = xxx
    scrb = al["ysc"]
    tgt_W = {"w": None, "k": "W_k", "v": "W_v", "r": "W_r", "g": "W_g"}
    tgt_dst = {"g": dr["g"]}
    last_xt = [mx_done]
    con_g = []
    prev_scr_dma = [None]
    prev_tok_dma = [None, None]
    for f, tgt in enumerate(("w", "k", "v", "r", "g")):
        wt2f = ws.load_rows(W["time_maa_w2"][f], 0, 32, 1024, base=MIX_BASE[f])
        xtg = []
        for oc in range(8):
            for t0 in range(0, T, TC):
                th, tc2 = t0 // 1024, t0 % 1024
                pp = ps[(oc + t0 // TC) % 2]
                wl = [wt2f.ready, mx_done]
                if pp.gate is not None:
                    wl.append(pp.gate)
                mg = P.op("tensor", lambda e, f=f, oc=oc, th=th, tc2=tc2, pp=pp, wt2f=wt2f, b=MIX_BASE[f]: e.matmul(
                    pp.t[:, 0:TC], wt2f.buf[b:b + 32, oc * 128:(oc + 1) * 128], mix_ap(f, th)[:, tc2:tc2 + TC],
                    start=True, stop=True), waits=tuple(w for w in wl if w), inc=True)
                v1 = P.op("vector", lambda e, f=f, oc=oc, t0=t0, pp=pp: e.scalar_tensor_tensor(
                    scrb[:, 0:TC], pp.t[:, 0:TC], al["tmaa"][:, (f + 1) * 8 + oc:(f + 1) * 8 + oc + 1],
                    BB[oc][:, t0:t0 + TC], ALU.add, ALU.mult), waits=(mg, xtg[-1] if xtg else None), inc=True)
                v2 = P.op("vector", lambda e, f=f, oc=oc, t0=t0: e.tensor_add(
                    xt[oc][:, t0:t0 + TC], scrb[:, 0:TC],
                    xn[oc][:, t0:t0 + TC]), waits=(v1,), inc=True)
                pp.gate = v2
                xtg.append(v2)
        wt2f.release(P.TE.now)
        xt_done = xtg[-1]
        if tgt == "w":
            w1bf = small[5][:, 0:512]
            w1f = small[5].bitcast(F32)[:, 256:512]
            wsrc = W["time_decay_w1"].rearrange("(ic p) o -> p ic o", p=128)
            wg_ = None
            w1g = None
            for rnd in range(2):
                dmaw = P.dma("sync", w1f.rearrange("p (ic o) -> p ic o", o=64), wsrc[:, rnd * 4:(rnd + 1) * 4, :],
                             waits=(w1g,))
                w1g = P.op("vector", lambda e, rnd=rnd: e.tensor_copy(w1bf[:, rnd * 256:(rnd + 1) * 256],
                                                                      w1f[:, 0:256]), waits=(dmaw,), inc=True)
            td_flat = W["time_decay"].rearrange("a b d -> (a b d)")
            wm_done = None
            wm_dma = [None, None]
            for t0 in range(0, T, TC):
                pp = ps[(t0 // TC) % 2]
                wl = [w1g, xt_done]
                if pp.gate is not None:
                    wl.append(pp.gate)
                mg = P.op("tensor", lambda e, t0=t0, pp=pp: [
                    e.matmul(pp.t[0:64, 0:TC], w1bf[:, ic * 64:(ic + 1) * 64], xt[ic][:, t0:t0 + TC],
                             start=(ic == 0), stop=(ic == 7)) for ic in range(8)][-1],
                    waits=tuple(wl), inc=True)
                th_g = P.op("scalar", lambda e, pp=pp: e.activation(scrb[0:64, 512:1024], pp.t[0:64, 0:TC], AF.Tanh),
                            waits=(mg, wm_done), inc=True)
                pp.gate = th_g
                sum_g = None
                for oc in range(8):
                    wtd2 = ws.load_rows(W["time_decay_w2"], 0, 64, 128, col0=oc * 128, bias_dram=td_flat)
                    pp2 = ps[(oc + 1) % 2]
                    wl2 = [wtd2.ready, th_g]
                    if pp2.gate is not None:
                        wl2.append(pp2.gate)
                    mg2 = P.op("tensor", lambda e, oc=oc, pp2=pp2, wtd2=wtd2: (
                        e.matmul(pp2.t[:, 0:TC], wtd2.buf[0:64, 0:128], scrb[0:64, 512:1024], start=True, stop=False),
                        e.matmul(pp2.t[:, 0:TC], ws.brow(wtd2, 0, 128), C.ones_row_bf[0:1, 0:TC], start=False, stop=True))[-1],
                        waits=tuple(wl2), inc=True)
                    wtd2.release(P.TE.now)
                    ex_g = P.op("scalar", lambda e, pp2=pp2: e.activation(scrb[:, 0:TC], pp2.t[:, 0:TC], AF.Exp),
                                waits=(mg2, sum_g), inc=True)
                    pp2.gate = ex_g
                    wl3 = [ex_g]
                    if oc == 0:
                        wl3 += list(env["a1_selg"])
                        if ps_small.gate is not None:
                            wl3.append(ps_small.gate)
                    sum_g = P.op("tensor", lambda e, oc=oc: e.matmul(ps_small.t[0:16, 0:TC], env["a1_sels"]["a1_bo16"][:, oc * 16:(oc + 1) * 16],
                                                                    scrb[:, 0:TC], start=(oc == 0), stop=(oc == 7)),
                                 waits=tuple(wl3), inc=True)
                half = (t0 // TC) % 2
                g1 = P.op("vector", lambda e, half=half: e.tensor_scalar(
                    al["wmroll"][0:16, half * TC:half * TC + TC], ps_small.t[0:16, 0:TC],
                    al["hdb"][0:16, 2:3], None, ALU.mult),
                          waits=(sum_g, hg4, wm_dma[half]), inc=True)
                ps_small.gate = g1
                wm_dma[half] = P.dma("gpsimd", dr["wmc"][0:16, t0:t0 + TC],
                                     al["wmroll"][0:16, half * TC:half * TC + TC], waits=(g1,))
                wm_done = wm_dma[half]
            con_g.append(wm_done)
            wm_all = tuple(w for w in wm_dma if w)
        else:
            wkey = tgt_W[tgt]
            if tgt in ("k", "v", "r"):
                for t0 in range(0, T, TC):
                    tev = None
                    for oc in range(8):
                        wt = ws.load(W[wkey], D, 128, col0=oc * 128)
                        pp = ps[(oc + t0 // TC) % 2]
                        wl = [wt.ready, xt_done]
                        if pp.gate is not None:
                            wl.append(pp.gate)
                        mg = P.op("tensor", lambda e, t0=t0, pp=pp, wt=wt: [
                            e.matmul(pp.t[:, 0:TC], wt.lhsT(ic, 0, 128), xt[ic][:, t0:t0 + TC],
                                     start=(ic == 0), stop=(ic == 7)) for ic in range(8)][-1],
                            waits=tuple(wl), inc=True)
                        sc_ = 0.125 if tgt == "r" else 1.0
                        ev = P.op("scalar", lambda e, pp=pp, sc_=sc_: e.activation(scrb[:, 0:TC], pp.t[:, 0:TC], AF.Copy,
                                                                                  scale=sc_),
                                  waits=(mg, prev_scr_dma[0]), inc=True)
                        pp.gate = ev
                        prev_scr_dma[0] = None
                        trg = None
                        for q in range(4):
                            trg = P.op("tensor", lambda e, q=q: e.transpose(ps_trv.t[:, q * 128:(q + 1) * 128],
                                                                           scrb[:, q * 128:(q + 1) * 128], ident[:, :]),
                                       waits=(ev,) + ((ps_trv.gate,) if q == 0 and ps_trv.gate else ()), inc=True)
                        dstt = {"k": dr["ktc"], "v": dr["vtc"], "r": dr["r"]}[tgt]
                        for q in range(4):
                            sl = small[4][:, (q % 2) * 128:(q % 2) * 128 + 128]
                            tev = P.op("scalar", lambda e, q=q, sl=sl: e.activation(
                                sl, ps_trv.t[:, q * 128:(q + 1) * 128], AF.Copy),
                                waits=(trg, prev_tok_dma[q % 2]), inc=True)
                            dg2 = P.dma("gpsimd", dstt[t0 + q * 128:t0 + (q + 1) * 128, oc * 128:(oc + 1) * 128],
                                        sl, waits=(tev,))
                            prev_tok_dma[q % 2] = dg2
                            con_g.append(dg2)
                        ps_trv.gate = tev
                        wt.release(P.TE.now)
            else:
                for col0 in range(0, D, 128):
                    wt = ws.load(W[wkey], D, 128, col0=col0)
                    oc = col0 // 128
                    for t0 in range(0, T, TC):
                        pp = ps[(oc + t0 // TC) % 2]
                        wl = [wt.ready, xt_done]
                        if pp.gate is not None:
                            wl.append(pp.gate)
                        mg = P.op("tensor", lambda e, t0=t0, pp=pp, wt=wt: [
                            e.matmul(pp.t[:, 0:TC], wt.lhsT(ic, 0, 128), xt[ic][:, t0:t0 + TC],
                                     start=(ic == 0), stop=(ic == 7)) for ic in range(8)][-1],
                            waits=tuple(wl), inc=True)
                        fn = AF.Silu if tgt == "g" else AF.Copy
                        ev = P.op("scalar", lambda e, pp=pp, fn=fn: e.activation(scrb[:, 0:TC], pp.t[:, 0:TC], fn),
                                  waits=(mg, prev_scr_dma[0]), inc=True)
                        pp.gate = ev
                        dg = P.dma("gpsimd", tgt_dst[tgt][oc * 128:(oc + 1) * 128, t0:t0 + TC], scrb[:, 0:TC], waits=(ev,))
                        prev_scr_dma[0] = dg
                        con_g.append(dg)
                    wt.release(P.TE.now)
    proj_done = con_g[-1]

    front = tuple(P.dma_frontier("gpsimd")) + tuple(P.dma_frontier("sync"))
    cc1 = P.collective("AllGather", ALU.bypass, [dr["ktc"]], [dr["ktg"]], waits=front)
    cc2 = P.collective("AllGather", ALU.bypass, [dr["vtc"]], [dr["vtg"]], waits=(cc1,))
    cc4 = P.collective("AllGather", ALU.bypass, [dr["wmc"]], [dr["wmg"]], waits=(cc2,))

    def g_(pt):
        return (pt.gate,) if pt.gate is not None else ()

    SEL = env["a1_sels"]
    identf = env["identf"]
    selG = env["a1_selg"]
    if upto == "proj":
        return [proj_done, cc4]
    exg1 = P.dma("sync", SEL["a1_expbf"], env["a1_sel_dram"]["a1_expbf"], waits=(proj_done,))
    zg = P.op("vector", lambda e: e.memset(al["z16"][:, :], 0.0), waits=(proj_done,), inc=True)
    exg2 = P.dma("sync", SEL["a1_expb"], env["a1_sel_dram"]["a1_expb"], waits=(proj_done,))

    sgc = al["ecol"][:, 0:16]
    nsgc = al["ecol"][:, 16:32]
    lamF = al["ecol"][:, 32:48]
    lamS = al["ecol"][:, 48:64]
    srow_s = al["srow_s"]
    s0 = P.op("tensor", lambda e: e.transpose(ps_small.t[0:1, 0:16], al["hdb"][0:16, 1:2], identf[0:16, 0:16]),
              waits=(hg5,) + selG + g_(ps_small), inc=True)
    s1 = P.op("scalar", lambda e: e.activation(srow_s, ps_small.t[0:1, 0:16], AF.Copy), waits=(s0,), inc=True)
    s2 = P.op("tensor", lambda e: e.matmul(ps_small.t[:, 16:32], C.onescol(0), srow_s, start=True, stop=True),
              waits=(s1,), inc=True)
    s3 = P.op("vector", lambda e: e.tensor_copy(sgc, ps_small.t[:, 16:32]), waits=(s2,), inc=True)
    s4 = P.op("vector", lambda e: e.tensor_scalar(nsgc, sgc, -1.0, None, ALU.mult), waits=(s3,), inc=True)
    ps_small.gate = s3

    cgb = al["cg"].bitcast(BF16)
    wmgv = dr["wmg"].rearrange("(a h) t -> a h t", a=4)
    sum_done = zg
    for c in range(GCH):
        slab, c4 = c % 4, c // 4
        dgt = P.dma("sync", cgb[0:16, 0:128], wmgv[slab][:, c4 * 128:(c4 + 1) * 128], waits=(cc4, sum_done))
        sg = P.op("vector", lambda e: e.tensor_tensor_scan(al["cg"][0:16, 128:256], cgb[0:16, 0:128],
                                                           al["z16"][0:16, 0:128], 0.0, ALU.add, ALU.add),
                  waits=(dgt, zg), inc=True)
        sum_done = P.op("vector", lambda e, c=c: e.tensor_copy(al["gcs"][0:16, c:c + 1], al["cg"][0:16, 255:256]),
                        waits=(sg,), inc=True)
        if c == 0:
            sum_done = P.dma("gpsimd", dr["dbg"][0:16, 1208:1464], al["cg"][0:16, 0:256], waits=(sum_done,))
    pc_g = P.op("vector", lambda e: e.tensor_tensor_scan(al["pc"][0:16, 0:64], al["gcs"][0:16, 0:64],
                                                         al["z16"][0:16, 0:64], 0.0, ALU.add, ALU.add),
                waits=(sum_done,), inc=True)
    tr1 = P.op("tensor", lambda e: e.transpose(ps_small.t[0:64, 32:48], al["pc"][0:16, 0:64], identf[0:16, 0:16]),
               waits=(pc_g,) + g_(ps_small), inc=True)
    tr1e = P.op("scalar", lambda e: e.activation(al["pct"][0:64, 0:16], ps_small.t[0:64, 32:48], AF.Copy),
                waits=(tr1,), inc=True)
    pm1 = P.op("tensor", lambda e: e.matmul(ps_small.t[0:16, 0:16], al["pct"][0:64, 0:16], SEL["a1_selpre"][:, :],
                                            start=True, stop=True), waits=(tr1e,), inc=True)
    pm1e = P.op("vector", lambda e: e.tensor_copy(al["pfx"][0:16, 0:16], ps_small.t[0:16, 0:16]), waits=(pm1,), inc=True)
    pm2 = P.op("tensor", lambda e: e.matmul(ps_small.t[0:16, 16:32], al["pct"][0:64, 0:16], SEL["a1_selgrp"][:, :],
                                            start=True, stop=True), waits=(pm1e,), inc=True)
    pm2e = P.op("vector", lambda e: e.tensor_sub(al["pfx"][0:16, 0:16], al["pfx"][0:16, 0:16], ps_small.t[0:16, 16:32]),
                waits=(pm2,), inc=True)
    ps_small.gate = pm2e
    cl_done = pm2e
    for j in range(NCH):
        dwj = P.dma("sync", cgb[0:16, 0:128], dr["wmc"][0:16, j * 128:(j + 1) * 128],
                    waits=(cl_done,) + wm_all)
        cl_done = P.op("vector", lambda e, j=j: e.tensor_tensor_scan(
            al["coloc"][:, j * 128:(j + 1) * 128], cgb[0:16, 0:128], al["z16"][0:16, 0:128],
            al["pfx"][0:16, j:j + 1], ALU.add, ALU.add), waits=(dwj,), inc=True)
    act = SF[2][:, 0:1024]
    carry = al["gcs"][0:16, 64:65]
    act_done = None
    prev_cp = cl_done
    prev_tg = None
    for c in range(GCH):
        slab, c4 = c % 4, c // 4
        dgt = P.dma("sync", cgb[0:16, 0:128], wmgv[slab][:, c4 * 128:(c4 + 1) * 128], waits=(prev_cp,))
        ini = al["z16"][0:16, 0:1] if c % 4 == 0 else carry
        sg = P.op("vector", lambda e, ini=ini: e.tensor_tensor_scan(al["cg"][0:16, 128:256], cgb[0:16, 0:128],
                                                                    al["z16"][0:16, 0:128], ini, ALU.add, ALU.add),
                  waits=(dgt, prev_tg), inc=True)
        prev_cp = P.op("vector", lambda e: e.tensor_copy(carry, al["cg"][0:16, 255:256]), waits=(sg,), inc=True)
        prev_tg = P.op("tensor", lambda e: e.transpose(ps_small.t[:, 32:48], al["cg"][0:16, 128:256],
                                                       identf[0:16, 0:16]), waits=(sg,) + g_(ps_small), inc=True)
        act_done = P.op("scalar", lambda e, c=c: e.activation(act[:, c * 16:(c + 1) * 16], ps_small.t[:, 32:48],
                                                              AF.Copy), waits=(prev_tg,), inc=True)
        ps_small.gate = act_done

    dbg1 = P.dma("gpsimd", dr["dbg"][:, 0:1024], act, waits=(act_done,))
    dbg2 = P.dma("gpsimd", dr["dbg"][0:16, 1024:1096], al["gcs"][0:16, 0:72], waits=(act_done, prev_cp, dbg1))
    dbg3 = P.dma("gpsimd", dr["dbg"][0:16, 1096:1160], al["pc"][0:16, 0:64], waits=(pc_g, dbg2))
    dbg4 = P.dma("gpsimd", dr["dbg"][0:64, 1160:1176], al["pct"][0:64, 0:16], waits=(tr1e, dbg3))
    dbg5 = P.dma("gpsimd", dr["dbg"][0:16, 1176:1192], al["pfx"][0:16, 0:16], waits=(pm2e, dbg4))
    dbg6 = P.dma("gpsimd", dr["dbg"][0:16, 1192:1208], al["coloc"][0:16, 0:16] if False else al["coloc"][0:16, 0:16], waits=(cl_done, dbg5))

    if upto == "C":
        return [dbg6, con_g_dbg, prev_cp]
    ktch = [BB[0][:, 0:1024], BB[0][:, 1024:2048], BB[1][:, 0:1024], BB[1][:, 1024:2048]]
    vtch = [BB[2][:, 0:1024], BB[2][:, 1024:2048], BB[3][:, 0:1024], BB[3][:, 1024:2048]]
    ktf = [BB[4][:, 0:1024], BB[4][:, 1024:2048]]
    kfr = [BB[5][0:64, 0:T], BB[6][0:64, 0:T], BB[7][0:64, 0:T], BB[5][64:128, 0:T]]
    rt64 = BB[4]
    rtok = small[4]
    AbF = SFB[0][0:64, 0:1024]
    AbS = SFB[0][0:64, 1024:2048]
    BlF = al["BlF"]
    BlS = al["BlS"]
    Ast = [al["A"][0:64, 0:1024], al["As"][0:64, 0:1024]]
    ebscr = al["ebs"]
    ebt = al["eb"]
    lsc = al["lsc"]
    expb = SEL["a1_expb"]
    expbf = SEL["a1_expbf"]
    ysb = al["ysc"]
    ynb = al["ysc"]
    ktgv = dr["ktg"].rearrange("(a t) d -> a t d", a=4)
    vtgv = dr["vtg"].rearrange("(a t) d -> a t d", a=4)

    def dma_group(G, waits):
        gs = []
        for r in range(4):
            gs.append(P.dma("sync", ktch[r], ktgv[r][G * 128:(G + 1) * 128, :], waits=waits))
            gs.append(P.dma("sync", vtch[r], vtgv[r][G * 128:(G + 1) * 128, :], waits=waits))
        return gs

    dbc_state = {"gate": None, "users": [], "dsub": None}

    def emit_dbc(G):
        drow = rows.seg(16)
        drow2 = rows.seg(16)
        pv = (dbc_state["dsub"],) if dbc_state["dsub"] else ()
        if G > 0:
            rg1 = P.dma("gpsimd", drow, al["pct"][4 * G - 1:4 * G, 0:16], waits=(tr1e,) + pv)
        else:
            rg1 = P.op("vector", lambda e, drow=drow: e.memset(drow, 0.0), waits=pv, inc=True)
        rg2 = P.dma("gpsimd", drow2, al["pct"][4 * G + 3:4 * G + 4, 0:16], waits=(tr1e,) + pv)
        dsub = P.op("vector", lambda e, drow=drow, drow2=drow2: e.tensor_sub(drow, drow2, drow),
                    waits=(rg1, rg2), inc=True)
        dbc_state["dsub"] = dsub
        mmd = P.op("tensor", lambda e, drow=drow: e.matmul(ps_small.t[:, 16:32], C.onescol(0), drow,
                                                           start=True, stop=True),
                   waits=(dsub,) + g_(ps_small), inc=True)
        w = tuple(dbc_state["users"]) + ((dbc_state["gate"],) if dbc_state["gate"] else ())
        dbc = P.op("vector", lambda e: e.tensor_copy(al["dbcs"], ps_small.t[:, 16:32]), waits=(mmd,) + w, inc=True)
        ps_small.gate = dbc
        dbc_state["gate"] = dbc
        dbc_state["users"] = []
        return dbc

    def emit_lam(dbc, lam_users):
        l1 = P.op("scalar", lambda e: e.activation(lamF, al["dbcs"], AF.Exp),
                  waits=(dbc,) + tuple(lam_users), inc=True)
        l2 = P.op("vector", lambda e: e.tensor_mul(al["gst"][:, 16:32], al["dbcs"], sgc),
                  waits=(dbc, s3), inc=True)
        l3 = P.op("scalar", lambda e: e.activation(lamS, al["gst"][:, 16:32], AF.Exp),
                  waits=(l2,) + tuple(lam_users), inc=True)
        dbc_state["users"] += [l1, l2]
        return (l1, l3)

    st = {"last_t3": None, "ktf_user": None}

    def state_pass(G, direction, dg, dbc, lamg, upd_extra=(), first_waits=()):
        upd = []
        for s in range(2):
            mlast = None
            for r in range(4):
                c = 4 * G + r
                ew = (act_done,) + (first_waits if (s == 0 and r == 0) else ())
                lt3 = (st["last_t3"],) if st["last_t3"] else ()
                if direction == "bwd":
                    if s == 0:
                        t2 = P.op("scalar", lambda e, c=c: e.activation(ebscr, act[:, c * 16:(c + 1) * 16], AF.Exp),
                                  waits=ew + lt3, inc=True)
                    else:
                        t1 = P.op("vector", lambda e, c=c: e.tensor_mul(al["gst"][:, 0:16],
                                                                        act[:, c * 16:(c + 1) * 16], sgc),
                                  waits=ew + (s3,), inc=True)
                        t2 = P.op("scalar", lambda e: e.activation(ebscr, al["gst"][:, 0:16], AF.Exp),
                                  waits=(t1,) + lt3, inc=True)
                else:
                    t1 = P.op("vector", lambda e, c=c: e.tensor_sub(al["gst"][:, 0:16], al["dbcs"],
                                                                    act[:, c * 16:(c + 1) * 16]),
                              waits=ew + (dbc,), inc=True)
                    dbc_state["users"].append(t1)
                    if s == 1:
                        t1 = P.op("vector", lambda e: e.tensor_mul(al["gst"][:, 0:16], al["gst"][:, 0:16], sgc),
                                  waits=(t1, s3), inc=True)
                    t2 = P.op("scalar", lambda e: e.activation(ebscr, al["gst"][:, 0:16], AF.Exp),
                              waits=(t1,) + lt3, inc=True)
                t3 = P.op("tensor", lambda e: e.transpose(ps_trv.t[0:16, 0:128], ebscr, ident[:, :]),
                          waits=(t2,) + g_(ps_trv), inc=True)
                st["last_t3"] = t3
                t4 = P.op("scalar", lambda e: e.activation(ebt[:, 0:128], ps_trv.t[0:16, 0:128], AF.Copy),
                          waits=(t3,), inc=True)
                ps_trv.gate = t4
                x1 = P.op("tensor", lambda e: e.matmul(ps[0].t[:, 0:TC], ebt[:, 0:128], expb[:, 0:TC],
                                                       start=True, stop=True), waits=(t4, exg2) + g_(ps[0]), inc=True)
                x2 = P.op("tensor", lambda e: e.matmul(ps[1].t[:, 0:TC], ebt[:, 0:128], expb[:, TC:1024],
                                                       start=True, stop=True), waits=(t4, exg2) + g_(ps[1]), inc=True)
                kw = (dg[2 * r],) + ((st["ktf_user"],) if st["ktf_user"] else ())
                k1 = P.op("vector", lambda e, r=r, s=s: e.tensor_mul(ktf[s][:, 0:TC], ktch[r][:, 0:TC],
                                                                     ps[0].t[:, 0:TC]), waits=(x1,) + kw, inc=True)
                k2 = P.op("vector", lambda e, r=r, s=s: e.tensor_mul(ktf[s][:, TC:1024], ktch[r][:, TC:1024],
                                                                     ps[1].t[:, 0:TC]), waits=(x2, k1), inc=True)
                ps[0].gate = k1
                ps[1].gate = k2
                def mspec(e, r=r, s=s):
                    m = None
                    for h in range(16):
                        m = e.matmul(ps_misc.t[0:64, h * 64:(h + 1) * 64], ktf[s][:, h * 64:(h + 1) * 64],
                                     vtch[r][:, h * 64:(h + 1) * 64],
                                     start=(r == 0 and h % 8 == 0), stop=(r == 3 and h % 8 == 7))
                    return m
                wl = [k1, k2, dg[2 * r + 1]]
                if s == 0 and r == 0:
                    wl += list(g_(ps_misc))
                mlast = P.op("tensor", mspec, waits=tuple(wl), inc=True)
                st["ktf_user"] = mlast
            lam_src = lamF if s == 0 else lamS
            lb = P.op("vector", lambda e, lam_src=lam_src: e.tensor_copy(ebscr, lam_src),
                      waits=(lamg[s], st["last_t3"]), inc=True)
            l4 = P.op("tensor", lambda e: e.transpose(ps_trv.t[0:16, 0:128], ebscr, ident[:, :]),
                      waits=(lb,) + g_(ps_trv), inc=True)
            st["last_t3"] = l4
            l5 = P.op("scalar", lambda e: e.activation(ebt[:, 128:256], ps_trv.t[0:16, 0:128], AF.Copy),
                      waits=(l4,), inc=True)
            ps_trv.gate = l5
            y1 = P.op("tensor", lambda e: e.matmul(ps[0].t[:, 0:TC], ebt[:, 128:256], expb[:, 0:TC],
                                                   start=True, stop=True), waits=(l5, exg2) + g_(ps[0]), inc=True)
            y2 = P.op("tensor", lambda e: e.matmul(ps[1].t[:, 0:TC], ebt[:, 128:256], expb[:, TC:1024],
                                                   start=True, stop=True), waits=(l5, exg2) + g_(ps[1]), inc=True)
            v1 = P.op("vector", lambda e, s=s: e.tensor_mul(lsc[:, 0:TC], Ast[s][:, 0:TC], ps[0].t[0:64, 0:TC]),
                      waits=(y1,) + upd_extra, inc=True)
            v2 = P.op("vector", lambda e, s=s: e.tensor_mul(lsc[:, TC:1024], Ast[s][:, TC:1024], ps[1].t[0:64, 0:TC]),
                      waits=(y2, v1), inc=True)
            ps[0].gate = v1
            ps[1].gate = v2
            v3 = P.op("vector", lambda e, s=s: e.tensor_add(Ast[s][:, 0:1024], lsc[:, 0:1024],
                                                            ps_misc.t[0:64, 0:1024]), waits=(v2, mlast), inc=True)
            ps_misc.gate = v3
            upd.append(v3)
        return upd

    z1 = P.op("vector", lambda e: e.memset(al["A"][0:64, 0:1024], 0.0),
              waits=(proj_done, act_done, prev_cp, cl_done, exg1, exg2, dbg6, con_g_dbg), inc=True)
    z2 = P.op("vector", lambda e: e.memset(al["As"][0:64, 0:1024], 0.0), waits=(z1,), inc=True)
    prev_step = (z2,)
    prev_d = (z2,)
    bwd_d = (cc2,)
    for G in range(NCH - 1, -1, -1):
        dg = dma_group(G, prev_step + (cc2,))
        sb1 = P.op("vector", lambda e: e.tensor_copy(BlF, Ast[0][:, :]), waits=prev_step + prev_d, inc=True)
        sb2 = P.op("vector", lambda e: e.tensor_copy(BlS, Ast[1][:, :]), waits=(sb1,), inc=True)
        d1 = P.dma("gpsimd", dr["bf"][G * 64:(G + 1) * 64, :], BlF, waits=(sb1,))
        d2 = P.dma("gpsimd", dr["bs"][G * 64:(G + 1) * 64, :], BlS, waits=(sb2,))
        dbc = emit_dbc(G)
        lamg = emit_lam(dbc, prev_step)
        upd = state_pass(G, "bwd", dg, dbc, lamg, upd_extra=(sb2,))
        prev_step = tuple(upd)
        prev_d = (d1, d2)
        bwd_d = (d1, d2)
    if upto == "bwd":
        return list(prev_step) + [dbg6, con_g_dbg]
    z3 = P.op("vector", lambda e: e.memset(al["A"][0:64, 0:1024], 0.0), waits=prev_step, inc=True)
    z4 = P.op("vector", lambda e: e.memset(al["As"][0:64, 0:1024], 0.0), waits=(z3,), inc=True)

    ynb = al["y"].bitcast(BF16)[:, 0:1024]
    SB1 = SF[1].bitcast(BF16)
    eFN = SB1[:, 0:128]
    eFNs = SB1[:, 128:256]
    stF = SB1[:, 256:384]
    stS = SB1[:, 384:512]
    Dt = SB1[:, 512:640]
    Wt = SB1[:, 640:768]
    ddf = SF[1][:, 384:512]
    kfr3l = al["ysc"]
    kfr3h = SF[1][:, 512:1024].bitcast(BF16)
    BlF2 = SFB[0][64:128, 0:1024]
    BlS2 = SFB[0][64:128, 1024:2048]
    gnscr = SF[1][:, 0:512]

    def kfr_ap(r2, h):
        if r2 < 3:
            return BB[5 + r2][:, h * 128:(h + 1) * 128]
        if h < 8:
            return kfr3l[:, h * 128:(h + 1) * 128]
        return kfr3h[:, (h - 8) * 128:(h - 7) * 128]

    def _kfr_lo(r2, h):
        return kfr_ap(r2, h)[0:64, :]

    zz = []
    for tens, sl in ((BB[5], None), (BB[6], None), (BB[7], None)):
        zz.append(P.op("vector", lambda e, tens=tens: e.memset(tens[64:128, 0:T], 0.0),
                       waits=(z4,) + ((zz[-1],) if zz else ()), inc=True))
    zz.append(P.op("vector", lambda e: e.memset(al["ysc"][64:128, 0:1024], 0.0), waits=(z4, zz[-1]), inc=True))
    zz.append(P.op("vector", lambda e: e.memset(SF[1][64:128, 512:1024], 0.0), waits=(z4, zz[-1]), inc=True))
    z5 = zz[-1]

    fwd_prev = (z5,)
    prev_bl = bwd_d
    psc_state = {"users": [], "gate": None}
    prev_q = ()
    prev_gn = [None]
    ba_gates = []
    NG_FWD = NCH if not (isinstance(upto, tuple) and upto[0] == "fwd") else upto[1]
    for G in range(NG_FWD):
        dg = dma_group(G, fwd_prev + (cc2,))
        rt_d = P.dma("sync", rtok[:, 0:1024], dr["r"][G * 128:(G + 1) * 128, :], waits=fwd_prev)
        bl1 = P.dma("sync", BlF2, dr["bf"][G * 64:(G + 1) * 64, :], waits=fwd_prev + prev_bl)
        bl2 = P.dma("sync", BlS2, dr["bs"][G * 64:(G + 1) * 64, :], waits=fwd_prev + prev_bl)
        prev_bl = ()
        ab1 = P.op("vector", lambda e: e.tensor_copy(AbF, Ast[0][:, :]), waits=fwd_prev, inc=True)
        ab2 = P.op("vector", lambda e: e.tensor_copy(AbS, Ast[1][:, :]), waits=(ab1,), inc=True)
        dbc = emit_dbc(G)
        lamg = emit_lam(dbc, fwd_prev)
        if upto == "fwdq0":
            return [lamg[1], bl2, rt_d, ab2, dbg6, con_g_dbg] + list(dg)
        trS = None
        trV = None
        for h in range(16):
            ta = P.op("tensor", lambda e, h=h: e.transpose(ps_trv.t[0:64, 0:128], rtok[:, h * 64:(h + 1) * 64],
                                                           ident[:, :]), waits=(rt_d,) + g_(ps_trv), inc=True)
            tb = P.op("tensor", lambda e, h=h: e.transpose(ps_trv.t[64:128, 0:128], rtok[:, h * 64:(h + 1) * 64],
                                                           ident[:, :]), waits=(ta,), inc=True)
            trS = P.op("scalar", lambda e, h=h: e.activation(rt64[0:64, h * 128:(h + 1) * 128],
                                                             ps_trv.t[0:64, 0:128], AF.Copy), waits=(ta,), inc=True)
            trV = P.op("vector", lambda e, h=h: e.tensor_copy(rt64[64:128, h * 128:(h + 1) * 128],
                                                              ps_trv.t[64:128, 0:128]), waits=(tb, trS), inc=True)
            ps_trv.gate = trV
        for r in range(4):
            for h in range(16):
                ta = P.op("tensor", lambda e, r=r, h=h: e.transpose(ps_trv.t[0:64, 0:128],
                                                                    ktch[r][:, h * 64:(h + 1) * 64], ident[:, :]),
                          waits=(dg[2 * r],) + g_(ps_trv), inc=True)
                if h % 2 == 0:
                    trS = P.op("scalar", lambda e, r=r, h=h: e.activation(_kfr_lo(r, h),
                                                                          ps_trv.t[0:64, 0:128], AF.Copy),
                               waits=(ta,), inc=True)
                    ps_trv.gate = trS
                else:
                    trV = P.op("vector", lambda e, r=r, h=h: e.tensor_copy(_kfr_lo(r, h), ps_trv.t[0:64, 0:128]),
                               waits=(ta,), inc=True)
                    ps_trv.gate = trV
        tr_done = (trS, trV)
        if upto == "fwdq1":
            return [trS, trV, lamg[1], bl2, ab2, dbg6, con_g_dbg]
        last_yf = None
        last_ys = None
        for h in range(16):
            pw = tuple(psc_state["users"]) + ((psc_state["gate"],) if psc_state["gate"] else ())
            psc = P.op("tensor", lambda e, G=G, h=h: (
                e.matmul(ps_acc.t[0:64, 0:128], SEL["a1_expbf"][:, h * 64:(h + 1) * 64],
                         al["coloc"][:, G * 128:(G + 1) * 128], start=True, stop=True),
                e.matmul(ps_acc.t[64:128, 0:128], SEL["a1_expbf"][:, h * 64:(h + 1) * 64],
                         al["coloc"][:, G * 128:(G + 1) * 128], start=True, stop=True))[-1],
                waits=(cl_done, exg1) + pw, inc=True)
            psc_state["gate"] = psc
            psc_state["users"] = []
            pq = prev_q[0] if prev_q else None
            e1 = P.op("scalar", lambda e: e.activation(eFN[0:64, :], ps_acc.t[0:64, 0:128], AF.Exp),
                      waits=(psc, pq), inc=True)
            scr_e = SF[1][64:128, 1536:1664]
            e2a = P.op("vector", lambda e, h=h: e.tensor_scalar(scr_e, ps_acc.t[64:128, 0:128],
                                                                al["dbcs"][64:128, h:h + 1], None, ALU.subtract),
                       waits=(psc, dbc, pq), inc=True)
            e2 = P.op("scalar", lambda e: e.activation(eFN[64:128, :], scr_e, AF.Exp, scale=-1.0),
                      waits=(e2a, e1), inc=True)
            e3 = P.op("scalar", lambda e, h=h: e.activation(eFNs[0:64, :], ps_acc.t[0:64, 0:128], AF.Exp,
                                                            scale=sgc[0:64, h:h + 1]), waits=(psc, e2, s3), inc=True)
            e4 = P.op("scalar", lambda e, h=h: e.activation(eFNs[64:128, :], scr_e, AF.Exp,
                                                            scale=nsgc[64:128, h:h + 1]),
                      waits=(e2a, e3, s4), inc=True)
            psc_state["users"] += [e1, e2a, e2, e3, e4]
            dbc_state["users"].append(e2a)
            if upto == "h_e" and h == 0:
                return [e2, e4, dbg6, con_g_dbg]
            sw = tr_done + (e4,)
            sp1 = P.op("vector", lambda e, h=h: e.tensor_mul(stF[0:64, :], rt64[0:64, h * 128:(h + 1) * 128],
                                                             eFN[0:64, :]), waits=sw + (pq,), inc=True)
            sp2 = P.op("vector", lambda e, h=h: e.tensor_mul(stF[64:128, :], rt64[64:128, h * 128:(h + 1) * 128],
                                                             eFN[64:128, :]), waits=(sp1, e2), inc=True)
            sp3 = P.op("vector", lambda e, h=h: e.tensor_mul(stS[0:64, :], rt64[0:64, h * 128:(h + 1) * 128],
                                                             eFNs[0:64, :]), waits=(sp2,), inc=True)
            sp4 = P.op("vector", lambda e, h=h: e.tensor_mul(stS[64:128, :], rt64[64:128, h * 128:(h + 1) * 128],
                                                             eFNs[64:128, :]), waits=(sp3, e4), inc=True)
            if upto == "h_sp" and h == 0:
                return [sp4, dbg6, con_g_dbg]
            def smm_spec(e, h=h):
                m = None
                for r2 in range(4):
                    m = e.matmul(ps_small.t[:, r2 * 128:(r2 + 1) * 128], kfr_ap(r2, h),
                                 rt64[:, h * 128:(h + 1) * 128], start=(r2 == 0), stop=(r2 == 3))
                return m
            smm = P.op("tensor", smm_spec, waits=sw + (z5,) + g_(ps_small), inc=True)
            if upto == "h_smm" and h == 0:
                return [smm, dbg6, con_g_dbg]
            yh_f = ps_misc.t[:, h * 64:(h + 1) * 64]
            yh_s = ps[h // 8].t[:, (h % 8) * 64:(h % 8 + 1) * 64]
            mw = [sp2, ab2, bl1]
            if h == 0:
                mw += list(g_(ps_misc))
            ypf = P.op("tensor", lambda e, h=h, yh_f=yh_f: e.matmul(yh_f, stF, SFB[0][:, h * 64:(h + 1) * 64],
                                                                    start=(h % 8 == 0), stop=False),
                       waits=tuple(mw), inc=True)
            mw2 = [sp4, ab2, bl2]
            if h == 0:
                mw2 += list(g_(ps[0])) + list(g_(ps[1]))
            yps = P.op("tensor", lambda e, h=h, yh_s=yh_s: e.matmul(yh_s, stS,
                                                                    SFB[0][:, 1024 + h * 64:1024 + (h + 1) * 64],
                                                                    start=(h % 8 == 0), stop=False),
                       waits=tuple(mw2), inc=True)
            if upto == "h_ypf" and h == 0:
                return [ypf, yps, dbg6, con_g_dbg]
            prev_dd = None
            prev_mlf = None
            prev_mls = None
            for r in range(4):
                c = 4 * G + r
                dd = P.op("vector", lambda e, c=c, h=h: e.tensor_scalar(ddf, ps_acc.t[:, 0:128],
                                                                        act[:, c * 16 + h:c * 16 + h + 1],
                                                                        None, ALU.subtract),
                          waits=(psc, act_done, prev_dd,
                                 prev_gn[0] if (h == 0 and r == 0) else None), inc=True)
                dd = P.op("scalar", lambda e: e.activation(ddf, ddf, AF.Abs), waits=(dd,), inc=True)
                psc_state["users"].append(dd)
                df = P.op("scalar", lambda e: e.activation(Dt, ddf, AF.Exp, scale=-1.0), waits=(dd,), inc=True)
                wf = P.op("vector", lambda e, r=r: e.tensor_mul(Wt, ps_small.t[:, r * 128:(r + 1) * 128], Dt),
                          waits=(df, smm, prev_mlf), inc=True)
                mlf = P.op("tensor", lambda e, r=r, h=h, yh_f=yh_f: e.matmul(yh_f, Wt,
                                                                             vtch[r][:, h * 64:(h + 1) * 64],
                                                                             start=False,
                                                                             stop=(r == 3 and h % 8 == 7)),
                           waits=(wf, dg[2 * r + 1], ypf), inc=True)
                ds = P.op("scalar", lambda e, h=h: e.activation(Dt, ddf, AF.Exp, scale=nsgc[:, h:h + 1]),
                          waits=(wf, s4), inc=True)
                ws2 = P.op("vector", lambda e, r=r: e.tensor_mul(Wt, ps_small.t[:, r * 128:(r + 1) * 128], Dt),
                           waits=(ds, mlf, prev_mls), inc=True)
                mls = P.op("tensor", lambda e, r=r, h=h, yh_s=yh_s: e.matmul(yh_s, Wt,
                                                                             vtch[r][:, h * 64:(h + 1) * 64],
                                                                             start=False,
                                                                             stop=(r == 3 and h % 8 == 7)),
                           waits=(ws2, dg[2 * r + 1], yps), inc=True)
                prev_dd = ws2
                prev_mlf, prev_mls = mlf, mls
                last_yf, last_ys = mlf, mls
            if upto == "h_loc" and h == 0:
                return [last_yf, last_ys, dbg6, con_g_dbg]
            ps_small.gate = prev_dd
            prev_q = (last_yf, last_ys)
        if upto == "fwdq2":
            return [last_yf, last_ys, dbg6, con_g_dbg]
        ev5 = None
        for hf in range(2):
            c0, c1 = hf * TC, (hf + 1) * TC
            e1v = P.op("vector", lambda e, hf=hf, c0=c0: e.tensor_copy(al["y"][:, c0:c0 + TC], ps[hf].t[:, 0:TC]),
                       waits=(last_yf, last_ys, ev5), inc=True)
            if hf == 0:
                ps[0].gate = e1v
            else:
                ps[1].gate = e1v
            e2v = P.op("vector", lambda e, c0=c0: e.tensor_sub(gnscr, ps_misc.t[:, c0:c0 + TC], al["y"][:, c0:c0 + TC]),
                       waits=(e1v,), inc=True)
            e3v = P.op("vector", lambda e, c0=c0: e.tensor_mul(gnscr, gnscr, al["alpha"][:, c0:c0 + TC]),
                       waits=(e2v, ag4), inc=True)
            ev5 = P.op("vector", lambda e, c0=c0: e.tensor_add(al["y"][:, c0:c0 + TC], al["y"][:, c0:c0 + TC], gnscr),
                       waits=(e3v,), inc=True)
            if hf == 1:
                ps_misc.gate = e2v
        if G == 8:
            dbgy = P.dma("gpsimd", dr["dbg"][:, 2688:3712], al["y"][:, 0:1024], waits=(ev5,))
            ev5 = dbgy
        r1 = P.op("vector", lambda e: e.reduce_sum(al["gst"][:, 0:16].rearrange("p (a u) -> p a u", u=1),
                                                   al["y"][:, 0:1024].rearrange("p (a c) -> p a c", c=64),
                                                   axis=AX.X), waits=(ev5,), inc=True)
        m1g = P.op("vector", lambda e: e.tensor_scalar(al["gst"][:, 0:16], al["gst"][:, 0:16], 1.0 / 64, None,
                                                       ALU.mult), waits=(r1,), inc=True)
        r2g = None
        for hf in range(2):
            c0 = hf * TC
            sq = P.op("scalar", lambda e, c0=c0: e.activation(gnscr, al["y"][:, c0:c0 + TC], AF.Square),
                      waits=(ev5, m1g if hf == 0 else r2g), inc=True)
            r2g = P.op("vector", lambda e, hf=hf: e.reduce_sum(
                al["gst"][:, 16 + hf * 8:24 + hf * 8].rearrange("p (a u) -> p a u", u=1),
                gnscr.rearrange("p (a c) -> p a c", c=64), axis=AX.X), waits=(sq,), inc=True)
        q1 = P.op("vector", lambda e: e.tensor_scalar(al["gst"][:, 16:32], al["gst"][:, 16:32], 1.0 / 64, None,
                                                      ALU.mult), waits=(r2g,), inc=True)
        mu2 = P.op("scalar", lambda e: e.activation(al["gst"][:, 32:48], al["gst"][:, 0:16], AF.Square),
                   waits=(m1g,), inc=True)
        vv = P.op("vector", lambda e: e.tensor_sub(al["gst"][:, 16:32], al["gst"][:, 16:32], al["gst"][:, 32:48]),
                  waits=(q1, mu2), inc=True)
        ln1 = P.op("scalar", lambda e: e.activation(al["gst"][:, 32:48], al["gst"][:, 16:32], AF.Ln, bias=6.4e-4),
                   waits=(vv,), inc=True)
        rsg = P.op("scalar", lambda e: e.activation(al["gst"][:, 32:48], al["gst"][:, 32:48], AF.Exp, scale=-0.5),
                   waits=(ln1,), inc=True)
        mtt = P.op("tensor", lambda e: e.transpose(ps_small.t[0:16, 0:128], al["gst"][:, 0:16], identf[:, :]),
                   waits=(mu2,) + g_(ps_small), inc=True)
        mte = P.op("scalar", lambda e: e.activation(al["mut"], ps_small.t[0:16, 0:128], AF.Copy),
                   waits=(mtt,), inc=True)
        rtt = P.op("tensor", lambda e: e.transpose(ps_small.t[0:16, 128:256], al["gst"][:, 32:48], identf[:, :]),
                   waits=(rsg, mte), inc=True)
        rte = P.op("scalar", lambda e: e.activation(al["rst"], ps_small.t[0:16, 128:256], AF.Copy),
                   waits=(rtt,), inc=True)
        ps_small.gate = rte
        mmu = P.op("tensor", lambda e: (e.matmul(ps_misc.t[:, 0:TC], al["mut"], expbf[:, 0:TC],
                                                 start=True, stop=True),
                                        e.matmul(ps_misc.t[:, TC:1024], al["mut"], expbf[:, TC:1024],
                                                 start=True, stop=True))[-1], waits=(mte, exg1) + g_(ps_misc), inc=True)
        mrs = P.op("tensor", lambda e: (e.matmul(ps[0].t[:, 0:TC], al["rst"], expbf[:, 0:TC],
                                                 start=True, stop=True),
                                        e.matmul(ps[1].t[:, 0:TC], al["rst"], expbf[:, TC:1024],
                                                 start=True, stop=True))[-1],
                   waits=(rte, exg1) + g_(ps[0]) + g_(ps[1]), inc=True)
        v2b = None
        for hf in range(2):
            c0 = hf * TC
            v1q = P.op("vector", lambda e, c0=c0: e.tensor_sub(gnscr, al["y"][:, c0:c0 + TC],
                                                               ps_misc.t[:, c0:c0 + TC]),
                       waits=(mmu, r2g if hf == 0 else v2b), inc=True)
            v2b = P.op("vector", lambda e, hf=hf, c0=c0: e.tensor_mul(ynb[:, c0:c0 + TC],
                                                                      gnscr, ps[hf].t[:, 0:TC]),
                       waits=(mrs, v1q), inc=True)
        ps_misc.gate = v1q
        ps[0].gate = v2b
        ps[1].gate = v2b
        prev_gn[0] = v2b
        gw = None
        for q in range(8):
            gdm = P.dma("sync", rtok[:, q * 128:(q + 1) * 128],
                        dr["g"][q * 128:(q + 1) * 128, G * 128:(G + 1) * 128], waits=tr_done + (gw,))
            tq = P.op("tensor", lambda e, q=q: e.transpose(ps_trv.t[:, q * 128:(q + 1) * 128],
                                                           ynb[:, q * 128:(q + 1) * 128], ident[:, :]),
                      waits=(v2b,) + ((gw,) if gw else g_(ps_trv)), inc=True)
            gw1 = P.op("vector", lambda e, q=q: e.tensor_scalar(SB1[:, 0:128],
                                                                ps_trv.t[:, q * 128:(q + 1) * 128],
                                                                al["gcol"][:, q:q + 1], None, ALU.mult),
                       waits=(tq, gbl), inc=True)
            gw = P.op("vector", lambda e, q=q, G=G: e.scalar_tensor_tensor(
                BA[q][:, G * 128:(G + 1) * 128], SB1[:, 0:128],
                al["bcol"][:, q:q + 1], rtok[:, q * 128:(q + 1) * 128], ALU.add, ALU.mult),
                      waits=(gw1, gdm), inc=True)
            ba_gates.append(gw)
        ps_trv.gate = gw
        if upto == "fwdq3":
            return [gw, dbg6, con_g_dbg]
        upd = state_pass(G, "fwd", dg, dbc, lamg, upd_extra=(e1v, gw), first_waits=(v1q, ev5))
        fwd_prev = tuple(upd) + (gw,)
        prev_q = (gw, gw)
    if isinstance(upto, tuple) and upto[0] == "fwd":
        return list(fwd_prev) + [dbg6, con_g_dbg]
    def xin_att(ic, t0, tw):
        return BA[ic][:, t0:t0 + tw]
    rg = []
    def epi_res(P_, pp, oc, t0, tw, mmg):
        g = P.op("vector", lambda e, pp=pp, oc=oc, t0=t0, tw=tw: e.tensor_add(
            x_tiles[oc][:, t0:t0 + tw], x_tiles[oc][:, t0:t0 + tw], pp.t[:, 0:tw]), waits=(mmg,), inc=True)
        pp.gate = g
        rg.append(g)
    linear_chunked(P, ws, W["W_o"], None, D, D, xin_att, T, C.onesrow, epi_res, ps,
                   extra_waits=tuple(fwd_prev), load_waits=tuple(fwd_prev))
    return [rg[-1]]



def make_env(P):
    env = {}
    env["x"] = [P.sbuf("x", [128, T], F32) for _ in range(ND)]
    env["BA"] = [P.sbuf("BA", [128, T], BF16) for _ in range(ND)]
    env["BB"] = [P.sbuf("BB", [128, T], BF16) for _ in range(ND)]
    env["SF"] = [P.sbuf("SF", [128, T], F32) for _ in range(4)]
    small = [P.sbuf("scr", [128, 2 * TC], BF16) for _ in range(6)]
    env["SMALL"] = [env["BB"][i][:, 0:TC] for i in range(8)] + [env["BB"][i][:, TC:2 * TC] for i in range(8)]
    env["t2"] = [small[i][:, 0:1024] for i in range(4)]
    env["t3"] = [small[4 + i][:, 0:1024] for i in range(2)]
    env["tmp_tt"] = [P.sbuf("tmptt", [128, TC], F32) for _ in range(2)]
    env["ps_misc"] = P.psum("psmisc", [128, 1024])
    env["ps_pair"] = [P.psum("pspa", [128, TC]), P.psum("pspb", [128, TC])]
    env["ps_small"] = P.psum("pssm", [128, TC])
    env["small6"] = small
    env["ps_acc"] = P.psum("psacc", [128, TC])
    env["ps_trv"] = PsumTile(P.es.enter_context(P.nc.psum_tensor(P.uname("pstrv"), [128, 1024], BF16)))
    env["ws"] = WStream(P, max_cols=1024, nbuf=2, bias_cols=128)
    env["rows"] = Rows(P, width=2432, reserve=640)
    env["srow"] = env["rows"].perm(TC + 8)
    env["srow_prev"] = [None]
    return env


def emit_load_x(P, env, x_in):
    gates = []
    for i in range(ND):
        gates.append(P.dma("sync", env["x"][i][:, 0:T], x_in[i * 128:(i + 1) * 128, 0:T]))
    return gates


def emit_store_x(P, env, x_out, waits):
    gates = []
    for i in range(ND):
        gates.append(P.dma("gpsimd", x_out[i * 128:(i + 1) * 128, 0:T], env["x"][i][:, 0:T],
                           waits=waits if i == 0 else ()))
    return gates


def finish(P, store_gates):
    def fn(e, gates=tuple(store_gates)):
        for (s, v) in gates:
            e.wait_ge(s.h, v)
    P.on("gpsimd", fn)


def emit_ident(P, env, ident_dram):
    env["ident"] = P.sbuf("identsb", [128, 128], BF16)
    env["ident_g"] = P.dma("sync", env["ident"][:, :], ident_dram)


def host_a0_consts(p):
    import numpy as np
    masks = np.zeros((128, 8192), dtype=np.float32)
    kk = np.arange(128)[:, None]
    qq = np.arange(128)[None, :]
    tri = (kk <= qq).astype(np.float32)
    for r in range(4):
        for w in range(4):
            off = (r * 4 + w) * 512
            ck = 4 * w + r
            for i in range(4):
                cq = 4 * i + p
                blk = masks[:, off + 128 * i: off + 128 * (i + 1)]
                if cq > ck:
                    blk[:] = 1.0
                elif cq == ck:
                    blk[:] = tri
    idx = own_token_index(p)
    cos, sin = rope_tables_host(SEQ, 32)
    tabs = np.concatenate([cos[idx].T, sin[idx].T], axis=0)
    permM = np.zeros((128, 32), dtype=np.float32)
    for c in range(16):
        permM[64 + c + 16, c] = -1.0
    for c in range(16, 32):
        permM[64 + c - 16, c] = 1.0
    return (masks.astype(NP_BF16), tabs.astype(NP_BF16), permM.astype(NP_BF16),
            np.eye(128).astype(NP_BF16))


def rope_tables_host(Tlen, dim):
    import numpy as np
    inv_freq = 1.0 / 10000.0 ** (np.arange(0, dim, 2, dtype=np.float32) / dim)
    t = np.arange(Tlen, dtype=np.float32)
    freqs = np.outer(t, inv_freq)
    emb = np.concatenate([freqs, freqs], -1)
    return np.cos(emb).astype(np.float32), np.sin(emb).astype(np.float32)


A0_WSHAPES = {
    "W_kv": (D, 256), "g_kv": (256,), "W_kdec": (256, D), "W_vdec": (256, D),
    "W_q": (D, 512), "W_qdec": (512, D), "W_krope": (D, 512), "W_qrope": (512, 512),
    "W_o": (D, D),
}


def build_a0_program(upto=None):
    nc = bass.Bass("TRN2", target_bir_lowering=False, debug=False)
    es = contextlib.ExitStack()
    with es:
        P = Prog(nc, es)
        x_in = nc.dram_tensor("x_in", [D, T], F32, kind="ExternalInput").ap()
        x_out = nc.dram_tensor("x_out", [D, T], F32, kind="ExternalOutput").ap()
        W = {k: nc.dram_tensor("W_" + k, list(shp), F32, kind="ExternalInput").ap()
             for k, shp in A0_WSHAPES.items()}
        W["a0_masks"] = nc.dram_tensor("a0_masks", [128, 8192], BF16, kind="ExternalInput").ap()
        W["a0_tabs"] = nc.dram_tensor("a0_tabs", [64, T], BF16, kind="ExternalInput").ap()
        W["perm32"] = nc.dram_tensor("perm32", [128, 32], BF16, kind="ExternalInput").ap()
        ident_in = nc.dram_tensor("ident", [128, 128], BF16, kind="ExternalInput").ap()
        C = emit_consts(P)
        env = make_env(P)
        emit_ident(P, env, ident_in)
        ld = emit_load_x(P, env, x_in)
        if upto == "dbg":
            env["dbg2"] = nc.dram_tensor("dbg2", [512, T], BF16, kind="ExternalOutput").ap()
            upto = None
        gates = emit_A0(P, C, W, env["x"], env, x_ready=tuple(ld) + (C.ready,), upto=upto)
        st = emit_store_x(P, env, x_out, waits=tuple(gates[-1:]))
        finish(P, st)
        P.finalize()
    return nc


F1_WSHAPES = {
    "wqkv": (2, D, 1536), "wo": (2, 512, D), "w1": (2, D, 5460), "b1": (2, 5460),
    "w2": (2, 2730, D), "b2": (2, D),
}


def host_f1_consts(p, d):
    import numpy as np
    sel = np.zeros((64, 16), dtype=np.float32)
    for j in range(16):
        g = 4 * j + p
        prev = g - 1
        if prev >= 0:
            sel[(prev % 4) * 16 + prev // 4, j] = 1.0
    def mk_mask(j0):
        m = np.zeros((128, 1024), dtype=np.float32)
        qq = np.arange(128)
        for r in range(4):
            for c2 in range(2):
                ck = 4 * (c2 + (0 if j0 == 0 else j0 - 1)) + r
                cq = 4 * j0 + p
                kk = np.arange(128)
                dchunk = 128 * (ck - cq)
                dd = dchunk + kk[None, :] - qq[:, None]
                blk = np.where((dd <= 0) & (dd >= -512), 0.0, -30000.0).astype(np.float32)
                m[:, (r * 2 + c2) * 128:(r * 2 + c2 + 1) * 128] = blk
        return m
    mask = mk_mask(1)
    mask0 = mk_mask(0)
    idx = own_token_index(p)
    DH, WIN = 64, 512
    half = np.arange(0, DH, 2, dtype=np.float32)
    inv_freq = 1.0 / 10000.0 ** (half / DH)
    tt = np.arange(SEQ, dtype=np.float32)
    freqs = tt[:, None] * inv_freq[None, :]
    freqs = np.concatenate([freqs, freqs], axis=-1)
    base = (half + 0.4 * DH) / (1.4 * DH)
    power = (tt - SEQ // 2) / float(WIN // 2)
    sc = base[None, :] ** power[:, None]
    sc = np.concatenate([sc, sc], axis=-1)
    cosf, sinf = np.cos(freqs), np.sin(freqs)
    qc = (cosf * sc)[idx].T
    qs = (sinf * sc)[idx].T
    kc = (cosf / sc)[idx].T
    ks = (sinf / sc)[idx].T
    tabs = np.concatenate([np.tile(a, (8, 1)) for a in (qc, qs, kc, ks)], axis=0)
    bo = np.zeros((128, 32), dtype=np.float32)
    for ic in range(4):
        for r in range(128):
            bo[r, ic * 8 + 2 * ic + r // 64] = 1.0
    bot = np.zeros((8, 512), dtype=np.float32)
    for rr in range(512):
        bot[rr // 64, rr] = 1.0
    perm = np.zeros((128, 128), dtype=np.float32)
    for blk in range(2):
        o = blk * 64
        for c in range(32):
            perm[o + c + 32, o + c] = -1.0
        for c in range(32, 64):
            perm[o + c - 32, o + c] = 1.0
    out = {
        "f1_sel": sel.astype(NP_BF16), "f1_mask": mask.astype(NP_BF16), "f1_mask0": mask0.astype(NP_BF16),
        "f1_tabs": tabs.astype(NP_BF16), "f1_bo": bo.astype(NP_BF16), "f1_bot": bot.astype(NP_BF16),
        "f1_perm": perm.astype(NP_BF16),
        "f1_qsc": np.ascontiguousarray(np.tile(np.asarray(d["L1_ffn_q_scale"]), (1, 2)).astype(np.float32)),
        "f1_ksc": np.ascontiguousarray(np.tile(np.asarray(d["L1_ffn_k_scale"]), (1, 2)).astype(np.float32)),
        "f1_lngt": np.ascontiguousarray(np.asarray(d["L1_ffn_ln_g"]).reshape(2, 8, 128).transpose(0, 2, 1).astype(np.float32)),
        "f1_lnbt": np.ascontiguousarray(np.asarray(d["L1_ffn_ln_b"]).reshape(2, 8, 128).transpose(0, 2, 1).astype(np.float32)),
    }
    return out


F1_CONST_SHAPES = {
    "f1_sel": ((64, 16), BF16), "f1_mask": ((128, 1024), BF16), "f1_mask0": ((128, 1024), BF16),
    "f1_tabs": ((2048, T), BF16), "f1_bo": ((128, 32), BF16), "f1_bot": ((8, 512), BF16),
    "f1_perm": ((128, 128), BF16), "f1_qsc": ((2, 128), F32), "f1_ksc": ((2, 128), F32),
    "f1_lngt": ((2, 128, 8), F32), "f1_lnbt": ((2, 128, 8), F32),
}


def build_f1_program():
    nc = bass.Bass("TRN2", target_bir_lowering=False, debug=False)
    es = contextlib.ExitStack()
    with es:
        P = Prog(nc, es)
        x_in = nc.dram_tensor("x_in", [D, T], F32, kind="ExternalInput").ap()
        x_out = nc.dram_tensor("x_out", [D, T], F32, kind="ExternalOutput").ap()
        W = {k: nc.dram_tensor("W_" + k, list(shp), F32, kind="ExternalInput").ap()
             for k, shp in F1_WSHAPES.items()}
        for k, (shp, dt) in F1_CONST_SHAPES.items():
            W[k] = nc.dram_tensor(k, list(shp), dt, kind="ExternalInput").ap()
        ident_in = nc.dram_tensor("identin", [128, 128], BF16, kind="ExternalInput").ap()
        C = emit_consts(P)
        env = make_env(P)
        emit_ident(P, env, ident_in)
        ld = emit_load_x(P, env, x_in)
        gates = emit_F1(P, C, W, env["x"], env, x_ready=tuple(ld) + (C.ready,))
        st = emit_store_x(P, env, x_out, waits=tuple(gates))
        finish(P, st)
        P.finalize()
    return nc


def build_f0_program(Wkeys):
    nc = bass.Bass("TRN2", target_bir_lowering=False, debug=False)
    es = contextlib.ExitStack()
    with es:
        P = Prog(nc, es)
        x_in = nc.dram_tensor("x_in", [D, T], F32, kind="ExternalInput").ap()
        x_out = nc.dram_tensor("x_out", [D, T], F32, kind="ExternalOutput").ap()
        W = {k: nc.dram_tensor("W_" + k, list(shp), F32, kind="ExternalInput").ap()
             for k, shp in Wkeys.items()}
        C = emit_consts(P)
        env = make_env(P)
        ld = emit_load_x(P, env, x_in)
        gates = emit_F0(P, C, W, env["x"], env, x_ready=tuple(ld) + (C.ready,))
        st = emit_store_x(P, env, x_out, waits=tuple(gates[-1:]))
        finish(P, st)
        P.finalize()
    return nc


F0_WSHAPES = {
    "cn_w1": (D, D), "cn_b1": (D,), "cn_w2": (D, 512), "cn_b2": (512,),
    "cn_w3": (512, 256), "cn_b3": (256,), "cn_w4": (256, 1), "cn_b4": (1,),
    "gc_w1": (D, 512), "gc_b1": (512,), "gc_w2": (512, 256), "gc_b2": (256,),
    "gc_w3": (256, 1), "gc_b3": (1,), "adaptive_range": (1,),
}


A1_WSHAPES = {
    "W_r": (D, D), "W_k": (D, D), "W_v": (D, D), "W_g": (D, D), "W_o": (D, D),
    "decay_mix": (D,), "head_decay_bias": (16,), "ln_beta": (D,), "ln_gamma": (D,),
    "slow_scale": (16,), "time_decay": (1, 1, D), "time_decay_w1": (D, 64), "time_decay_w2": (64, D),
    "time_maa_g": (1, 1, D), "time_maa_k": (1, 1, D), "time_maa_r": (1, 1, D), "time_maa_v": (1, 1, D),
    "time_maa_w": (1, 1, D), "time_maa_w1": (D, 160), "time_maa_w2": (5, 32, D), "time_maa_x": (1, 1, D),
}

A1_CONST_SHAPES = {
    "identf": ((128, 128), F32), "a1_selp": ((128, 16), BF16), "a1_seln": ((128, 16), BF16),
    "a1_bo16": ((128, 128), BF16), "a1_expb": ((16, 1024), BF16),
    "a1_expbf": ((16, 1024), F32), "a1_selpre": ((64, 16), F32), "a1_selgrp": ((64, 16), F32),
}


def host_a1_consts(p):
    import ml_dtypes
    bf = ml_dtypes.bfloat16
    d = {}
    d["identf"] = np.eye(128, dtype=np.float32)
    selp = np.zeros((128, 16), dtype=np.float32)
    seln = np.zeros((128, 16), dtype=np.float32)
    for j in range(NCH):
        c = 4 * j + p
        if c - 1 >= 0:
            selp[((c - 1) % 4) * 32 + (c - 1) // 4, j] = 1.0
        if c + 1 < GCH:
            seln[((c + 1) % 4) * 32 + 16 + (c + 1) // 4, j] = 1.0
    d["a1_selp"] = selp.astype(bf)
    d["a1_seln"] = seln.astype(bf)
    bo = np.zeros((128, 128), dtype=np.float32)
    for oc in range(8):
        bo[0:64, oc * 16 + 2 * oc] = 1.0
        bo[64:128, oc * 16 + 2 * oc + 1] = 1.0
    d["a1_bo16"] = bo.astype(bf)
    expb = np.zeros((16, 1024), dtype=np.float32)
    for h in range(16):
        expb[h, h * 64:(h + 1) * 64] = 1.0
    d["a1_expb"] = expb.astype(bf)
    d["a1_expbf"] = expb.copy()
    spre = np.zeros((64, 16), dtype=np.float32)
    sgrp = np.zeros((64, 16), dtype=np.float32)
    for j in range(NCH):
        if 4 * j + p - 1 >= 0:
            spre[4 * j + p - 1, j] = 1.0
        if 4 * j - 1 >= 0:
            sgrp[4 * j - 1, j] = 1.0
    d["a1_selpre"] = spre
    d["a1_selgrp"] = sgrp
    return d


def build_a1_program(upto=None):
    nc = bass.Bass("TRN2", target_bir_lowering=False, debug=False)
    es = contextlib.ExitStack()
    with es:
        P = Prog(nc, es)
        x_in = nc.dram_tensor("x_in", [D, T], F32, kind="ExternalInput").ap()
        x_out = nc.dram_tensor("x_out", [D, T], F32, kind="ExternalOutput").ap()
        W = {k: nc.dram_tensor("W_" + k, list(shp), F32, kind="ExternalInput").ap()
             for k, shp in A1_WSHAPES.items()}
        ident_in = nc.dram_tensor("identin", [128, 128], BF16, kind="ExternalInput").ap()
        C = emit_consts(P)
        env = make_env(P)
        env["a1_dbg_out"] = nc.dram_tensor("dbg_out", [128, 4096], F32, kind="ExternalOutput").ap()
        env["a1_sel_dram"] = {k: nc.dram_tensor(k, list(shp), dt, kind="ExternalInput").ap()
                              for k, (shp, dt) in A1_CONST_SHAPES.items()}
        emit_ident(P, env, ident_in)
        ld = emit_load_x(P, env, x_in)
        gates = emit_A1(P, C, W, env["x"], env, x_ready=tuple(ld) + (C.ready,), upto=upto)
        st = emit_store_x(P, env, x_out, waits=tuple(gates))
        finish(P, st)
        P.finalize()
    return nc


def own_token_index(p):
    j = np.arange(NCH)
    return ((4 * j[:, None] + p) * 128 + np.arange(128)[None, :]).reshape(-1)


def build_full_program(F0keys, nparts=6):
    nc = bass.Bass("TRN2", target_bir_lowering=False, debug=False)
    es = contextlib.ExitStack()
    with es:
        P = Prog(nc, es)
        x_in = nc.dram_tensor("x_in", [D, T], F32, kind="ExternalInput").ap()
        x_out = nc.dram_tensor("x_out", [D, T], F32, kind="ExternalOutput").ap()
        WF0 = {k: nc.dram_tensor("W_" + k, list(shp), F32, kind="ExternalInput").ap()
               for k, shp in F0keys.items()}
        WA0 = {k: nc.dram_tensor("A_" + k, list(shp), F32, kind="ExternalInput").ap()
               for k, shp in A0_WSHAPES.items()}
        WF1 = {k: nc.dram_tensor("F_" + k, list(shp), F32, kind="ExternalInput").ap()
               for k, shp in F1_WSHAPES.items()}
        for k, (shp, dt) in F1_CONST_SHAPES.items():
            WF1[k] = nc.dram_tensor(k, list(shp), dt, kind="ExternalInput").ap()
        WA1 = {k: nc.dram_tensor("B_" + k, list(shp), F32, kind="ExternalInput").ap()
               for k, shp in A1_WSHAPES.items()}
        WA0["a0_masks"] = nc.dram_tensor("a0_masks", [128, 8192], BF16, kind="ExternalInput").ap()
        WA0["a0_tabs"] = nc.dram_tensor("a0_tabs", [64, T], BF16, kind="ExternalInput").ap()
        WA0["perm32"] = nc.dram_tensor("perm32", [128, 32], BF16, kind="ExternalInput").ap()
        ident_in = nc.dram_tensor("identin", [128, 128], BF16, kind="ExternalInput").ap()
        C = emit_consts(P)
        env = make_env(P)
        env["a1_sel_dram"] = {k: nc.dram_tensor(k, list(shp), dt, kind="ExternalInput").ap()
                              for k, (shp, dt) in A1_CONST_SHAPES.items()}
        emit_ident(P, env, ident_in)
        ld = emit_load_x(P, env, x_in)
        g = tuple(ld) + (C.ready,)
        emitters = [
            lambda g: emit_F0(P, C, WF0, env["x"], env, x_ready=g),
            lambda g: emit_A0(P, C, WA0, env["x"], env, x_ready=g),
            lambda g: emit_F0(P, C, WF0, env["x"], env, x_ready=g),
            lambda g: emit_F1(P, C, WF1, env["x"], env, x_ready=g),
            lambda g: emit_A1(P, C, WA1, env["x"], env, x_ready=g),
            lambda g: emit_F1(P, C, WF1, env["x"], env, x_ready=g),
        ]
        for em in emitters[:nparts]:
            g = tuple(em(g))
        st = emit_store_x(P, env, x_out, waits=tuple(g))
        finish(P, st)
        P.finalize()
    return nc


def kernel(**inputs):
    import ml_dtypes
    x = np.asarray(inputs["L0_attn_x"], dtype=np.float32)
    pref = lambda pre: {k[len(pre):]: np.ascontiguousarray(np.asarray(v, dtype=np.float32))
                        for k, v in inputs.items() if k.startswith(pre)}
    pF0 = pref("L0_ffn_")
    pA0 = {k: v for k, v in pref("L0_attn_").items() if k != "x"}
    pF1 = pref("L1_ffn_")
    pA1 = pref("L1_attn_")
    idxs = [own_token_index(p) for p in range(4)]
    xc = [np.ascontiguousarray(x[c // 4][idxs[c % 4]].T) for c in range(8)]

    ncF0 = build_f0_program({k: v.shape for k, v in pF0.items()})
    ncA0 = build_a0_program()
    ncF1 = build_f1_program()
    ncA1 = build_a1_program()

    a0c = [host_a0_consts(p) for p in range(4)]
    f1c = [host_f1_consts(p, inputs) for p in range(4)]
    a1c = [host_a1_consts(p) for p in range(4)]
    eye_bf = np.eye(128, dtype=ml_dtypes.bfloat16)

    def launch(nc, mk):
        res = run_bass_kernel_spmd(nc, [mk(c) for c in range(8)], core_ids=list(range(8)))
        return [np.ascontiguousarray(res.results[c]["x_out"]) for c in range(8)]

    def mk_f0(c):
        m = {"x_in": xc[c]}
        for k, v in pF0.items():
            m["W_" + k] = v
        return m

    def mk_a0(c):
        masks, tabs, perm, ident = a0c[c % 4]
        m = {"x_in": xc[c], "a0_masks": masks, "a0_tabs": tabs, "perm32": perm, "ident": ident}
        for k, v in pA0.items():
            m["W_" + k] = v
        return m

    def mk_f1(c):
        m = {"x_in": xc[c], "identin": eye_bf}
        m.update(f1c[c % 4])
        for k in F1_WSHAPES:
            m["W_" + k] = pF1[k]
        return m

    def mk_a1(c):
        m = {"x_in": xc[c], "identin": eye_bf}
        m.update(a1c[c % 4])
        for k in A1_WSHAPES:
            m["W_" + k] = pA1[k]
        return m

    xc = launch(ncF0, mk_f0)
    xc = launch(ncA0, mk_a0)
    xc = launch(ncF0, mk_f0)
    xc = launch(ncF1, mk_f1)
    xc = launch(ncA1, mk_a1)
    xc = launch(ncF1, mk_f1)

    out = np.empty((2, SEQ, D), dtype=np.float32)
    for c in range(8):
        out[c // 4][idxs[c % 4]] = xc[c].T
    return out
